# version 14
# baseline (speedup 1.0000x reference)
import numpy as np
import contextlib
import concourse.bass as bass
import concourse.mybir as mybir
from concourse.bass_utils import run_bass_kernel_spmd

F32 = mybir.dt.float32
BF16 = mybir.dt.bfloat16
AF = mybir.ActivationFunctionType
ALU = mybir.AluOpType
AX = mybir.AxisListType

P = 128
D = 1024
DS = D // P
INTER = 512
NEXP = 16
NPE = 18
EPS = 1e-5
NCORES = 8
A3LVL = 99


class Eng:
    def __init__(self, eng, sem, n):
        self.e = eng
        self.sem = sem
        self.n = n
        self._last = None
        self._marked = True

    def op(self, name, *a, **k):
        self._last = getattr(self.e, name)(*a, **k)
        self._marked = False
        return self._last

    def __getattr__(self, name):
        if name.startswith("_") or name in ("e", "sem", "n"):
            raise AttributeError(name)

        def f(*a, **k):
            return self.op(name, *a, **k)
        return f

    def mark(self):
        if not self._marked:
            self._last.then_inc(self.sem, 1)
            self.n += 1
            self._marked = True
        return (self.sem, self.n)

    def wait(self, mark):
        if mark is None:
            return
        sem, n = mark
        if n > 0:
            self.e.wait_ge(sem, n)

    def dma(self, out, in_, s):
        sem, counts, key = s
        self.e.dma_start(out=out, in_=in_).then_inc(sem, 16)
        counts[key] += 16
        return (sem, counts[key])


class Prog:
    ENGS = ("t", "v", "s", "g", "d")

    def __init__(self, block, sems, dsems, marks_in):
        self.block = block
        self.sems = sems
        self.dsems = dsems
        self.dcounts = {k: 0 for k in dsems}
        self.counts = {e: 0 for e in self.ENGS}
        self.M = dict(marks_in) if marks_in else {}
        self.q = {e: [] for e in self.ENGS}

    def ds(self, key):
        return (self.dsems[key], self.dcounts, key)

    def mget(self, key):
        return self.M.get(key)

    def mput(self, key, mark):
        self.M[key] = mark

    def on(self, eng, fn):
        self.q[eng].append(fn)

    def flush(self):
        openers = {"t": self.block.tensor, "v": self.block.vector,
                   "s": self.block.scalar, "g": self.block.gpsimd,
                   "d": self.block.sync}
        for e in self.ENGS:
            fns = self.q[e]
            if not fns:
                continue
            self.q[e] = []

            def run(raw, e=e, fns=fns):
                w = Eng(raw, self.sems[e], self.counts[e])
                for fn in fns:
                    fn(w)
                self.counts[e] = w.n
            openers[e](run)


def build(S, marks_in, stop=99):
    NB = 2
    TT = S // P
    CH = 512
    NCH = S // CH
    QB = S // P
    KC = max(1, S // 512)
    TC = NCH
    NU = NPE * TC

    nc = bass.Bass("TRN2", target_bir_lowering=False, dynamic_dma_scratch_size=4096)

    x_in = nc.dram_tensor("x", [NB, S, D], F32, kind="ExternalInput").ap()
    Wq = nc.dram_tensor("Wq", [D, D], F32, kind="ExternalInput").ap()
    Wk = nc.dram_tensor("Wk", [D, D], F32, kind="ExternalInput").ap()
    Wv = nc.dram_tensor("Wv", [D, D], F32, kind="ExternalInput").ap()
    bqd = nc.dram_tensor("bq", [1, D], F32, kind="ExternalInput").ap()
    bkd = nc.dram_tensor("bk", [1, D], F32, kind="ExternalInput").ap()
    bvd = nc.dram_tensor("bv", [1, D], F32, kind="ExternalInput").ap()
    gate_w = nc.dram_tensor("gate_w", [NEXP, D], F32, kind="ExternalInput").ap()
    w1 = nc.dram_tensor("w1", [NEXP, D, INTER], F32, kind="ExternalInput").ap()
    w2 = nc.dram_tensor("w2", [NEXP, INTER, D], F32, kind="ExternalInput").ap()
    w3 = nc.dram_tensor("w3", [NEXP, D, INTER], F32, kind="ExternalInput").ap()
    ws1 = nc.dram_tensor("ws1", [D, 2 * INTER], F32, kind="ExternalInput").ap()
    ws2 = nc.dram_tensor("ws2", [2 * INTER, D], F32, kind="ExternalInput").ap()
    ws3 = nc.dram_tensor("ws3", [D, 2 * INTER], F32, kind="ExternalInput").ap()
    out_t = nc.dram_tensor("out", [NB, S, D], F32, kind="ExternalOutput").ap()
    x1park = nc.dram_tensor("x1park", [S, D], F32, kind="Internal").ap()
    wbf16 = nc.dram_tensor("wbf16", [NPE, P, 12288], BF16, kind="Internal").ap()

    with contextlib.ExitStack() as ctx:
        ent = ctx.enter_context
        identb = ent(nc.sbuf_tensor("identb", [P, P], BF16))
        identf = ent(nc.sbuf_tensor("identf", [P, P], F32))
        ones = ent(nc.sbuf_tensor("ones", [1, P], F32))
        onescol = ent(nc.sbuf_tensor("onescol", [P, 1], F32))
        gwT = ent(nc.sbuf_tensor("gwT", [P, DS * NEXP], F32))
        bqc = ent(nc.sbuf_tensor("bqc", [P, DS], F32))
        bkc = ent(nc.sbuf_tensor("bkc", [P, DS], F32))
        st = ent(nc.sbuf_tensor("st", [P, 32], F32))
        den = ent(nc.sbuf_tensor("den", [P, 2 * QB], F32))
        c_buf = ent(nc.sbuf_tensor("c_buf", [P, TT * NEXP], F32))
        glog = ent(nc.sbuf_tensor("glog", [P, QB * NEXP], F32))
        smkb = ent(nc.sbuf_tensor("smkb", [P, QB * NEXP], F32))
        gsc = ent(nc.sbuf_tensor("gsc", [P, 10 * QB], F32))

        RA = ent(nc.sbuf_tensor("RA", [P, 16384], F32))
        RQ = ent(nc.sbuf_tensor("RQ", [P, 16384], BF16))
        RK = ent(nc.sbuf_tensor("RK", [P, 8192], F32))
        RV = ent(nc.sbuf_tensor("RV", [P, 8192], F32))
        RW = ent(nc.sbuf_tensor("RW", [P, 24576], BF16))

        PS = ent(nc.psum_tensor("PS", [P, 4096], F32))

        sems = {}
        for e in Prog.ENGS:
            sems[e] = ent(nc.semaphore("sem_" + e))
        DSEM_NAMES = ["xA", "xB", "xC", "xD", "wA", "wB", "fA", "fB", "xrA", "xrB",
                      "pkA", "pkB", "x1rA", "x1rB", "outA", "outB",
                      "br0", "br1", "br2", "teA", "teB", "teC", "teD",
                      "w2A", "w2B"]
        dsems = {k: ent(nc.semaphore("dm_" + k)) for k in DSEM_NAMES}
        block = ent(nc.Block())
        pr = Prog(block, sems, dsems, marks_in)
        mget, mput = pr.mget, pr.mput
        phase_no = [0]

        def phase_on():
            phase_no[0] += 1
            return phase_no[0] <= stop

        RAb = RA[:].bitcast(BF16)
        Wbf = RAb[:, 0:8192]
        wstA = RA[:, 12288:16384]
        brow = RV[0:1, 0:3072]
        gwsv = RV[0:NEXP, 3072:4096]
        xstage = RA[:, 4096:8192]
        xnbf = RAb[:, 16384:20480]
        sqA1 = RA[:, 10240:11264]
        bvrep = RA[:, 11264:12288]
        xnTc = RAb[:, 0:4096]
        Ab = RAb
        Ysb = RA

        QTb = RQ
        xn2T = RQ
        KTb = RK[:].bitcast(BF16)
        sqdum = RK[:, 0:2048]
        x1t = RK[:, 2048:4096]
        silu_sb = RK[:, 0:2048]
        xnft = RK[:, 2048:4096]
        xfTt = RK[:, 4096:6144]
        apb = RK[:].bitcast(BF16)[:, 12288:16384]
        Vb = RV[:].bitcast(BF16)
        wstF = RV[:, 0:4096]
        outt = RV[:, 4096:6144]
        lnt2 = RV[:, 6144:8192]
        WTq = RW[:, 0:8192]
        WTk = RW[:, 8192:16384]
        WTv = RW[:, 16384:24576]
        wbufs = [WTq, WTk]
        w2bufs = [WTv[:, 0:4096], WTv[:, 4096:8192]]

        def bank(i, w=1):
            return PS[:, i * 512:(i + w) * 512]

        def bankb(i, w=1):
            return PS[:, i * 512:(i + w) * 512].bitcast(BF16)

        def emit_ln(key, src_ap, dst_ap, slot, sq_ap, pre_v=None, pre_s=None,
                    out_dt=None):
            c0 = slot * 4
            negm = st[:, c0:c0 + 1]
            veps = st[:, c0 + 1:c0 + 2]
            rstd = st[:, c0 + 2:c0 + 3]
            mrstd = st[:, c0 + 3:c0 + 4]

            def vb(v):
                for m in (pre_v or []):
                    v.wait(mget(m) if isinstance(m, str) else m)
                v.reduce_sum(negm, src_ap, axis=AX.X)
                v.wait(v.mark())
                v.tensor_scalar_mul(negm, negm, -1.0 / D)
                v.wait(v.mark())
                v.scalar_tensor_tensor(out=sq_ap, in0=src_ap, scalar=negm,
                                       in1=src_ap, op0=ALU.add, op1=ALU.mult,
                                       accum_out=veps)
                mput(key + "v1", v.mark())
                v.wait(mget(key + "v1"))
                v.tensor_scalar(out=veps, in0=veps, scalar1=1.0 / D, scalar2=EPS,
                                op0=ALU.mult, op1=ALU.add)
                mput(key + "vs", v.mark())
            pr.on("v", vb)

            def sb(s):
                s.wait(mget(key + "vs"))
                s.activation(rstd, veps, AF.Sqrt)
                mput(key + "sq", s.mark())
            pr.on("s", sb)

            def vb2(v):
                v.wait(mget(key + "sq"))
                v.reciprocal(rstd, rstd)
                v.wait(v.mark())
                v.tensor_tensor(out=mrstd, in0=negm, in1=rstd, op=ALU.mult)
                v.wait(v.mark())
                for m in (pre_s or []):
                    v.wait(mget(m) if isinstance(m, str) else m)
                v.tensor_scalar(out=dst_ap, in0=src_ap, scalar1=rstd,
                                scalar2=mrstd, op0=ALU.mult, op1=ALU.add)
                mput(key + "ln", v.mark())
            pr.on("v", vb2)

        def ph0_d(q):
            mput("d_bq", q.dma(brow[:, 0:D], bqd, pr.ds("br0")))
            mput("d_bk", q.dma(brow[:, D:2 * D], bkd, pr.ds("br1")))
            mput("d_gws", q.dma(gwsv[:], gate_w, pr.ds("br2")))
        pr.on("d", ph0_d)

        def ph0_g(g):
            g.memset(identb[:], 0.0)
            g.memset(identf[:], 0.0)
            g.memset(ones[:], 1.0)
            g.memset(onescol[:], 1.0)
            g.wait(g.mark())
            g.affine_select(out=identb[:], in_=identb[:], compare_op=ALU.not_equal,
                            fill=1.0, base=0, pattern=[[-1, P]], channel_multiplier=1)
            g.affine_select(out=identf[:], in_=identf[:], compare_op=ALU.not_equal,
                            fill=1.0, base=0, pattern=[[-1, P]], channel_multiplier=1)
            mput("g_ident", g.mark())
        pr.on("g", ph0_g)

        def ph0_t(t):
            t.wait(mget("d_bq"))
            t.wait(mget("d_bk"))
            t.wait(mget("d_gws"))
            t.wait(mget("g_ident"))
            for j in range(DS):
                t.matmul(bank(7)[:, j:j + 1], brow[0:1, j * P:(j + 1) * P],
                         ones[0:1, 0:1], start=True, stop=True)
                t.matmul(bank(7)[:, 8 + j:9 + j], brow[0:1, D + j * P:D + (j + 1) * P],
                         ones[0:1, 0:1], start=True, stop=True)
            for j in range(DS):
                t.transpose(bank(7)[:, 16 + j * 16:16 + (j + 1) * 16],
                            gwsv[:, j * P:(j + 1) * P], identf[0:NEXP, 0:NEXP])
            mput("t_ph0", t.mark())
        pr.on("t", ph0_t)

        def ph0_s(s):
            s.wait(mget("t_ph0"))
            s.copy(bqc[:], bank(7)[:, 0:DS])
            s.copy(bkc[:], bank(7)[:, 8:8 + DS])
            s.copy(gwT[:], bank(7)[:, 16:16 + DS * NEXP])
            mput("s_bias", s.mark())
        pr.on("s", ph0_s)
        pr.flush()

        class _StopBuild(Exception):
            pass

        fcount = [0]

        def ck():
            fcount[0] += 1
            if fcount[0] >= stop:
                raise _StopBuild()

        try:
            for b in range(NB):
                def a0_brow(q, b=b):
                    if b == 0:
                        q.wait(mget("t_ph0"))
                    else:
                        q.wait(mget(f"b{b - 1}bvmm"))
                    mput(f"b{b}brow", q.dma(brow[:, 2 * D:3 * D], bvd, pr.ds("br2")))
                pr.on("d", a0_brow)

                for wi, (Wsrc, WTdst) in enumerate([(Wq, WTq), (Wk, WTk), (Wv, WTv)]):
                    key = f"b{b}w{wi}"

                    def a0_d(q, Wsrc=Wsrc, key=key, wi=wi, b=b):
                        if wi > 0:
                            q.wait(mget(f"b{b}w{wi - 1}cast"))
                        elif b > 0:
                            q.wait(mget(f"b{b - 1}ffn_cast_s"))
                            q.wait(mget(f"b{b - 1}t{TT - 1}lny"))
                        mput(key + "h0", q.dma(
                            wstA[:].rearrange("p (a d) -> p a d", a=4),
                            Wsrc[0:512, :].rearrange("(a p) d -> p a d", p=P),
                            pr.ds("wA")))
                    pr.on("d", a0_d)

                    def a0_v(v, key=key, wi=wi, b=b):
                        if wi > 0:
                            v.wait(mget(f"b{b}w{wi - 1}tr"))
                        elif b > 0:
                            v.wait(mget(f"b{b - 1}fin_done"))
                        v.wait(mget(key + "h0"))
                        v.tensor_copy(Wbf[:, 0:4096], wstA[:])
                        mput(key + "c0", v.mark())
                    pr.on("v", a0_v)

                    def a0_d1(q, Wsrc=Wsrc, key=key, b=b):
                        q.wait(mget(key + "c0"))
                        mput(key + "h1", q.dma(
                            wstA[:].rearrange("p (a d) -> p a d", a=4),
                            Wsrc[512:1024, :].rearrange("(a p) d -> p a d", p=P),
                            pr.ds("wB")))
                    pr.on("d", a0_d1)

                    def a0_v1(v, key=key, b=b):
                        v.wait(mget(key + "h1"))
                        v.tensor_copy(Wbf[:, 4096:8192], wstA[:])
                        mput(key + "cast", v.mark())
                    pr.on("v", a0_v1)

                    def a0_t(t, key=key, wi=wi, b=b):
                        t.wait(mget(key + "cast"))
                        if b == 0 and wi == 0:
                            t.wait(mget("s_bias"))
                        if b > 0 and wi == 0:
                            t.wait(mget(f"b{b - 1}fin_done"))
                        for dt in range(DS):
                            pb = 6 + dt % 2
                            if dt >= 2:
                                t.wait(mget(key + f"tre{dt - 2}"))
                            elif wi > 0:
                                t.wait(mget(f"b{b}w{wi - 1}tre{dt + 6}"))
                            for a in range(DS):
                                t.transpose(bankb(pb)[:, a * P:(a + 1) * P],
                                            Wbf[:, a * 1024 + dt * P: a * 1024 + dt * P + P],
                                            identb[:])
                            mput(key + f"tr{dt}", t.mark())
                        mput(key + "tr", t.mark())
                    pr.on("t", a0_t)

                    def a0_s(s, key=key, WTdst=WTdst):
                        for dt in range(DS):
                            pb = 6 + dt % 2
                            s.wait(mget(key + f"tr{dt}"))
                            s.copy(WTdst[:, dt * 1024:(dt + 1) * 1024], bankb(pb)[:, 0:1024])
                            mput(key + f"tre{dt}", s.mark())
                    pr.on("s", a0_s)

                def a0_bv_t(t, b=b):
                    t.wait(mget(f"b{b}brow"))
                    t.wait(mget(f"b{b}w2tre7"))
                    if b > 0:
                        t.wait(mget(f"b{b - 1}k{QB - 1}gevf"))
                        t.wait(mget(f"b{b - 1}k{QB - 1}gevac"))
                    t.matmul(bank(4), ones[0:1, 0:P], brow[0:1, 2 * D:2 * D + 512],
                             start=True, stop=True)
                    t.matmul(bank(5), ones[0:1, 0:P], brow[0:1, 2 * D + 512:3 * D],
                             start=True, stop=True)
                    mput(f"b{b}bvmm", t.mark())
                pr.on("t", a0_bv_t)

                def a0_bv_s(s, b=b):
                    s.wait(mget(f"b{b}bvmm"))
                    s.wait(mget(f"b{b}w2cast"))
                    s.copy(bvrep[:, 0:512], bank(4))
                    s.copy(bvrep[:, 512:1024], bank(5))
                    mput(f"b{b}bvrep", s.mark())
                pr.on("s", a0_bv_s)

                for cc in range(NCH):
                    gcc = b * NCH + cc

                    def a1_d(q, b=b, cc=cc, gcc=gcc):
                        if cc == 0 and b > 0:
                            q.wait(mget(f"b{b - 1}t{TT - 1}lny"))
                        for ti in range(4):
                            gt = cc * 4 + ti
                            gti = b * TT + gt
                            q.wait(mget(f"T{gti - 4}ln"))
                            mput(f"T{gti}x",
                                 q.dma(xstage[:, (gti % 4) * D:(gti % 4 + 1) * D],
                                       x_in[b, gt * P:(gt + 1) * P, :],
                                       pr.ds(["xA", "xB", "xC", "xD"][gti % 4])))
                    pr.on("d", a1_d)

                    for ti in range(4):
                        gt = cc * 4 + ti
                        gti = b * TT + gt
                        xt = xstage[:, (gti % 4) * D:(gti % 4 + 1) * D]
                        emit_ln(f"T{gti}", xt, xnbf[:, (gti % 4) * D:(gti % 4 + 1) * D],
                                slot=gti % 4,
                                sq_ap=sqA1[:, 0:D],
                                pre_v=[f"T{gti}x"],
                                pre_s=[f"T{gti - 4}tp"])

                        def a1_tr(t, b=b, cc=cc, gcc=gcc, ti=ti, gti=gti):
                            t.wait(mget(f"T{gti}ln"))
                            pb = 6 + gti % 2
                            t.wait(mget(f"T{gti - 2}tre"))
                            if cc == 0 and ti < 2:
                                t.wait(mget(f"b{b}w2tre{6 + ti}"))
                            for js in range(DS):
                                t.transpose(bankb(pb)[:, js * P:(js + 1) * P],
                                            xnbf[:, (gti % 4) * D + js * P:
                                                 (gti % 4) * D + (js + 1) * P],
                                            identb[:])
                            mput(f"T{gti}tp", t.mark())
                        pr.on("t", a1_tr)

                        def a1_ev(v, gcc=gcc, ti=ti, gti=gti):
                            v.wait(mget(f"T{gti}tp"))
                            if ti == 0:
                                v.wait(mget(f"g{gcc - 1}qkvmm"))
                            pb = 6 + gti % 2
                            v.tensor_copy(
                                xnTc.rearrange("p (a t) -> p a t", a=DS)[:, :, ti * P:(ti + 1) * P],
                                bankb(pb)[:, 0:1024].rearrange("p (a t) -> p a t", a=DS))
                            mput(f"T{gti}tre", v.mark())
                        pr.on("v", a1_ev)

                    def a1_t(t, b=b, cc=cc, gcc=gcc):
                        for ti in range(4):
                            t.wait(mget(f"T{b * TT + cc * 4 + ti}tre"))
                        if cc == 0:
                            t.wait(mget(f"b{b}w2tre7"))
                        for qki in range(2 * DS):
                            which, es = qki // DS, qki % DS
                            WT = WTq if which == 0 else WTk
                            pb = qki % 2
                            if qki >= 2:
                                t.wait(mget(f"g{gcc}qke{qki - 2}"))
                            else:
                                t.wait(mget(f"g{gcc - 1}qke{qki + 14}"))
                            for dt in range(DS):
                                t.matmul(bank(pb),
                                         WT[:, dt * 1024 + es * P: dt * 1024 + (es + 1) * P],
                                         xnTc[:, dt * CH:(dt + 1) * CH],
                                         start=(dt == 0), stop=(dt == DS - 1))
                            mput(f"g{gcc}qkmm{qki}", t.mark())
                        for ti in range(4):
                            pp = 2 + (ti % 2) * 2
                            if ti >= 2:
                                t.wait(mget(f"g{gcc}vev{ti - 2}"))
                            else:
                                t.wait(mget(f"g{gcc - 1}vev{ti + 2}"))
                                if cc == 0 and ti == 1:
                                    t.wait(mget(f"b{b}bvrep"))
                            for eh in range(2):
                                for dt in range(DS):
                                    t.matmul(bank(pp + eh),
                                             xnTc[:, dt * CH + ti * P: dt * CH + (ti + 1) * P],
                                             WTv[:, dt * 1024 + eh * 512: dt * 1024 + (eh + 1) * 512],
                                             start=(dt == 0), stop=(dt == DS - 1))
                            mput(f"g{gcc}vmm{ti}", t.mark())
                        mput(f"g{gcc}qkvmm", t.mark())
                    pr.on("t", a1_t)

                    def a1_se(s, b=b, cc=cc, gcc=gcc):
                        for qki in range(2 * DS):
                            which, es = qki // DS, qki % DS
                            dstb = QTb if which == 0 else KTb
                            bc = bqc if which == 0 else bkc
                            pb = qki % 2
                            s.wait(mget(f"g{gcc}qkmm{qki}"))
                            s.activation(dstb[:, es * S + cc * CH: es * S + (cc + 1) * CH],
                                         bank(pb), AF.Identity,
                                         bias=bc[:, es:es + 1], scale=1.0)
                            mput(f"g{gcc}qke{qki}", s.mark())
                    pr.on("s", a1_se)

                    def a1_ve(v, b=b, cc=cc, gcc=gcc):
                        for ti in range(4):
                            gt = cc * 4 + ti
                            pp = 2 + (ti % 2) * 2
                            v.wait(mget(f"g{gcc}vmm{ti}"))
                            if cc == 0 and ti == 0:
                                v.wait(mget(f"b{b}bvrep"))
                            v.tensor_tensor(out=Vb[:, gt * D:(gt + 1) * D],
                                            in0=bank(pp, 2), in1=bvrep[:], op=ALU.add)
                            mput(f"g{gcc}vev{ti}", v.mark())
                    pr.on("v", a1_ve)
                pr.flush()
                ck()

                for qb in range(QB):
                    st4 = (qb % 2) * 4

                    def a2_t(t, b=b, qb=qb, st4=st4):
                        if qb == 0:
                            t.wait(mget(f"g{b * NCH + NCH - 1}qke{2 * DS - 1}"))
                            t.wait(mget(f"g{b * NCH + NCH - 1}vev3"))
                        t.wait(mget(f"b{b}q{qb - 2}exp"))
                        for kc in range(KC):
                            for dt in range(DS):
                                t.matmul(bank(st4 + kc),
                                         QTb[:, dt * S + qb * P: dt * S + (qb + 1) * P],
                                         KTb[:, dt * S + kc * 512: dt * S + (kc + 1) * 512],
                                         start=(dt == 0), stop=(dt == DS - 1))
                        mput(f"b{b}q{qb}mm", t.mark())
                    pr.on("t", a2_t)

                    def a2_v(v, b=b, qb=qb, st4=st4):
                        v.wait(mget(f"b{b}q{qb}mm"))
                        v.reduce_max(den[:, qb:qb + 1], bank(st4, KC), axis=AX.X,
                                     negate=True)
                        mput(f"b{b}q{qb}max", v.mark())
                    pr.on("v", a2_v)

                    def a2_s(s, b=b, qb=qb, st4=st4):
                        s.wait(mget(f"b{b}q{qb}max"))
                        if qb == 0 and b > 0:
                            s.wait(mget(f"b{b - 1}fin_done"))
                        s.mul(den[:, QB + qb:QB + qb + 1], den[:, qb:qb + 1], 1.0 / 32.0)
                        s.wait(s.mark())
                        s.activation(Ab[:, qb * S:(qb + 1) * S], bank(st4, KC), AF.Exp,
                                     bias=den[:, QB + qb:QB + qb + 1], scale=1.0 / 32.0,
                                     accum_out=den[:, qb:qb + 1])
                        mput(f"b{b}q{qb}exp", s.mark())
                    pr.on("s", a2_s)

                    def a2_v2(v, b=b, qb=qb):
                        v.wait(mget(f"b{b}q{qb}exp"))
                        v.reciprocal(den[:, QB + qb:QB + qb + 1], den[:, qb:qb + 1])
                        mput(f"b{b}q{qb}rec", v.mark())
                    pr.on("v", a2_v2)

                    def a2_s2(s, b=b, qb=qb):
                        s.wait(mget(f"b{b}q{qb}rec"))
                        s.activation(Vb[:, qb * D:(qb + 1) * D], Vb[:, qb * D:(qb + 1) * D],
                                     AF.Identity, bias=0.0,
                                     scale=den[:, QB + qb:QB + qb + 1])
                        mput(f"b{b}q{qb}vs", s.mark())
                    pr.on("s", a2_s2)
                pr.flush()
                ck()

                for kb in range(QB):
                    pp = (kb % 2) * 2

                    def a3_d(q, b=b, kb=kb):
                        q.wait(mget(f"b{b}k{kb - 2}park"))
                        if kb < 2:
                            q.wait(mget(f"b{b}q{QB - 1}mm"))
                            if b > 0:
                                q.wait(mget(f"b{b - 1}k{QB - 1}gtr"))
                                q.wait(mget(f"b{b - 1}ffn_v_done"))
                        mput(f"b{b}k{kb}xr",
                             q.dma(x1t[:, (kb % 2) * D:(kb % 2 + 1) * D],
                                   x_in[b, kb * P:(kb + 1) * P, :],
                                   pr.ds("xrA" if kb % 2 == 0 else "xrB")))
                    pr.on("d", a3_d)

                    def a3_t(t, b=b, kb=kb, pp=pp):
                        if kb == 0:
                            t.wait(mget(f"b{b}q{QB - 1}vs"))
                            t.wait(mget(f"b{b}q{QB - 1}exp"))
                            t.wait(mget(f"b{b}q{QB - 1}rec"))
                            if b == 0:
                                t.wait(mget("s_bias"))
                        t.wait(mget(f"b{b}k{kb - 2}lnev"))
                        for qc in range(QB):
                            for dh in range(2):
                                t.matmul(bank(pp + dh),
                                         Ab[:, qc * S + kb * P: qc * S + (kb + 1) * P],
                                         Vb[:, qc * D + dh * 512: qc * D + (dh + 1) * 512],
                                         start=(qc == 0), stop=(qc == QB - 1))
                        mput(f"b{b}k{kb}mm", t.mark())
                    pr.on("t", a3_t)

                    def a3_sq(s, b=b, kb=kb, pp=pp):
                        c0 = (4 + kb % 2) * 4
                        s.wait(mget(f"b{b}k{kb}mm"))
                        if kb < 2 and b > 0:
                            s.wait(mget(f"b{b - 1}ffn_v_done"))
                        s.activation(sqdum[:, (kb % 2) * D:(kb % 2 + 1) * D],
                                     bank(pp, 2), AF.Square,
                                     accum_out=st[:, c0 + 1:c0 + 2])
                        mput(f"b{b}k{kb}asq", s.mark())
                    pr.on("s", a3_sq)

                    def a3_av(v, b=b, kb=kb, pp=pp):
                        c0 = (4 + kb % 2) * 4
                        negm = st[:, c0:c0 + 1]
                        veps = st[:, c0 + 1:c0 + 2]
                        rstd = st[:, c0 + 2:c0 + 3]
                        v.wait(mget(f"b{b}k{kb}asq"))
                        v.reduce_sum(negm, bank(pp, 2), axis=AX.X)
                        v.wait(v.mark())
                        v.tensor_scalar_mul(negm, negm, -1.0 / D)
                        v.wait(v.mark())
                        v.tensor_tensor(out=rstd, in0=negm, in1=negm, op=ALU.mult)
                        v.wait(v.mark())
                        v.tensor_scalar(out=veps, in0=veps, scalar1=1.0 / D,
                                        scalar2=EPS, op0=ALU.mult, op1=ALU.add)
                        v.wait(v.mark())
                        v.tensor_tensor(out=veps, in0=veps, in1=rstd, op=ALU.subtract)
                        mput(f"b{b}k{kb}avs", v.mark())
                    pr.on("v", a3_av)

                    def a3_as(s, b=b, kb=kb):
                        c0 = (4 + kb % 2) * 4
                        s.wait(mget(f"b{b}k{kb}avs"))
                        s.activation(st[:, c0 + 2:c0 + 3], st[:, c0 + 1:c0 + 2],
                                     AF.Sqrt)
                        mput(f"b{b}k{kb}asr", s.mark())
                    pr.on("s", a3_as)

                    def a3_av2(v, b=b, kb=kb, pp=pp):
                        c0 = (4 + kb % 2) * 4
                        sl = (kb % 2) * D
                        v.wait(mget(f"b{b}k{kb}asr"))
                        v.reciprocal(st[:, c0 + 2:c0 + 3], st[:, c0 + 2:c0 + 3])
                        v.wait(v.mark())
                        v.tensor_tensor(out=st[:, c0 + 3:c0 + 4], in0=st[:, c0:c0 + 1],
                                        in1=st[:, c0 + 2:c0 + 3], op=ALU.mult)
                        v.wait(v.mark())
                        v.wait(mget(f"b{b}k{kb}xr"))
                        v.scalar_tensor_tensor(out=x1t[:, sl:sl + D], in0=bank(pp, 2),
                                               scalar=st[:, c0 + 2:c0 + 3],
                                               in1=x1t[:, sl:sl + D],
                                               op0=ALU.mult, op1=ALU.add)
                        v.wait(v.mark())
                        v.tensor_scalar(out=x1t[:, sl:sl + D], in0=x1t[:, sl:sl + D],
                                        scalar1=st[:, c0 + 3:c0 + 4], scalar2=None,
                                        op0=ALU.add)
                        mput(f"b{b}k{kb}lnev", v.mark())
                    pr.on("v", a3_av2)

                    def a3_d2(q, b=b, kb=kb):
                        q.wait(mget(f"b{b}k{kb}lnev"))
                        mput(f"b{b}k{kb}park",
                             q.dma(x1park[kb * P:(kb + 1) * P, :],
                                   x1t[:, (kb % 2) * D:(kb % 2 + 1) * D],
                                   pr.ds("pkA" if kb % 2 == 0 else "pkB")))
                    pr.on("d", a3_d2)

                def wsrc(e, piece):
                    if e < NEXP:
                        if piece < 2:
                            srcw = w1[e]
                        elif piece < 4:
                            srcw = w3[e]
                        else:
                            srcw = w2[e]
                        h = piece % 2
                        if piece < 4:
                            return srcw.rearrange("(a p) i -> p a i", p=P)[:, h * 4:(h + 1) * 4, :]
                        return srcw.rearrange("(a p) d -> p a d", p=P)[:, h * 2:(h + 1) * 2, :]
                    hh = e - NEXP
                    if piece < 2:
                        return ws1.rearrange("(a p) i -> p a i", p=P)[
                            :, piece * 4:(piece + 1) * 4, hh * 512:(hh + 1) * 512]
                    if piece < 4:
                        return ws3.rearrange("(a p) i -> p a i", p=P)[
                            :, (piece - 2) * 4:(piece - 1) * 4, hh * 512:(hh + 1) * 512]
                    return ws2.rearrange("(a p) d -> p a d", p=P)[
                        :, hh * 4 + (piece - 4) * 2: hh * 4 + (piece - 4) * 2 + 2, :]

                def back_ln(kb, b=b):
                    def bk_d(q, b=b, kb=kb):
                        q.wait(mget(f"b{b}k{kb - 2}gtr"))
                        if kb < 2:
                            q.wait(mget(f"b{b}k{QB - 2 + kb % 2}park"))
                        mput(f"b{b}k{kb}x1l",
                             q.dma(xnft[:, (kb % 2) * D:(kb % 2 + 1) * D],
                                   x1park[kb * P:(kb + 1) * P, :],
                                   pr.ds("xrA" if kb % 2 == 0 else "xrB")))
                    pr.on("d", bk_d)

                    emit_ln(f"b{b}k{kb}x1",
                            xnft[:, (kb % 2) * D:(kb % 2 + 1) * D],
                            xnft[:, (kb % 2) * D:(kb % 2 + 1) * D],
                            slot=6 + kb % 2,
                            sq_ap=xfTt[:, (kb % 2) * D:(kb % 2 + 1) * D],
                            pre_v=[f"b{b}k{kb}x1l", f"b{b}k{kb - 2}gmm",
                                   f"b{b}k{kb - 2}gevf"] +
                                  ([f"b{b - 1}k{QB - 1}gmm"] if kb < 2 and b > 0 else []),
                            pre_s=[])
                    mput(f"b{b}k{kb}xn2ln", mget(f"b{b}k{kb}x1ln"))

                def back_tr(kb, b=b):
                    def bk_t(t, b=b, kb=kb):
                        t.wait(mget(f"b{b}k{kb}xn2ln"))
                        t.wait(mget(f"b{b}k{kb - 1}gevac"))
                        if kb % 4 == 0 and kb >= 4:
                            uL = (kb // 4 - 1) * NPE + NPE - 1
                            t.wait(mget(f"b{b}U{uL}stt2"))
                            t.wait(mget(f"b{b}U{uL}stt3"))
                        for js in range(DS):
                            t.transpose(bank(4 + js // 4)[:, (js % 4) * P:(js % 4 + 1) * P],
                                        xnft[:, (kb % 2) * D + js * P: (kb % 2) * D + (js + 1) * P],
                                        identf[:])
                        mput(f"b{b}k{kb}gtr", t.mark())
                    pr.on("t", bk_t)

                    def bk_s(s, b=b, kb=kb):
                        s.wait(mget(f"b{b}k{kb}gtr"))
                        s.wait(mget(f"b{b}k{kb - 2}gmm"))
                        if kb < 2 and b > 0:
                            s.wait(mget(f"b{b - 1}ffn_mm_done"))
                        sl = (kb % 2) * D
                        s.copy(xfTt[:, sl:sl + 512], bank(4))
                        s.copy(xfTt[:, sl + 512:sl + D], bank(5))
                        mput(f"b{b}k{kb}gevf", s.mark())
                        dst = xn2T[:, 0:DS * S].rearrange("p (a t) -> p a t", a=DS)
                        s.copy(dst[:, 0:4, kb * P:(kb + 1) * P],
                               bank(4).rearrange("p (a t) -> p a t", a=4))
                        s.copy(dst[:, 4:8, kb * P:(kb + 1) * P],
                               bank(5).rearrange("p (a t) -> p a t", a=4))
                        mput(f"b{b}k{kb}gevac", s.mark())
                    pr.on("s", bk_s)

                    def bk_t2(t, b=b, kb=kb):
                        t.wait(mget(f"b{b}k{kb}gevf"))
                        t.wait(mget(f"b{b}k{kb - 2}glog"))
                        gb = 6 + kb % 2
                        sl = (kb % 2) * D
                        for dt in range(DS):
                            t.matmul(bank(gb)[:, 0:NEXP],
                                     xfTt[:, sl + dt * P: sl + (dt + 1) * P],
                                     gwT[:, dt * NEXP:(dt + 1) * NEXP],
                                     start=(dt == 0), stop=(dt == DS - 1))
                        mput(f"b{b}k{kb}gmm", t.mark())
                    pr.on("t", bk_t2)

                    def bk_gl(s, b=b, kb=kb):
                        gb = 6 + kb % 2
                        s.wait(mget(f"b{b}k{kb}gmm"))
                        if kb < 2 and b > 0:
                            s.wait(mget(f"b{b - 1}gate{NCH - 1}"))
                        s.copy(glog[:, kb * NEXP:(kb + 1) * NEXP],
                               bank(gb)[:, 0:NEXP])
                        mput(f"b{b}k{kb}glog", s.mark())
                    pr.on("s", bk_gl)

                def gate_group(g, b=b):
                    KB = 4
                    E = NEXP

                    def gexp(s, b=b, g=g):
                        s.wait(mget(f"b{b}k{4 * g + 3}glog"))
                        s.activation(glog[:, g * KB * E:(g + 1) * KB * E],
                                     glog[:, g * KB * E:(g + 1) * KB * E], AF.Exp)
                        mput(f"b{b}gexp{g}", s.mark())
                    pr.on("s", gexp)

                    def gv(v, b=b, g=g):
                        gl = glog[:, g * KB * E:(g + 1) * KB * E]
                        sm = smkb[:, g * KB * E:(g + 1) * KB * E]
                        cb = c_buf[:, g * KB * E:(g + 1) * KB * E]
                        gmax = gsc[:, 0:4 * KB]
                        den16 = gsc[:, 4 * KB:5 * KB]
                        gtmp = gsc[:, 5 * KB:9 * KB]
                        gcnt = sm[:, 0:4 * KB]
                        v.wait(mget(f"b{b}gexp{g}"))
                        if g == 0 and b > 0:
                            v.wait(mget(f"b{b - 1}ffn_v_done"))
                        g3 = gl.rearrange("p (k e) -> p k e", k=KB)
                        v.tensor_reduce(out=den16, in_=g3, axis=AX.X, op=ALU.add)
                        v.wait(v.mark())
                        v.reciprocal(den16, den16)
                        v.wait(v.mark())
                        v.tensor_tensor(out=g3, in0=g3,
                                        in1=den16.rearrange("p (k o) -> p k o", o=1
                                                            ).to_broadcast([P, KB, E]),
                                        op=ALU.mult)
                        v.wait(v.mark())
                        v.tensor_reduce(out=gmax,
                                        in_=gl.rearrange("p (kg c) -> p kg c", c=4),
                                        axis=AX.X, op=ALU.max)
                        v.wait(v.mark())
                        gm3 = gmax.rearrange("p (k g) -> p k g", g=4)
                        gc3 = gcnt.rearrange("p (k g) -> p k g", g=4)
                        gt3 = gtmp[:, 0:4 * KB].rearrange("p (k g) -> p k g", g=4)
                        for sh in range(1, 4):
                            for piece in range(2):
                                if piece == 0:
                                    w = 4 - sh
                                    a_ap, b_ap = gm3[:, :, sh:4], gm3[:, :, 0:w]
                                    o_ap, t_ap = gc3[:, :, 0:w], gt3[:, :, 0:w]
                                else:
                                    w = sh
                                    a_ap, b_ap = gm3[:, :, 0:sh], gm3[:, :, 4 - sh:4]
                                    o_ap, t_ap = gc3[:, :, 4 - sh:4], gt3[:, :, 4 - sh:4]
                                v.tensor_tensor(out=t_ap, in0=a_ap, in1=b_ap, op=ALU.is_gt)
                                v.wait(v.mark())
                                if sh == 1:
                                    v.tensor_copy(o_ap, t_ap)
                                else:
                                    v.tensor_tensor(out=o_ap, in0=o_ap, in1=t_ap, op=ALU.add)
                                v.wait(v.mark())
                        v.tensor_scalar(out=gcnt, in0=gcnt, scalar1=1.5, scalar2=None,
                                        op0=ALU.is_le)
                        v.wait(v.mark())
                        v.tensor_copy(gtmp[:, 0:4 * KB], gcnt)
                        v.wait(v.mark())
                        v.tensor_tensor(out=sm.rearrange("p (kg c) -> p kg c", c=4),
                                        in0=gl.rearrange("p (kg c) -> p kg c", c=4),
                                        in1=gtmp[:, 0:4 * KB].rearrange("p (kg o) -> p kg o", o=1
                                                                        ).to_broadcast([P, 4 * KB, 4]),
                                        op=ALU.mult)
                        v.wait(v.mark())
                        s3 = sm.rearrange("p (k e) -> p k e", k=KB)
                        c3 = cb.rearrange("p (k e) -> p k e", k=KB)
                        t3 = gl.rearrange("p (k e) -> p k e", k=KB)
                        for sh in range(1, E):
                            for piece in range(2):
                                if piece == 0:
                                    w = E - sh
                                    a_ap, b_ap = s3[:, :, sh:E], s3[:, :, 0:w]
                                    o_ap, t_ap = c3[:, :, 0:w], t3[:, :, 0:w]
                                else:
                                    w = sh
                                    a_ap, b_ap = s3[:, :, 0:sh], s3[:, :, E - sh:E]
                                    o_ap, t_ap = c3[:, :, E - sh:E], t3[:, :, E - sh:E]
                                v.tensor_tensor(out=t_ap, in0=a_ap, in1=b_ap, op=ALU.is_gt)
                                v.wait(v.mark())
                                if sh == 1:
                                    v.tensor_copy(o_ap, t_ap)
                                else:
                                    v.tensor_tensor(out=o_ap, in0=o_ap, in1=t_ap, op=ALU.add)
                                v.wait(v.mark())
                        v.tensor_scalar(out=cb, in0=cb, scalar1=3.5, scalar2=None,
                                        op0=ALU.is_le)
                        v.wait(v.mark())
                        v.tensor_tensor(out=cb, in0=cb, in1=sm, op=ALU.mult)
                        mput(f"b{b}gate{g}", v.mark())
                    pr.on("v", gv)

                def unit_dma(u, b=b):
                    g, e = u // NPE, u % NPE
                    conv = (b == 0 and g == 0)
                    par = u % 2

                    def ud(q, b=b, u=u, g=g, e=e, conv=conv, par=par):
                        if u == 0:
                            q.wait(mget(f"b{b}k{QB - 1}mm"))
                        if conv:
                            for piece in range(6):
                                gslot = (u * 6 + piece) % 2
                                prev = (u * 6 + piece) - 2
                                if prev >= 0:
                                    q.wait(mget(f"b{b}U{prev // 6}p{prev % 6}cast"))
                                na = 4 if piece < 4 else 2
                                mput(f"b{b}U{u}p{piece}dma",
                                     q.dma(wstF[:, gslot * 2048:(gslot + 1) * 2048]
                                           .rearrange("p (a i) -> p a i", a=na),
                                           wsrc(e, piece),
                                           pr.ds("fA" if gslot == 0 else "fB")))
                        else:
                            if u >= 2:
                                q.wait(mget(f"b{b}U{u - 2}h1"))
                                q.wait(mget(f"b{b}U{u - 2}h3"))
                            if b == 0 and 0 <= u - 2 < NPE:
                                q.wait(mget(f"T{u - 2}tee1"))
                            m1 = q.dma(wbufs[par], wbf16[e, :, 0:8192],
                                       pr.ds("fA" if par == 0 else "fB"))
                            if u >= 2:
                                q.wait(mget(f"b{b}U{u - 2}y3"))
                            if b == 0 and 0 <= u - 2 < NPE:
                                q.wait(mget(f"T{u - 2}tee2"))
                            m2 = q.dma(w2bufs[par], wbf16[e, :, 8192:12288],
                                       pr.ds("w2A" if par == 0 else "w2B"))
                            for piece in range(4):
                                mput(f"b{b}U{u}p{piece}cast", m1)
                            mput(f"b{b}U{u}p4cast", m2)
                            mput(f"b{b}U{u}p5cast", m2)
                    pr.on("d", ud)

                    if conv:
                        def tee(q, u=u, e=e, par=par):
                            q.wait(mget(f"b0U{u}p0cast"))
                            q.wait(mget(f"b0U{u}p1cast"))
                            q.wait(mget(f"b0U{u}p2cast"))
                            q.wait(mget(f"b0U{u}p3cast"))
                            q.wait(mget(f"T{u - 2}tee1"))
                            mput(f"T{u}tee1",
                                 q.dma(wbf16[e, :, 0:8192], wbufs[par],
                                       pr.ds("teA" if par == 0 else "teB")))
                        pr.on("d", tee)

                        def tee2(q, u=u, e=e, par=par):
                            q.wait(mget(f"b0U{u}p4cast"))
                            q.wait(mget(f"b0U{u}p5cast"))
                            q.wait(mget(f"T{u - 2}tee2"))
                            mput(f"T{u}tee2",
                                 q.dma(wbf16[e, :, 8192:12288], w2bufs[par],
                                       pr.ds("teC" if par == 0 else "teD")))
                        pr.on("d", tee2)

                def cast_body(x, u, piece, b=b):
                    gslot = (u * 6 + piece) % 2
                    par = u % 2
                    x.wait(mget(f"b{b}U{u}p{piece}dma"))
                    if u >= 2:
                        x.wait(mget(f"T{u - 2}tee1" if piece < 4 else f"T{u - 2}tee2"))
                    if u >= 2:
                        if piece < 2:
                            x.wait(mget(f"b{b}U{u - 2}h1"))
                        elif piece < 4:
                            x.wait(mget(f"b{b}U{u - 2}h3"))
                        else:
                            x.wait(mget(f"b{b}U{u - 2}y3"))
                    else:
                        x.wait(mget(f"g{b * NCH + NCH - 1}qkvmm"))
                        if b > 0:
                            x.wait(mget(f"b{b - 1}ffn_mm_done"))
                    if piece < 2:
                        col, dstb = piece * 2048, wbufs[par]
                    elif piece < 4:
                        col, dstb = 4096 + (piece - 2) * 2048, wbufs[par]
                    else:
                        col, dstb = (piece - 4) * 2048, w2bufs[par]
                    if isinstance(x.e, bass.BassScalarEngine):
                        x.copy(dstb[:, col:col + 2048],
                               wstF[:, gslot * 2048:(gslot + 1) * 2048])
                    else:
                        x.tensor_copy(dstb[:, col:col + 2048],
                                      wstF[:, gslot * 2048:(gslot + 1) * 2048])
                    mput(f"b{b}U{u}p{piece}cast", x.mark())

                def unit_casts(u, b=b):
                    def uc_s(s, u=u):
                        for piece in (0, 2, 4):
                            cast_body(s, u, piece)
                    pr.on("s", uc_s)

                    def uc_v(v, u=u):
                        for piece in (1, 3, 5):
                            cast_body(v, u, piece)
                    pr.on("v", uc_v)

                def unit_h1(t, u, b=b):
                    g, e = u // NPE, u % NPE
                    par = u % 2
                    ch = g
                    if u == 0:
                        t.wait(mget(f"b{b}k{QB - 1}lnev"))
                        t.wait(mget(f"b{b}k{QB - 2}lnev"))
                    if e == 0:
                        t.wait(mget(f"b{b}k{4 * g + 3}gevac"))
                    t.wait(mget(f"b{b}U{u}p0cast"))
                    t.wait(mget(f"b{b}U{u}p1cast"))
                    for gp in range(4):
                        t.wait(mget(f"b{b}U{u - 1}amul{gp}"))
                        for dt in range(DS):
                            t.matmul(bank(gp),
                                     wbufs[par][:, dt * 512 + gp * P: dt * 512 + (gp + 1) * P],
                                     xn2T[:, dt * S + ch * CH: dt * S + (ch + 1) * CH],
                                     start=(dt == 0), stop=(dt == DS - 1))
                        mput(f"b{b}U{u}h1g{gp}", t.mark())
                    mput(f"b{b}U{u}h1", t.mark())

                def unit_h3(t, u, b=b):
                    g = u // NPE
                    par = u % 2
                    ch = g
                    t.wait(mget(f"b{b}U{u}p2cast"))
                    t.wait(mget(f"b{b}U{u}p3cast"))
                    for gp in range(4):
                        t.wait(mget(f"b{b}U{u}silu{gp}"))
                        for dt in range(DS):
                            t.matmul(bank(gp),
                                     wbufs[par][:, 4096 + dt * 512 + gp * P:
                                                4096 + dt * 512 + (gp + 1) * P],
                                     xn2T[:, dt * S + ch * CH: dt * S + (ch + 1) * CH],
                                     start=(dt == 0), stop=(dt == DS - 1))
                        mput(f"b{b}U{u}h3g{gp}", t.mark())
                    mput(f"b{b}U{u}h3", t.mark())

                def unit_y(t, u, b=b):
                    g, e = u // NPE, u % NPE
                    par = u % 2
                    t.wait(mget(f"b{b}U{u}amul3"))
                    t.wait(mget(f"b{b}U{u}p4cast"))
                    t.wait(mget(f"b{b}U{u}p5cast"))
                    for q_ in range(4):
                        yp = 4 + (q_ % 2) * 2
                        if q_ >= 2:
                            t.wait(mget(f"b{b}U{u}stt{q_ - 2}"))
                        else:
                            t.wait(mget(f"b{b}U{u - 1}stt{q_ + 2}"))
                        if e == 0 and q_ < 2:
                            t.wait(mget(f"b{b}k{4 * g + 3}gevac"))
                            t.wait(mget(f"b{b}k{4 * g + 3}glog"))
                        for is_ in range(4):
                            for dh in range(2):
                                t.matmul(bank(yp + dh),
                                         apb[:, par * 2048 + is_ * CH + q_ * P:
                                             par * 2048 + is_ * CH + (q_ + 1) * P],
                                         w2bufs[par][:, is_ * 1024 + dh * 512:
                                                     is_ * 1024 + (dh + 1) * 512],
                                         start=(is_ == 0), stop=(is_ == 3))
                        mput(f"b{b}U{u}y{q_}", t.mark())
                    mput(f"b{b}U{u}y3", t.mark())

                def unit_sv(u, b=b):
                    g, e = u // NPE, u % NPE

                    def us(s, u=u):
                        for gp in range(4):
                            s.wait(mget(f"b{b}U{u}h1g{gp}"))
                            s.wait(mget(f"b{b}U{u - 1}amul{gp}"))
                            if u == 0 and b > 0:
                                s.wait(mget(f"b{b - 1}ffn_v_done"))
                            s.activation(silu_sb[:, gp * CH:(gp + 1) * CH],
                                         bank(gp), AF.Silu)
                            mput(f"b{b}U{u}silu{gp}", s.mark())
                    pr.on("s", us)

                    def uv(v, u=u, g=g, e=e):
                        for gp in range(4):
                            v.wait(mget(f"b{b}U{u}silu{gp}"))
                            v.wait(mget(f"b{b}U{u}h3g{gp}"))
                            if u >= 2:
                                v.wait(mget(f"b{b}U{u - 2}y3"))
                            elif b > 0:
                                v.wait(mget(f"b{b - 1}ffn_mm_done"))
                            v.tensor_tensor(
                                out=apb[:, (u % 2) * 2048 + gp * CH:
                                        (u % 2) * 2048 + (gp + 1) * CH],
                                in0=silu_sb[:, gp * CH:(gp + 1) * CH],
                                in1=bank(gp), op=ALU.mult)
                            mput(f"b{b}U{u}amul{gp}", v.mark())
                    pr.on("v", uv)

                def unit_stt(u, b=b):
                    g, e = u // NPE, u % NPE

                    def uw(v, u=u, g=g, e=e):
                        for q_ in range(4):
                            gt = g * 4 + q_
                            yp = 4 + (q_ % 2) * 2
                            v.wait(mget(f"b{b}U{u}y{q_}"))
                            if e > 0:
                                v.wait(mget(f"b{b}U{u - 1}stt{q_}"))
                            elif b > 0:
                                v.wait(mget(f"b{b - 1}t{TT - 1}lny"))
                            ydst = Ysb[:, gt * D:(gt + 1) * D]
                            if e == 0:
                                cs = c_buf[:, gt * NEXP: gt * NEXP + 1]
                                v.tensor_scalar(out=ydst, in0=bank(yp, 2),
                                                scalar1=cs, scalar2=None,
                                                op0=ALU.mult)
                            elif e < NEXP:
                                cs = c_buf[:, gt * NEXP + e: gt * NEXP + e + 1]
                                v.scalar_tensor_tensor(out=ydst, in0=bank(yp, 2),
                                                       scalar=cs, in1=ydst,
                                                       op0=ALU.mult, op1=ALU.add)
                            else:
                                v.tensor_tensor(out=ydst, in0=bank(yp, 2),
                                                in1=ydst, op=ALU.add)
                            mput(f"b{b}U{u}stt{q_}", v.mark())
                    pr.on("v", uw)

                def fin_tile(ti, b=b):
                    g = ti // 4
                    uL = g * NPE + NPE - 1

                    def fin_v1pre(v, b=b, ti=ti):
                        pass
                    emit_ln(f"b{b}t{ti}y", Ysb[:, ti * D:(ti + 1) * D],
                            lnt2[:, (ti % 2) * D:(ti % 2 + 1) * D],
                            slot=ti % 4,
                            sq_ap=outt[:, (ti % 2) * D:(ti % 2 + 1) * D],
                            pre_v=[f"b{b}U{uL}stt{ti % 4}"] +
                                  ([f"b{b}t{ti - 2}outdma"] if ti >= 2 else
                                   ([f"b{b - 1}t{TT - 2 + ti}outdma"] if b > 0 else [])) +
                                  ([f"b{b}t{ti - 4}yln"] if ti >= 4 else []),
                            pre_s=([f"b{b}t{ti - 2}outadd"] if ti >= 2 else []))
                    mput(f"b{b}t{ti}lny", mget(f"b{b}t{ti}yln"))

                    def fin_d(q, b=b, ti=ti):
                        q.wait(mget(f"b{b}t{ti}yv1"))
                        if ti >= 2:
                            q.wait(mget(f"b{b}t{ti - 2}outadd"))
                        mput(f"b{b}t{ti}x1r",
                             q.dma(outt[:, (ti % 2) * D:(ti % 2 + 1) * D],
                                   x1park[ti * P:(ti + 1) * P, :],
                                   pr.ds("x1rA" if ti % 2 == 0 else "x1rB")))
                    pr.on("d", fin_d)

                    def fin_g2(g_, b=b, ti=ti):
                        g_.wait(mget(f"b{b}t{ti}lny"))
                        g_.wait(mget(f"b{b}t{ti}x1r"))
                        sl = (ti % 2) * D
                        g_.tensor_tensor(out=outt[:, sl:sl + D], in0=outt[:, sl:sl + D],
                                         in1=lnt2[:, sl:sl + D], op=ALU.add)
                        mput(f"b{b}t{ti}outadd", g_.mark())
                    pr.on("g", fin_g2)

                    def fin_d2(q, b=b, ti=ti):
                        q.wait(mget(f"b{b}t{ti}outadd"))
                        mput(f"b{b}t{ti}outdma",
                             q.dma(out_t[b, ti * P:(ti + 1) * P, :],
                                   outt[:, (ti % 2) * D:(ti % 2 + 1) * D],
                                   pr.ds("outA" if ti % 2 == 0 else "outB")))
                    pr.on("d", fin_d2)

                for g in range(NCH):
                    if g == 0:
                        back_ln(0)
                        back_ln(1)
                        back_tr(0)
                        back_ln(2)
                        back_tr(1)
                        back_ln(3)
                        back_tr(2)
                        back_tr(3)
                    if g > 0:
                        def yprev(t, b=b, g=g):
                            unit_y(t, g * NPE - 1)
                        pr.on("t", yprev)
                        unit_stt(g * NPE - 1)
                        back_tr(4 * g)
                        back_tr(4 * g + 1)
                        back_ln(4 * g + 2)
                        back_ln(4 * g + 3)
                        back_tr(4 * g + 2)
                        back_tr(4 * g + 3)
                    gate_group(g)
                    for e in range(NPE):
                        u = g * NPE + e
                        unit_dma(u)
                        if b == 0 and g == 0:
                            unit_casts(u)

                        def ut(t, b=b, u=u, e=e):
                            unit_h1(t, u)
                            if e > 0:
                                unit_y(t, u - 1)
                            unit_h3(t, u)
                        pr.on("t", ut)
                        if e > 0:
                            unit_stt(u - 1)
                        unit_sv(u)
                        if e == 12 and g + 1 < NCH:
                            back_ln(4 * (g + 1))
                            back_ln(4 * (g + 1) + 1)
                    if g > 0:
                        for ti in range(4 * (g - 1), 4 * g):
                            fin_tile(ti)

                def tail_t(t, b=b):
                    unit_y(t, NU - 1)
                    mput(f"b{b}ffn_mm_done", t.mark())
                pr.on("t", tail_t)
                unit_stt(NU - 1)
                for ti in range(4 * (NCH - 1), 4 * NCH):
                    fin_tile(ti)

                def fin_bar_t(t, b=b):
                    t.wait(mget(f"b{b}t{TT - 1}outdma"))
                    t.wait(mget(f"b{b}t{TT - 2}outdma"))
                    t.wait(mget(f"b{b}t{TT - 1}x1r"))
                    t.wait(mget(f"b{b}t{TT - 2}x1r"))
                    if b == 0:
                        for uu in (NPE - 2, NPE - 1):
                            t.wait(mget(f"T{uu}tee1"))
                            t.wait(mget(f"T{uu}tee2"))
                    mput(f"b{b}fin_done", t.mark())
                pr.on("t", fin_bar_t)

                def endmarks_s(s, b=b):
                    mput(f"b{b}ffn_cast_s", s.mark())
                pr.on("s", endmarks_s)

                def endmarks_g(g_, b=b):
                    mput(f"b{b}ffn_cast_g", g_.mark())
                pr.on("g", endmarks_g)

                def endmarks_v(v, b=b):
                    mput(f"b{b}ffn_v_done", v.mark())
                pr.on("v", endmarks_v)
                pr.flush()
                ck()


        except _StopBuild:
            pr.flush()
        finals = dict(pr.dcounts)

        def final_d(raw):
            for k, n in finals.items():
                if n > 0:
                    raw.wait_ge(dsems[k], n)
        block.sync(final_d)

        marks_out = dict(pr.M)
    return nc, marks_out


def kernel(**inputs):
    S = 2048
    x = np.ascontiguousarray(inputs["L0_attn_x"])
    per_core = [
        {
            "x": x[2 * c:2 * c + 2],
            "Wq": inputs["L0_attn_Wq"], "Wk": inputs["L0_attn_Wk"],
            "Wv": inputs["L0_attn_Wv"],
            "bq": np.asarray(inputs["L0_attn_bq"]).reshape(1, D),
            "bk": np.asarray(inputs["L0_attn_bk"]).reshape(1, D),
            "bv": np.asarray(inputs["L0_attn_bv"]).reshape(1, D),
            "gate_w": inputs["L0_ffn_gate_w"],
            "w1": inputs["L0_ffn_w1"], "w2": inputs["L0_ffn_w2"],
            "w3": inputs["L0_ffn_w3"],
            "ws1": inputs["L0_ffn_ws1"], "ws2": inputs["L0_ffn_ws2"],
            "ws3": inputs["L0_ffn_ws3"],
        }
        for c in range(NCORES)
    ]
    nc, marks = build(S, None)
    nc, marks = build(S, marks)
    nc, _ = build(S, marks)
    res = run_bass_kernel_spmd(nc, per_core, core_ids=list(range(NCORES)))
    out = np.concatenate([r["out"] for r in res.results], axis=0)
    return np.ascontiguousarray(out.astype(np.float32))


# version 15
# speedup vs baseline: 1.0426x; 1.0426x over previous
import numpy as np
import contextlib
import concourse.bass as bass
import concourse.mybir as mybir
from concourse.bass_utils import run_bass_kernel_spmd

F32 = mybir.dt.float32
BF16 = mybir.dt.bfloat16
AF = mybir.ActivationFunctionType
ALU = mybir.AluOpType
AX = mybir.AxisListType

P = 128
D = 1024
DS = D // P
INTER = 512
NEXP = 16
NPE = 18
EPS = 1e-5
NCORES = 8
A3LVL = 99


class Eng:
    def __init__(self, eng, sem, n):
        self.e = eng
        self.sem = sem
        self.n = n
        self._last = None
        self._marked = True

    def op(self, name, *a, **k):
        self._last = getattr(self.e, name)(*a, **k)
        self._marked = False
        return self._last

    def __getattr__(self, name):
        if name.startswith("_") or name in ("e", "sem", "n"):
            raise AttributeError(name)

        def f(*a, **k):
            return self.op(name, *a, **k)
        return f

    def mark(self):
        if not self._marked:
            self._last.then_inc(self.sem, 1)
            self.n += 1
            self._marked = True
        return (self.sem, self.n)

    def wait(self, mark):
        if mark is None:
            return
        sem, n = mark
        if n > 0:
            self.e.wait_ge(sem, n)

    def dma(self, out, in_, s):
        sem, counts, key = s
        self.e.dma_start(out=out, in_=in_).then_inc(sem, 16)
        counts[key] += 16
        return (sem, counts[key])


class Prog:
    ENGS = ("t", "v", "s", "g", "d")

    def __init__(self, block, sems, dsems, marks_in):
        self.block = block
        self.sems = sems
        self.dsems = dsems
        self.dcounts = {k: 0 for k in dsems}
        self.counts = {e: 0 for e in self.ENGS}
        self.M = dict(marks_in) if marks_in else {}
        self.q = {e: [] for e in self.ENGS}

    def ds(self, key):
        return (self.dsems[key], self.dcounts, key)

    def mget(self, key):
        return self.M.get(key)

    def mput(self, key, mark):
        self.M[key] = mark

    def on(self, eng, fn):
        self.q[eng].append(fn)

    def flush(self):
        openers = {"t": self.block.tensor, "v": self.block.vector,
                   "s": self.block.scalar, "g": self.block.gpsimd,
                   "d": self.block.sync}
        for e in self.ENGS:
            fns = self.q[e]
            if not fns:
                continue
            self.q[e] = []

            def run(raw, e=e, fns=fns):
                w = Eng(raw, self.sems[e], self.counts[e])
                for fn in fns:
                    fn(w)
                self.counts[e] = w.n
            openers[e](run)


def build(S, marks_in, stop=99):
    NB = 2
    TT = S // P
    CH = 512
    NCH = S // CH
    QB = S // P
    KC = max(1, S // 512)
    TC = NCH
    NU = NPE * TC

    nc = bass.Bass("TRN2", target_bir_lowering=False, dynamic_dma_scratch_size=4096)

    x_in = nc.dram_tensor("x", [NB, S, D], F32, kind="ExternalInput").ap()
    Wq = nc.dram_tensor("Wq", [D, D], F32, kind="ExternalInput").ap()
    Wk = nc.dram_tensor("Wk", [D, D], F32, kind="ExternalInput").ap()
    Wv = nc.dram_tensor("Wv", [D, D], F32, kind="ExternalInput").ap()
    bqd = nc.dram_tensor("bq", [1, D], F32, kind="ExternalInput").ap()
    bkd = nc.dram_tensor("bk", [1, D], F32, kind="ExternalInput").ap()
    bvd = nc.dram_tensor("bv", [1, D], F32, kind="ExternalInput").ap()
    gate_w = nc.dram_tensor("gate_w", [NEXP, D], F32, kind="ExternalInput").ap()
    w1 = nc.dram_tensor("w1", [NEXP, D, INTER], F32, kind="ExternalInput").ap()
    w2 = nc.dram_tensor("w2", [NEXP, INTER, D], F32, kind="ExternalInput").ap()
    w3 = nc.dram_tensor("w3", [NEXP, D, INTER], F32, kind="ExternalInput").ap()
    ws1 = nc.dram_tensor("ws1", [D, 2 * INTER], F32, kind="ExternalInput").ap()
    ws2 = nc.dram_tensor("ws2", [2 * INTER, D], F32, kind="ExternalInput").ap()
    ws3 = nc.dram_tensor("ws3", [D, 2 * INTER], F32, kind="ExternalInput").ap()
    out_t = nc.dram_tensor("out", [NB, S, D], F32, kind="ExternalOutput").ap()
    x1park = nc.dram_tensor("x1park", [S, D], F32, kind="Internal").ap()
    wbf16 = nc.dram_tensor("wbf16", [NPE, P, 12288], BF16, kind="Internal").ap()

    with contextlib.ExitStack() as ctx:
        ent = ctx.enter_context
        identb = ent(nc.sbuf_tensor("identb", [P, P], BF16))
        identf = ent(nc.sbuf_tensor("identf", [P, P], F32))
        ones = ent(nc.sbuf_tensor("ones", [1, P], F32))
        onescol = ent(nc.sbuf_tensor("onescol", [P, 1], F32))
        gwT = ent(nc.sbuf_tensor("gwT", [P, DS * NEXP], F32))
        bqc = ent(nc.sbuf_tensor("bqc", [P, DS], F32))
        bkc = ent(nc.sbuf_tensor("bkc", [P, DS], F32))
        st = ent(nc.sbuf_tensor("st", [P, 32], F32))
        den = ent(nc.sbuf_tensor("den", [P, 2 * QB], F32))
        c_buf = ent(nc.sbuf_tensor("c_buf", [P, TT * NEXP], F32))
        glog = ent(nc.sbuf_tensor("glog", [P, QB * NEXP], F32))
        smkb = ent(nc.sbuf_tensor("smkb", [P, QB * NEXP], F32))
        gsc = ent(nc.sbuf_tensor("gsc", [P, 10 * QB], F32))

        RA = ent(nc.sbuf_tensor("RA", [P, 16384], F32))
        RQ = ent(nc.sbuf_tensor("RQ", [P, 16384], BF16))
        RK = ent(nc.sbuf_tensor("RK", [P, 8192], F32))
        RV = ent(nc.sbuf_tensor("RV", [P, 8192], F32))
        RW = ent(nc.sbuf_tensor("RW", [P, 24576], BF16))

        PS = ent(nc.psum_tensor("PS", [P, 4096], F32))

        sems = {}
        for e in Prog.ENGS:
            sems[e] = ent(nc.semaphore("sem_" + e))
        DSEM_NAMES = ["xA", "xB", "xC", "xD", "wA", "wB", "fA", "fB", "xrA", "xrB",
                      "pkA", "pkB", "x1rA", "x1rB", "outA", "outB",
                      "br0", "br1", "br2", "teA", "teB", "teC", "teD",
                      "w2A", "w2B"]
        dsems = {k: ent(nc.semaphore("dm_" + k)) for k in DSEM_NAMES}
        block = ent(nc.Block())
        pr = Prog(block, sems, dsems, marks_in)
        mget, mput = pr.mget, pr.mput
        phase_no = [0]

        def phase_on():
            phase_no[0] += 1
            return phase_no[0] <= stop

        RAb = RA[:].bitcast(BF16)
        Wbf = RAb[:, 0:8192]
        wstA = RA[:, 12288:16384]
        brow = RV[0:1, 0:3072]
        gwsv = RV[0:NEXP, 3072:4096]
        xstage = RA[:, 4096:8192]
        xnbf = RAb[:, 16384:20480]
        sqA1 = RA[:, 10240:11264]
        bvrep = RA[:, 11264:12288]
        xnTc = RAb[:, 0:4096]
        Ab = RAb
        Ysb = RA

        QTb = RQ
        xn2T = RQ
        KTb = RK[:].bitcast(BF16)
        sqdum = RK[:, 0:2048]
        x1t = RK[:, 2048:4096]
        silu_sb = RK[:, 0:2048]
        xnft = RK[:, 2048:4096]
        xfTt = RK[:, 4096:6144]
        apb = RK[:].bitcast(BF16)[:, 12288:16384]
        Vb = RV[:].bitcast(BF16)
        wstF = RV[:, 0:4096]
        outt = RV[:, 4096:6144]
        lnt2 = RV[:, 6144:8192]
        WTq = RW[:, 0:8192]
        WTk = RW[:, 8192:16384]
        WTv = RW[:, 16384:24576]
        wbufs = [WTq, WTk]
        w2bufs = [WTv[:, 0:4096], WTv[:, 4096:8192]]

        def bank(i, w=1):
            return PS[:, i * 512:(i + w) * 512]

        def bankb(i, w=1):
            return PS[:, i * 512:(i + w) * 512].bitcast(BF16)

        def emit_ln(key, src_ap, dst_ap, slot, sq_ap, pre_v=None, pre_s=None,
                    out_dt=None):
            c0 = slot * 4
            negm = st[:, c0:c0 + 1]
            veps = st[:, c0 + 1:c0 + 2]
            rstd = st[:, c0 + 2:c0 + 3]
            mrstd = st[:, c0 + 3:c0 + 4]

            def vb(v):
                for m in (pre_v or []):
                    v.wait(mget(m) if isinstance(m, str) else m)
                v.reduce_sum(negm, src_ap, axis=AX.X)
                v.wait(v.mark())
                v.tensor_scalar_mul(negm, negm, -1.0 / D)
                v.wait(v.mark())
                v.scalar_tensor_tensor(out=sq_ap, in0=src_ap, scalar=negm,
                                       in1=src_ap, op0=ALU.add, op1=ALU.mult,
                                       accum_out=veps)
                mput(key + "v1", v.mark())
                v.wait(mget(key + "v1"))
                v.tensor_scalar(out=veps, in0=veps, scalar1=1.0 / D, scalar2=EPS,
                                op0=ALU.mult, op1=ALU.add)
                mput(key + "vs", v.mark())
            pr.on("v", vb)

            def sb(s):
                s.wait(mget(key + "vs"))
                s.activation(rstd, veps, AF.Sqrt)
                mput(key + "sq", s.mark())
            pr.on("s", sb)

            def vb2(v):
                v.wait(mget(key + "sq"))
                v.reciprocal(rstd, rstd)
                v.wait(v.mark())
                v.tensor_tensor(out=mrstd, in0=negm, in1=rstd, op=ALU.mult)
                v.wait(v.mark())
                for m in (pre_s or []):
                    v.wait(mget(m) if isinstance(m, str) else m)
                v.tensor_scalar(out=dst_ap, in0=src_ap, scalar1=rstd,
                                scalar2=mrstd, op0=ALU.mult, op1=ALU.add)
                mput(key + "ln", v.mark())
            pr.on("v", vb2)

        def ph0_d(q):
            mput("d_bq", q.dma(brow[:, 0:D], bqd, pr.ds("br0")))
            mput("d_bk", q.dma(brow[:, D:2 * D], bkd, pr.ds("br1")))
            mput("d_gws", q.dma(gwsv[:], gate_w, pr.ds("br2")))
        pr.on("d", ph0_d)

        def ph0_g(g):
            g.memset(identb[:], 0.0)
            g.memset(identf[:], 0.0)
            g.memset(ones[:], 1.0)
            g.memset(onescol[:], 1.0)
            g.wait(g.mark())
            g.affine_select(out=identb[:], in_=identb[:], compare_op=ALU.not_equal,
                            fill=1.0, base=0, pattern=[[-1, P]], channel_multiplier=1)
            g.affine_select(out=identf[:], in_=identf[:], compare_op=ALU.not_equal,
                            fill=1.0, base=0, pattern=[[-1, P]], channel_multiplier=1)
            mput("g_ident", g.mark())
        pr.on("g", ph0_g)

        def ph0_t(t):
            t.wait(mget("d_bq"))
            t.wait(mget("d_bk"))
            t.wait(mget("d_gws"))
            t.wait(mget("g_ident"))
            for j in range(DS):
                t.matmul(bank(7)[:, j:j + 1], brow[0:1, j * P:(j + 1) * P],
                         ones[0:1, 0:1], start=True, stop=True)
                t.matmul(bank(7)[:, 8 + j:9 + j], brow[0:1, D + j * P:D + (j + 1) * P],
                         ones[0:1, 0:1], start=True, stop=True)
            for j in range(DS):
                t.transpose(bank(7)[:, 16 + j * 16:16 + (j + 1) * 16],
                            gwsv[:, j * P:(j + 1) * P], identf[0:NEXP, 0:NEXP])
            mput("t_ph0", t.mark())
        pr.on("t", ph0_t)

        def ph0_s(s):
            s.wait(mget("t_ph0"))
            s.copy(bqc[:], bank(7)[:, 0:DS])
            s.copy(bkc[:], bank(7)[:, 8:8 + DS])
            s.copy(gwT[:], bank(7)[:, 16:16 + DS * NEXP])
            mput("s_bias", s.mark())
        pr.on("s", ph0_s)
        pr.flush()

        class _StopBuild(Exception):
            pass

        fcount = [0]

        def ck():
            fcount[0] += 1
            if fcount[0] >= stop:
                raise _StopBuild()

        try:
            for b in range(NB):
                def a0_brow(q, b=b):
                    if b == 0:
                        q.wait(mget("t_ph0"))
                    else:
                        q.wait(mget(f"b{b - 1}bvmm"))
                    mput(f"b{b}brow", q.dma(brow[:, 2 * D:3 * D], bvd, pr.ds("br2")))
                pr.on("d", a0_brow)

                for wi, (Wsrc, WTdst) in enumerate([(Wq, WTq), (Wk, WTk), (Wv, WTv)]):
                    key = f"b{b}w{wi}"

                    def a0_d(q, Wsrc=Wsrc, key=key, wi=wi, b=b):
                        if wi > 0:
                            q.wait(mget(f"b{b}w{wi - 1}cast"))
                        elif b > 0:
                            q.wait(mget(f"b{b - 1}ffn_cast_s"))
                            q.wait(mget(f"b{b - 1}t{TT - 1}lny"))
                        mput(key + "h0", q.dma(
                            wstA[:].rearrange("p (a d) -> p a d", a=4),
                            Wsrc[0:512, :].rearrange("(a p) d -> p a d", p=P),
                            pr.ds("wA")))
                    pr.on("d", a0_d)

                    def a0_v(v, key=key, wi=wi, b=b):
                        if wi > 0:
                            v.wait(mget(f"b{b}w{wi - 1}tr"))
                        elif b > 0:
                            v.wait(mget(f"b{b - 1}fin_done"))
                        v.wait(mget(key + "h0"))
                        v.tensor_copy(Wbf[:, 0:4096], wstA[:])
                        mput(key + "c0", v.mark())
                    pr.on("v", a0_v)

                    def a0_d1(q, Wsrc=Wsrc, key=key, b=b):
                        q.wait(mget(key + "c0"))
                        mput(key + "h1", q.dma(
                            wstA[:].rearrange("p (a d) -> p a d", a=4),
                            Wsrc[512:1024, :].rearrange("(a p) d -> p a d", p=P),
                            pr.ds("wB")))
                    pr.on("d", a0_d1)

                    def a0_v1(v, key=key, b=b):
                        v.wait(mget(key + "h1"))
                        v.tensor_copy(Wbf[:, 4096:8192], wstA[:])
                        mput(key + "cast", v.mark())
                    pr.on("v", a0_v1)

                    def a0_t(t, key=key, wi=wi, b=b):
                        t.wait(mget(key + "cast"))
                        if b == 0 and wi == 0:
                            t.wait(mget("s_bias"))
                        if b > 0 and wi == 0:
                            t.wait(mget(f"b{b - 1}fin_done"))
                        for dt in range(DS):
                            pb = 6 + dt % 2
                            if dt >= 2:
                                t.wait(mget(key + f"tre{dt - 2}"))
                            elif wi > 0:
                                t.wait(mget(f"b{b}w{wi - 1}tre{dt + 6}"))
                            for a in range(DS):
                                t.transpose(bankb(pb)[:, a * P:(a + 1) * P],
                                            Wbf[:, a * 1024 + dt * P: a * 1024 + dt * P + P],
                                            identb[:])
                            mput(key + f"tr{dt}", t.mark())
                        mput(key + "tr", t.mark())
                    pr.on("t", a0_t)

                    def a0_s(s, key=key, WTdst=WTdst):
                        for dt in range(DS):
                            pb = 6 + dt % 2
                            s.wait(mget(key + f"tr{dt}"))
                            s.copy(WTdst[:, dt * 1024:(dt + 1) * 1024], bankb(pb)[:, 0:1024])
                            mput(key + f"tre{dt}", s.mark())
                    pr.on("s", a0_s)

                def a0_bv_t(t, b=b):
                    t.wait(mget(f"b{b}brow"))
                    t.wait(mget(f"b{b}w2tre7"))
                    if b > 0:
                        t.wait(mget(f"b{b - 1}k{QB - 1}gevf"))
                        t.wait(mget(f"b{b - 1}k{QB - 1}gevac"))
                    t.matmul(bank(4), ones[0:1, 0:P], brow[0:1, 2 * D:2 * D + 512],
                             start=True, stop=True)
                    t.matmul(bank(5), ones[0:1, 0:P], brow[0:1, 2 * D + 512:3 * D],
                             start=True, stop=True)
                    mput(f"b{b}bvmm", t.mark())
                pr.on("t", a0_bv_t)

                def a0_bv_s(s, b=b):
                    s.wait(mget(f"b{b}bvmm"))
                    s.wait(mget(f"b{b}w2cast"))
                    s.copy(bvrep[:, 0:512], bank(4))
                    s.copy(bvrep[:, 512:1024], bank(5))
                    mput(f"b{b}bvrep", s.mark())
                pr.on("s", a0_bv_s)

                for cc in range(NCH):
                    gcc = b * NCH + cc

                    def a1_d(q, b=b, cc=cc, gcc=gcc):
                        if cc == 0 and b > 0:
                            q.wait(mget(f"b{b - 1}t{TT - 1}lny"))
                        for ti in range(4):
                            gt = cc * 4 + ti
                            gti = b * TT + gt
                            q.wait(mget(f"T{gti - 4}ln"))
                            mput(f"T{gti}x",
                                 q.dma(xstage[:, (gti % 4) * D:(gti % 4 + 1) * D],
                                       x_in[b, gt * P:(gt + 1) * P, :],
                                       pr.ds(["xA", "xB", "xC", "xD"][gti % 4])))
                    pr.on("d", a1_d)

                    for ti in range(4):
                        gt = cc * 4 + ti
                        gti = b * TT + gt
                        xt = xstage[:, (gti % 4) * D:(gti % 4 + 1) * D]
                        emit_ln(f"T{gti}", xt, xnbf[:, (gti % 4) * D:(gti % 4 + 1) * D],
                                slot=gti % 4,
                                sq_ap=sqA1[:, 0:D],
                                pre_v=[f"T{gti}x"],
                                pre_s=[f"T{gti - 4}tp"])

                        def a1_tr(t, b=b, cc=cc, gcc=gcc, ti=ti, gti=gti):
                            t.wait(mget(f"T{gti}ln"))
                            pb = 6 + gti % 2
                            t.wait(mget(f"T{gti - 2}tre"))
                            if cc == 0 and ti < 2:
                                t.wait(mget(f"b{b}w2tre{6 + ti}"))
                            for js in range(DS):
                                t.transpose(bankb(pb)[:, js * P:(js + 1) * P],
                                            xnbf[:, (gti % 4) * D + js * P:
                                                 (gti % 4) * D + (js + 1) * P],
                                            identb[:])
                            mput(f"T{gti}tp", t.mark())
                        pr.on("t", a1_tr)

                        def a1_ev(v, gcc=gcc, ti=ti, gti=gti):
                            v.wait(mget(f"T{gti}tp"))
                            if ti == 0:
                                v.wait(mget(f"g{gcc - 1}qkvmm"))
                            pb = 6 + gti % 2
                            v.tensor_copy(
                                xnTc.rearrange("p (a t) -> p a t", a=DS)[:, :, ti * P:(ti + 1) * P],
                                bankb(pb)[:, 0:1024].rearrange("p (a t) -> p a t", a=DS))
                            mput(f"T{gti}tre", v.mark())
                        pr.on("v", a1_ev)

                    def a1_t(t, b=b, cc=cc, gcc=gcc):
                        for ti in range(4):
                            t.wait(mget(f"T{b * TT + cc * 4 + ti}tre"))
                        if cc == 0:
                            t.wait(mget(f"b{b}w2tre7"))
                        def vtile(ti):
                            pp = 2 + (ti % 2) * 2
                            if ti >= 2:
                                t.wait(mget(f"g{gcc}vev{ti - 2}"))
                            else:
                                t.wait(mget(f"g{gcc - 1}vev{ti + 2}"))
                                if cc == 0 and ti == 1:
                                    t.wait(mget(f"b{b}bvrep"))
                            for eh in range(2):
                                for dt in range(DS):
                                    t.matmul(bank(pp + eh),
                                             xnTc[:, dt * CH + ti * P: dt * CH + (ti + 1) * P],
                                             WTv[:, dt * 1024 + eh * 512: dt * 1024 + (eh + 1) * 512],
                                             start=(dt == 0), stop=(dt == DS - 1))
                            mput(f"g{gcc}vmm{ti}", t.mark())
                        for qki in range(2 * DS):
                            which, es = qki // DS, qki % DS
                            WT = WTq if which == 0 else WTk
                            pb = qki % 2
                            if qki >= 2:
                                t.wait(mget(f"g{gcc}qke{qki - 2}"))
                            else:
                                t.wait(mget(f"g{gcc - 1}qke{qki + 14}"))
                            for dt in range(DS):
                                t.matmul(bank(pb),
                                         WT[:, dt * 1024 + es * P: dt * 1024 + (es + 1) * P],
                                         xnTc[:, dt * CH:(dt + 1) * CH],
                                         start=(dt == 0), stop=(dt == DS - 1))
                            mput(f"g{gcc}qkmm{qki}", t.mark())
                            if qki % 4 == 3:
                                vtile(qki // 4)
                        mput(f"g{gcc}qkvmm", t.mark())
                    pr.on("t", a1_t)

                    def a1_se(s, b=b, cc=cc, gcc=gcc):
                        for qki in range(2 * DS):
                            which, es = qki // DS, qki % DS
                            dstb = QTb if which == 0 else KTb
                            bc = bqc if which == 0 else bkc
                            pb = qki % 2
                            s.wait(mget(f"g{gcc}qkmm{qki}"))
                            s.activation(dstb[:, es * S + cc * CH: es * S + (cc + 1) * CH],
                                         bank(pb), AF.Identity,
                                         bias=bc[:, es:es + 1], scale=1.0)
                            mput(f"g{gcc}qke{qki}", s.mark())
                    pr.on("s", a1_se)

                    def a1_ve(v, b=b, cc=cc, gcc=gcc):
                        for ti in range(4):
                            gt = cc * 4 + ti
                            pp = 2 + (ti % 2) * 2
                            v.wait(mget(f"g{gcc}vmm{ti}"))
                            if cc == 0 and ti == 0:
                                v.wait(mget(f"b{b}bvrep"))
                            v.tensor_tensor(out=Vb[:, gt * D:(gt + 1) * D],
                                            in0=bank(pp, 2), in1=bvrep[:], op=ALU.add)
                            mput(f"g{gcc}vev{ti}", v.mark())
                    pr.on("v", a1_ve)
                pr.flush()
                ck()

                for qb in range(QB):
                    st4 = (qb % 2) * 4

                    def a2_t(t, b=b, qb=qb, st4=st4):
                        if qb == 0:
                            t.wait(mget(f"g{b * NCH + NCH - 1}qke{2 * DS - 1}"))
                            t.wait(mget(f"g{b * NCH + NCH - 1}vev3"))
                        t.wait(mget(f"b{b}q{qb - 2}exp"))
                        for kc in range(KC):
                            for dt in range(DS):
                                t.matmul(bank(st4 + kc),
                                         QTb[:, dt * S + qb * P: dt * S + (qb + 1) * P],
                                         KTb[:, dt * S + kc * 512: dt * S + (kc + 1) * 512],
                                         start=(dt == 0), stop=(dt == DS - 1))
                        mput(f"b{b}q{qb}mm", t.mark())
                    pr.on("t", a2_t)

                    def a2_v(v, b=b, qb=qb, st4=st4):
                        v.wait(mget(f"b{b}q{qb}mm"))
                        v.reduce_max(den[:, qb:qb + 1], bank(st4, KC), axis=AX.X,
                                     negate=True)
                        mput(f"b{b}q{qb}max", v.mark())
                    pr.on("v", a2_v)

                    def a2_s(s, b=b, qb=qb, st4=st4):
                        s.wait(mget(f"b{b}q{qb}max"))
                        if qb == 0 and b > 0:
                            s.wait(mget(f"b{b - 1}fin_done"))
                        s.mul(den[:, QB + qb:QB + qb + 1], den[:, qb:qb + 1], 1.0 / 32.0)
                        s.wait(s.mark())
                        s.activation(Ab[:, qb * S:(qb + 1) * S], bank(st4, KC), AF.Exp,
                                     bias=den[:, QB + qb:QB + qb + 1], scale=1.0 / 32.0,
                                     accum_out=den[:, qb:qb + 1])
                        mput(f"b{b}q{qb}exp", s.mark())
                    pr.on("s", a2_s)

                    def a2_v2(v, b=b, qb=qb):
                        v.wait(mget(f"b{b}q{qb}exp"))
                        v.reciprocal(den[:, QB + qb:QB + qb + 1], den[:, qb:qb + 1])
                        mput(f"b{b}q{qb}rec", v.mark())
                    pr.on("v", a2_v2)

                    def a2_s2(s, b=b, qb=qb):
                        s.wait(mget(f"b{b}q{qb}rec"))
                        s.activation(Vb[:, qb * D:(qb + 1) * D], Vb[:, qb * D:(qb + 1) * D],
                                     AF.Identity, bias=0.0,
                                     scale=den[:, QB + qb:QB + qb + 1])
                        mput(f"b{b}q{qb}vs", s.mark())
                    pr.on("s", a2_s2)
                pr.flush()
                ck()

                for kb in range(QB):
                    pp = (kb % 2) * 2

                    def a3_d(q, b=b, kb=kb):
                        q.wait(mget(f"b{b}k{kb - 2}park"))
                        if kb < 2:
                            q.wait(mget(f"b{b}q{QB - 1}mm"))
                            if b > 0:
                                q.wait(mget(f"b{b - 1}k{QB - 1}gtr"))
                                q.wait(mget(f"b{b - 1}ffn_v_done"))
                        mput(f"b{b}k{kb}xr",
                             q.dma(x1t[:, (kb % 2) * D:(kb % 2 + 1) * D],
                                   x_in[b, kb * P:(kb + 1) * P, :],
                                   pr.ds("xrA" if kb % 2 == 0 else "xrB")))
                    pr.on("d", a3_d)

                    def a3_t(t, b=b, kb=kb, pp=pp):
                        if kb == 0:
                            t.wait(mget(f"b{b}q{QB - 1}vs"))
                            t.wait(mget(f"b{b}q{QB - 1}exp"))
                            t.wait(mget(f"b{b}q{QB - 1}rec"))
                            if b == 0:
                                t.wait(mget("s_bias"))
                        t.wait(mget(f"b{b}k{kb - 2}lnev"))
                        for qc in range(QB):
                            for dh in range(2):
                                t.matmul(bank(pp + dh),
                                         Ab[:, qc * S + kb * P: qc * S + (kb + 1) * P],
                                         Vb[:, qc * D + dh * 512: qc * D + (dh + 1) * 512],
                                         start=(qc == 0), stop=(qc == QB - 1))
                        mput(f"b{b}k{kb}mm", t.mark())
                    pr.on("t", a3_t)

                    def a3_sq(s, b=b, kb=kb, pp=pp):
                        c0 = (4 + kb % 2) * 4
                        s.wait(mget(f"b{b}k{kb}mm"))
                        if kb < 2 and b > 0:
                            s.wait(mget(f"b{b - 1}ffn_v_done"))
                        s.activation(sqdum[:, (kb % 2) * D:(kb % 2 + 1) * D],
                                     bank(pp, 2), AF.Square,
                                     accum_out=st[:, c0 + 1:c0 + 2])
                        mput(f"b{b}k{kb}asq", s.mark())
                    pr.on("s", a3_sq)

                    def a3_av(v, b=b, kb=kb, pp=pp):
                        c0 = (4 + kb % 2) * 4
                        negm = st[:, c0:c0 + 1]
                        veps = st[:, c0 + 1:c0 + 2]
                        rstd = st[:, c0 + 2:c0 + 3]
                        v.wait(mget(f"b{b}k{kb}asq"))
                        v.reduce_sum(negm, bank(pp, 2), axis=AX.X)
                        v.wait(v.mark())
                        v.tensor_scalar_mul(negm, negm, -1.0 / D)
                        v.wait(v.mark())
                        v.tensor_tensor(out=rstd, in0=negm, in1=negm, op=ALU.mult)
                        v.wait(v.mark())
                        v.tensor_scalar(out=veps, in0=veps, scalar1=1.0 / D,
                                        scalar2=EPS, op0=ALU.mult, op1=ALU.add)
                        v.wait(v.mark())
                        v.tensor_tensor(out=veps, in0=veps, in1=rstd, op=ALU.subtract)
                        mput(f"b{b}k{kb}avs", v.mark())
                    pr.on("v", a3_av)

                    def a3_as(s, b=b, kb=kb):
                        c0 = (4 + kb % 2) * 4
                        s.wait(mget(f"b{b}k{kb}avs"))
                        s.activation(st[:, c0 + 2:c0 + 3], st[:, c0 + 1:c0 + 2],
                                     AF.Sqrt)
                        mput(f"b{b}k{kb}asr", s.mark())
                    pr.on("s", a3_as)

                    def a3_av2(v, b=b, kb=kb, pp=pp):
                        c0 = (4 + kb % 2) * 4
                        sl = (kb % 2) * D
                        v.wait(mget(f"b{b}k{kb}asr"))
                        v.reciprocal(st[:, c0 + 2:c0 + 3], st[:, c0 + 2:c0 + 3])
                        v.wait(v.mark())
                        v.tensor_tensor(out=st[:, c0 + 3:c0 + 4], in0=st[:, c0:c0 + 1],
                                        in1=st[:, c0 + 2:c0 + 3], op=ALU.mult)
                        v.wait(v.mark())
                        v.wait(mget(f"b{b}k{kb}xr"))
                        v.scalar_tensor_tensor(out=x1t[:, sl:sl + D], in0=bank(pp, 2),
                                               scalar=st[:, c0 + 2:c0 + 3],
                                               in1=x1t[:, sl:sl + D],
                                               op0=ALU.mult, op1=ALU.add)
                        v.wait(v.mark())
                        v.tensor_scalar(out=x1t[:, sl:sl + D], in0=x1t[:, sl:sl + D],
                                        scalar1=st[:, c0 + 3:c0 + 4], scalar2=None,
                                        op0=ALU.add)
                        mput(f"b{b}k{kb}lnev", v.mark())
                    pr.on("v", a3_av2)

                    def a3_d2(q, b=b, kb=kb):
                        q.wait(mget(f"b{b}k{kb}lnev"))
                        mput(f"b{b}k{kb}park",
                             q.dma(x1park[kb * P:(kb + 1) * P, :],
                                   x1t[:, (kb % 2) * D:(kb % 2 + 1) * D],
                                   pr.ds("pkA" if kb % 2 == 0 else "pkB")))
                    pr.on("d", a3_d2)

                def wsrc(e, piece):
                    if e < NEXP:
                        if piece < 2:
                            srcw = w1[e]
                        elif piece < 4:
                            srcw = w3[e]
                        else:
                            srcw = w2[e]
                        h = piece % 2
                        if piece < 4:
                            return srcw.rearrange("(a p) i -> p a i", p=P)[:, h * 4:(h + 1) * 4, :]
                        return srcw.rearrange("(a p) d -> p a d", p=P)[:, h * 2:(h + 1) * 2, :]
                    hh = e - NEXP
                    if piece < 2:
                        return ws1.rearrange("(a p) i -> p a i", p=P)[
                            :, piece * 4:(piece + 1) * 4, hh * 512:(hh + 1) * 512]
                    if piece < 4:
                        return ws3.rearrange("(a p) i -> p a i", p=P)[
                            :, (piece - 2) * 4:(piece - 1) * 4, hh * 512:(hh + 1) * 512]
                    return ws2.rearrange("(a p) d -> p a d", p=P)[
                        :, hh * 4 + (piece - 4) * 2: hh * 4 + (piece - 4) * 2 + 2, :]

                def back_ln(kb, b=b):
                    def bk_d(q, b=b, kb=kb):
                        q.wait(mget(f"b{b}k{kb - 2}gtr"))
                        if kb < 2:
                            q.wait(mget(f"b{b}k{QB - 2 + kb % 2}park"))
                        mput(f"b{b}k{kb}x1l",
                             q.dma(xnft[:, (kb % 2) * D:(kb % 2 + 1) * D],
                                   x1park[kb * P:(kb + 1) * P, :],
                                   pr.ds("xrA" if kb % 2 == 0 else "xrB")))
                    pr.on("d", bk_d)

                    emit_ln(f"b{b}k{kb}x1",
                            xnft[:, (kb % 2) * D:(kb % 2 + 1) * D],
                            xnft[:, (kb % 2) * D:(kb % 2 + 1) * D],
                            slot=6 + kb % 2,
                            sq_ap=xfTt[:, (kb % 2) * D:(kb % 2 + 1) * D],
                            pre_v=[f"b{b}k{kb}x1l", f"b{b}k{kb - 2}gmm",
                                   f"b{b}k{kb - 2}gevf"] +
                                  ([f"b{b - 1}k{QB - 1}gmm"] if kb < 2 and b > 0 else []),
                            pre_s=[])
                    mput(f"b{b}k{kb}xn2ln", mget(f"b{b}k{kb}x1ln"))

                def back_tr(kb, b=b):
                    def bk_t(t, b=b, kb=kb):
                        t.wait(mget(f"b{b}k{kb}xn2ln"))
                        t.wait(mget(f"b{b}k{kb - 1}gevac"))
                        if kb % 4 == 0 and kb >= 4:
                            uL = (kb // 4 - 1) * NPE + NPE - 1
                            t.wait(mget(f"b{b}U{uL}stt2"))
                            t.wait(mget(f"b{b}U{uL}stt3"))
                        for js in range(DS):
                            t.transpose(bank(4 + js // 4)[:, (js % 4) * P:(js % 4 + 1) * P],
                                        xnft[:, (kb % 2) * D + js * P: (kb % 2) * D + (js + 1) * P],
                                        identf[:])
                        mput(f"b{b}k{kb}gtr", t.mark())
                    pr.on("t", bk_t)

                    def bk_s(s, b=b, kb=kb):
                        s.wait(mget(f"b{b}k{kb}gtr"))
                        s.wait(mget(f"b{b}k{kb - 2}gmm"))
                        if kb < 2 and b > 0:
                            s.wait(mget(f"b{b - 1}ffn_mm_done"))
                        sl = (kb % 2) * D
                        s.copy(xfTt[:, sl:sl + 512], bank(4))
                        s.copy(xfTt[:, sl + 512:sl + D], bank(5))
                        mput(f"b{b}k{kb}gevf", s.mark())
                        dst = xn2T[:, 0:DS * S].rearrange("p (a t) -> p a t", a=DS)
                        s.copy(dst[:, 0:4, kb * P:(kb + 1) * P],
                               bank(4).rearrange("p (a t) -> p a t", a=4))
                        s.copy(dst[:, 4:8, kb * P:(kb + 1) * P],
                               bank(5).rearrange("p (a t) -> p a t", a=4))
                        mput(f"b{b}k{kb}gevac", s.mark())
                    pr.on("s", bk_s)

                    def bk_t2(t, b=b, kb=kb):
                        t.wait(mget(f"b{b}k{kb}gevf"))
                        t.wait(mget(f"b{b}k{kb - 2}glog"))
                        gb = 6 + kb % 2
                        sl = (kb % 2) * D
                        for dt in range(DS):
                            t.matmul(bank(gb)[:, 0:NEXP],
                                     xfTt[:, sl + dt * P: sl + (dt + 1) * P],
                                     gwT[:, dt * NEXP:(dt + 1) * NEXP],
                                     start=(dt == 0), stop=(dt == DS - 1))
                        mput(f"b{b}k{kb}gmm", t.mark())
                    pr.on("t", bk_t2)

                    def bk_gl(s, b=b, kb=kb):
                        gb = 6 + kb % 2
                        s.wait(mget(f"b{b}k{kb}gmm"))
                        if kb < 2 and b > 0:
                            s.wait(mget(f"b{b - 1}gate{NCH - 1}"))
                        s.copy(glog[:, kb * NEXP:(kb + 1) * NEXP],
                               bank(gb)[:, 0:NEXP])
                        mput(f"b{b}k{kb}glog", s.mark())
                    pr.on("s", bk_gl)

                def gate_group(g, b=b):
                    KB = 4
                    E = NEXP

                    def gexp(s, b=b, g=g):
                        s.wait(mget(f"b{b}k{4 * g + 3}glog"))
                        s.activation(glog[:, g * KB * E:(g + 1) * KB * E],
                                     glog[:, g * KB * E:(g + 1) * KB * E], AF.Exp)
                        mput(f"b{b}gexp{g}", s.mark())
                    pr.on("s", gexp)

                    def gv(v, b=b, g=g):
                        gl = glog[:, g * KB * E:(g + 1) * KB * E]
                        sm = smkb[:, g * KB * E:(g + 1) * KB * E]
                        cb = c_buf[:, g * KB * E:(g + 1) * KB * E]
                        gmax = gsc[:, 0:4 * KB]
                        den16 = gsc[:, 4 * KB:5 * KB]
                        gtmp = gsc[:, 5 * KB:9 * KB]
                        gcnt = sm[:, 0:4 * KB]
                        v.wait(mget(f"b{b}gexp{g}"))
                        if g == 0 and b > 0:
                            v.wait(mget(f"b{b - 1}ffn_v_done"))
                        g3 = gl.rearrange("p (k e) -> p k e", k=KB)
                        v.tensor_reduce(out=den16, in_=g3, axis=AX.X, op=ALU.add)
                        v.wait(v.mark())
                        v.reciprocal(den16, den16)
                        v.wait(v.mark())
                        v.tensor_tensor(out=g3, in0=g3,
                                        in1=den16.rearrange("p (k o) -> p k o", o=1
                                                            ).to_broadcast([P, KB, E]),
                                        op=ALU.mult)
                        v.wait(v.mark())
                        v.tensor_reduce(out=gmax,
                                        in_=gl.rearrange("p (kg c) -> p kg c", c=4),
                                        axis=AX.X, op=ALU.max)
                        v.wait(v.mark())
                        gm3 = gmax.rearrange("p (k g) -> p k g", g=4)
                        gc3 = gcnt.rearrange("p (k g) -> p k g", g=4)
                        gt3 = gtmp[:, 0:4 * KB].rearrange("p (k g) -> p k g", g=4)
                        for sh in range(1, 4):
                            for piece in range(2):
                                if piece == 0:
                                    w = 4 - sh
                                    a_ap, b_ap = gm3[:, :, sh:4], gm3[:, :, 0:w]
                                    o_ap, t_ap = gc3[:, :, 0:w], gt3[:, :, 0:w]
                                else:
                                    w = sh
                                    a_ap, b_ap = gm3[:, :, 0:sh], gm3[:, :, 4 - sh:4]
                                    o_ap, t_ap = gc3[:, :, 4 - sh:4], gt3[:, :, 4 - sh:4]
                                v.tensor_tensor(out=t_ap, in0=a_ap, in1=b_ap, op=ALU.is_gt)
                                v.wait(v.mark())
                                if sh == 1:
                                    v.tensor_copy(o_ap, t_ap)
                                else:
                                    v.tensor_tensor(out=o_ap, in0=o_ap, in1=t_ap, op=ALU.add)
                                v.wait(v.mark())
                        v.tensor_scalar(out=gcnt, in0=gcnt, scalar1=1.5, scalar2=None,
                                        op0=ALU.is_le)
                        v.wait(v.mark())
                        v.tensor_copy(gtmp[:, 0:4 * KB], gcnt)
                        v.wait(v.mark())
                        v.tensor_tensor(out=sm.rearrange("p (kg c) -> p kg c", c=4),
                                        in0=gl.rearrange("p (kg c) -> p kg c", c=4),
                                        in1=gtmp[:, 0:4 * KB].rearrange("p (kg o) -> p kg o", o=1
                                                                        ).to_broadcast([P, 4 * KB, 4]),
                                        op=ALU.mult)
                        v.wait(v.mark())
                        s3 = sm.rearrange("p (k e) -> p k e", k=KB)
                        c3 = cb.rearrange("p (k e) -> p k e", k=KB)
                        t3 = gl.rearrange("p (k e) -> p k e", k=KB)
                        for sh in range(1, E):
                            for piece in range(2):
                                if piece == 0:
                                    w = E - sh
                                    a_ap, b_ap = s3[:, :, sh:E], s3[:, :, 0:w]
                                    o_ap, t_ap = c3[:, :, 0:w], t3[:, :, 0:w]
                                else:
                                    w = sh
                                    a_ap, b_ap = s3[:, :, 0:sh], s3[:, :, E - sh:E]
                                    o_ap, t_ap = c3[:, :, E - sh:E], t3[:, :, E - sh:E]
                                v.tensor_tensor(out=t_ap, in0=a_ap, in1=b_ap, op=ALU.is_gt)
                                v.wait(v.mark())
                                if sh == 1:
                                    v.tensor_copy(o_ap, t_ap)
                                else:
                                    v.tensor_tensor(out=o_ap, in0=o_ap, in1=t_ap, op=ALU.add)
                                v.wait(v.mark())
                        v.tensor_scalar(out=cb, in0=cb, scalar1=3.5, scalar2=None,
                                        op0=ALU.is_le)
                        v.wait(v.mark())
                        v.tensor_tensor(out=cb, in0=cb, in1=sm, op=ALU.mult)
                        mput(f"b{b}gate{g}", v.mark())
                    pr.on("v", gv)

                def unit_dma(u, b=b):
                    g, e = u // NPE, u % NPE
                    conv = (b == 0 and g == 0)
                    par = u % 2

                    def ud(q, b=b, u=u, g=g, e=e, conv=conv, par=par):
                        if u == 0:
                            q.wait(mget(f"b{b}k{QB - 1}mm"))
                        if conv:
                            for piece in range(6):
                                gslot = (u * 6 + piece) % 2
                                prev = (u * 6 + piece) - 2
                                if prev >= 0:
                                    q.wait(mget(f"b{b}U{prev // 6}p{prev % 6}cast"))
                                na = 4 if piece < 4 else 2
                                mput(f"b{b}U{u}p{piece}dma",
                                     q.dma(wstF[:, gslot * 2048:(gslot + 1) * 2048]
                                           .rearrange("p (a i) -> p a i", a=na),
                                           wsrc(e, piece),
                                           pr.ds("fA" if gslot == 0 else "fB")))
                        else:
                            if u >= 2:
                                q.wait(mget(f"b{b}U{u - 2}h1"))
                                q.wait(mget(f"b{b}U{u - 2}h3"))
                            if b == 0 and 0 <= u - 2 < NPE:
                                q.wait(mget(f"T{u - 2}tee1"))
                            m1 = q.dma(wbufs[par], wbf16[e, :, 0:8192],
                                       pr.ds("fA" if par == 0 else "fB"))
                            if u >= 2:
                                q.wait(mget(f"b{b}U{u - 2}y3"))
                            if b == 0 and 0 <= u - 2 < NPE:
                                q.wait(mget(f"T{u - 2}tee2"))
                            m2 = q.dma(w2bufs[par], wbf16[e, :, 8192:12288],
                                       pr.ds("w2A" if par == 0 else "w2B"))
                            for piece in range(4):
                                mput(f"b{b}U{u}p{piece}cast", m1)
                            mput(f"b{b}U{u}p4cast", m2)
                            mput(f"b{b}U{u}p5cast", m2)
                    pr.on("d", ud)

                    if conv:
                        def tee(q, u=u, e=e, par=par):
                            q.wait(mget(f"b0U{u}p0cast"))
                            q.wait(mget(f"b0U{u}p1cast"))
                            q.wait(mget(f"b0U{u}p2cast"))
                            q.wait(mget(f"b0U{u}p3cast"))
                            q.wait(mget(f"T{u - 2}tee1"))
                            mput(f"T{u}tee1",
                                 q.dma(wbf16[e, :, 0:8192], wbufs[par],
                                       pr.ds("teA" if par == 0 else "teB")))
                        pr.on("d", tee)

                        def tee2(q, u=u, e=e, par=par):
                            q.wait(mget(f"b0U{u}p4cast"))
                            q.wait(mget(f"b0U{u}p5cast"))
                            q.wait(mget(f"T{u - 2}tee2"))
                            mput(f"T{u}tee2",
                                 q.dma(wbf16[e, :, 8192:12288], w2bufs[par],
                                       pr.ds("teC" if par == 0 else "teD")))
                        pr.on("d", tee2)

                def cast_body(x, u, piece, b=b):
                    gslot = (u * 6 + piece) % 2
                    par = u % 2
                    x.wait(mget(f"b{b}U{u}p{piece}dma"))
                    if u >= 2:
                        x.wait(mget(f"T{u - 2}tee1" if piece < 4 else f"T{u - 2}tee2"))
                    if u >= 2:
                        if piece < 2:
                            x.wait(mget(f"b{b}U{u - 2}h1"))
                        elif piece < 4:
                            x.wait(mget(f"b{b}U{u - 2}h3"))
                        else:
                            x.wait(mget(f"b{b}U{u - 2}y3"))
                    else:
                        x.wait(mget(f"g{b * NCH + NCH - 1}qkvmm"))
                        if b > 0:
                            x.wait(mget(f"b{b - 1}ffn_mm_done"))
                    if piece < 2:
                        col, dstb = piece * 2048, wbufs[par]
                    elif piece < 4:
                        col, dstb = 4096 + (piece - 2) * 2048, wbufs[par]
                    else:
                        col, dstb = (piece - 4) * 2048, w2bufs[par]
                    if isinstance(x.e, bass.BassScalarEngine):
                        x.copy(dstb[:, col:col + 2048],
                               wstF[:, gslot * 2048:(gslot + 1) * 2048])
                    else:
                        x.tensor_copy(dstb[:, col:col + 2048],
                                      wstF[:, gslot * 2048:(gslot + 1) * 2048])
                    mput(f"b{b}U{u}p{piece}cast", x.mark())

                def unit_casts(u, b=b):
                    def uc_s(s, u=u):
                        for piece in (0, 2, 4):
                            cast_body(s, u, piece)
                    pr.on("s", uc_s)

                    def uc_v(v, u=u):
                        for piece in (1, 3, 5):
                            cast_body(v, u, piece)
                    pr.on("v", uc_v)

                def unit_h1(t, u, b=b):
                    g, e = u // NPE, u % NPE
                    par = u % 2
                    ch = g
                    if u == 0:
                        t.wait(mget(f"b{b}k{QB - 1}lnev"))
                        t.wait(mget(f"b{b}k{QB - 2}lnev"))
                    if e == 0:
                        t.wait(mget(f"b{b}k{4 * g + 3}gevac"))
                    t.wait(mget(f"b{b}U{u}p0cast"))
                    t.wait(mget(f"b{b}U{u}p1cast"))
                    for gp in range(4):
                        t.wait(mget(f"b{b}U{u - 1}amul{gp}"))
                        for dt in range(DS):
                            t.matmul(bank(gp),
                                     wbufs[par][:, dt * 512 + gp * P: dt * 512 + (gp + 1) * P],
                                     xn2T[:, dt * S + ch * CH: dt * S + (ch + 1) * CH],
                                     start=(dt == 0), stop=(dt == DS - 1))
                        mput(f"b{b}U{u}h1g{gp}", t.mark())
                    mput(f"b{b}U{u}h1", t.mark())

                def unit_h3(t, u, b=b):
                    g = u // NPE
                    par = u % 2
                    ch = g
                    t.wait(mget(f"b{b}U{u}p2cast"))
                    t.wait(mget(f"b{b}U{u}p3cast"))
                    for gp in range(4):
                        t.wait(mget(f"b{b}U{u}silu{gp}"))
                        for dt in range(DS):
                            t.matmul(bank(gp),
                                     wbufs[par][:, 4096 + dt * 512 + gp * P:
                                                4096 + dt * 512 + (gp + 1) * P],
                                     xn2T[:, dt * S + ch * CH: dt * S + (ch + 1) * CH],
                                     start=(dt == 0), stop=(dt == DS - 1))
                        mput(f"b{b}U{u}h3g{gp}", t.mark())
                    mput(f"b{b}U{u}h3", t.mark())

                def unit_y(t, u, b=b):
                    g, e = u // NPE, u % NPE
                    par = u % 2
                    t.wait(mget(f"b{b}U{u}amul3"))
                    t.wait(mget(f"b{b}U{u}p4cast"))
                    t.wait(mget(f"b{b}U{u}p5cast"))
                    for q_ in range(4):
                        yp = 4 + (q_ % 2) * 2
                        if q_ >= 2:
                            t.wait(mget(f"b{b}U{u}stt{q_ - 2}"))
                        else:
                            t.wait(mget(f"b{b}U{u - 1}stt{q_ + 2}"))
                        if e == 0 and q_ < 2:
                            t.wait(mget(f"b{b}k{4 * g + 3}gevac"))
                            t.wait(mget(f"b{b}k{4 * g + 3}glog"))
                        for is_ in range(4):
                            for dh in range(2):
                                t.matmul(bank(yp + dh),
                                         apb[:, par * 2048 + is_ * CH + q_ * P:
                                             par * 2048 + is_ * CH + (q_ + 1) * P],
                                         w2bufs[par][:, is_ * 1024 + dh * 512:
                                                     is_ * 1024 + (dh + 1) * 512],
                                         start=(is_ == 0), stop=(is_ == 3))
                        mput(f"b{b}U{u}y{q_}", t.mark())
                    mput(f"b{b}U{u}y3", t.mark())

                def unit_sv(u, b=b):
                    g, e = u // NPE, u % NPE

                    def us(s, u=u):
                        for gp in range(4):
                            s.wait(mget(f"b{b}U{u}h1g{gp}"))
                            s.wait(mget(f"b{b}U{u - 1}amul{gp}"))
                            if u == 0 and b > 0:
                                s.wait(mget(f"b{b - 1}ffn_v_done"))
                            s.activation(silu_sb[:, gp * CH:(gp + 1) * CH],
                                         bank(gp), AF.Silu)
                            mput(f"b{b}U{u}silu{gp}", s.mark())
                    pr.on("s", us)

                    def uv(v, u=u, g=g, e=e):
                        for gp in range(4):
                            v.wait(mget(f"b{b}U{u}silu{gp}"))
                            v.wait(mget(f"b{b}U{u}h3g{gp}"))
                            if u >= 2:
                                v.wait(mget(f"b{b}U{u - 2}y3"))
                            elif b > 0:
                                v.wait(mget(f"b{b - 1}ffn_mm_done"))
                            v.tensor_tensor(
                                out=apb[:, (u % 2) * 2048 + gp * CH:
                                        (u % 2) * 2048 + (gp + 1) * CH],
                                in0=silu_sb[:, gp * CH:(gp + 1) * CH],
                                in1=bank(gp), op=ALU.mult)
                            mput(f"b{b}U{u}amul{gp}", v.mark())
                    pr.on("v", uv)

                def unit_stt(u, b=b):
                    g, e = u // NPE, u % NPE

                    def uw(v, u=u, g=g, e=e):
                        for q_ in range(4):
                            gt = g * 4 + q_
                            yp = 4 + (q_ % 2) * 2
                            v.wait(mget(f"b{b}U{u}y{q_}"))
                            if e > 0:
                                v.wait(mget(f"b{b}U{u - 1}stt{q_}"))
                            elif b > 0:
                                v.wait(mget(f"b{b - 1}t{TT - 1}lny"))
                            ydst = Ysb[:, gt * D:(gt + 1) * D]
                            if e == 0:
                                cs = c_buf[:, gt * NEXP: gt * NEXP + 1]
                                v.tensor_scalar(out=ydst, in0=bank(yp, 2),
                                                scalar1=cs, scalar2=None,
                                                op0=ALU.mult)
                            elif e < NEXP:
                                cs = c_buf[:, gt * NEXP + e: gt * NEXP + e + 1]
                                v.scalar_tensor_tensor(out=ydst, in0=bank(yp, 2),
                                                       scalar=cs, in1=ydst,
                                                       op0=ALU.mult, op1=ALU.add)
                            else:
                                v.tensor_tensor(out=ydst, in0=bank(yp, 2),
                                                in1=ydst, op=ALU.add)
                            mput(f"b{b}U{u}stt{q_}", v.mark())
                    pr.on("v", uw)

                def fin_tile(ti, b=b):
                    g = ti // 4
                    uL = g * NPE + NPE - 1

                    def fin_v1pre(v, b=b, ti=ti):
                        pass
                    emit_ln(f"b{b}t{ti}y", Ysb[:, ti * D:(ti + 1) * D],
                            lnt2[:, (ti % 2) * D:(ti % 2 + 1) * D],
                            slot=ti % 4,
                            sq_ap=outt[:, (ti % 2) * D:(ti % 2 + 1) * D],
                            pre_v=[f"b{b}U{uL}stt{ti % 4}"] +
                                  ([f"b{b}t{ti - 2}outdma"] if ti >= 2 else
                                   ([f"b{b - 1}t{TT - 2 + ti}outdma"] if b > 0 else [])) +
                                  ([f"b{b}t{ti - 4}yln"] if ti >= 4 else []),
                            pre_s=([f"b{b}t{ti - 2}outadd"] if ti >= 2 else []))
                    mput(f"b{b}t{ti}lny", mget(f"b{b}t{ti}yln"))

                    def fin_d(q, b=b, ti=ti):
                        q.wait(mget(f"b{b}t{ti}yv1"))
                        if ti >= 2:
                            q.wait(mget(f"b{b}t{ti - 2}outadd"))
                        mput(f"b{b}t{ti}x1r",
                             q.dma(outt[:, (ti % 2) * D:(ti % 2 + 1) * D],
                                   x1park[ti * P:(ti + 1) * P, :],
                                   pr.ds("x1rA" if ti % 2 == 0 else "x1rB")))
                    pr.on("d", fin_d)

                    def fin_g2(g_, b=b, ti=ti):
                        g_.wait(mget(f"b{b}t{ti}lny"))
                        g_.wait(mget(f"b{b}t{ti}x1r"))
                        sl = (ti % 2) * D
                        g_.tensor_tensor(out=outt[:, sl:sl + D], in0=outt[:, sl:sl + D],
                                         in1=lnt2[:, sl:sl + D], op=ALU.add)
                        mput(f"b{b}t{ti}outadd", g_.mark())
                    pr.on("g", fin_g2)

                    def fin_d2(q, b=b, ti=ti):
                        q.wait(mget(f"b{b}t{ti}outadd"))
                        mput(f"b{b}t{ti}outdma",
                             q.dma(out_t[b, ti * P:(ti + 1) * P, :],
                                   outt[:, (ti % 2) * D:(ti % 2 + 1) * D],
                                   pr.ds("outA" if ti % 2 == 0 else "outB")))
                    pr.on("d", fin_d2)

                for g in range(NCH):
                    if g == 0:
                        back_ln(0)
                        back_ln(1)
                        back_tr(0)
                        back_ln(2)
                        back_tr(1)
                        back_ln(3)
                        back_tr(2)
                        back_tr(3)
                    if g > 0:
                        def yprev(t, b=b, g=g):
                            unit_y(t, g * NPE - 1)
                        pr.on("t", yprev)
                        unit_stt(g * NPE - 1)
                        back_tr(4 * g)
                        back_tr(4 * g + 1)
                        back_ln(4 * g + 2)
                        back_ln(4 * g + 3)
                        back_tr(4 * g + 2)
                        back_tr(4 * g + 3)
                    gate_group(g)
                    for e in range(NPE):
                        u = g * NPE + e
                        unit_dma(u)
                        if b == 0 and g == 0:
                            unit_casts(u)

                        def ut(t, b=b, u=u, e=e):
                            unit_h1(t, u)
                            if e > 0:
                                unit_y(t, u - 1)
                            unit_h3(t, u)
                        pr.on("t", ut)
                        if e > 0:
                            unit_stt(u - 1)
                        unit_sv(u)
                        if e == 12 and g + 1 < NCH:
                            back_ln(4 * (g + 1))
                            back_ln(4 * (g + 1) + 1)
                    if g > 0:
                        for ti in range(4 * (g - 1), 4 * g):
                            fin_tile(ti)

                def tail_t(t, b=b):
                    unit_y(t, NU - 1)
                    mput(f"b{b}ffn_mm_done", t.mark())
                pr.on("t", tail_t)
                unit_stt(NU - 1)
                for ti in range(4 * (NCH - 1), 4 * NCH):
                    fin_tile(ti)

                def fin_bar_t(t, b=b):
                    t.wait(mget(f"b{b}t{TT - 1}outdma"))
                    t.wait(mget(f"b{b}t{TT - 2}outdma"))
                    t.wait(mget(f"b{b}t{TT - 1}x1r"))
                    t.wait(mget(f"b{b}t{TT - 2}x1r"))
                    if b == 0:
                        for uu in (NPE - 2, NPE - 1):
                            t.wait(mget(f"T{uu}tee1"))
                            t.wait(mget(f"T{uu}tee2"))
                    mput(f"b{b}fin_done", t.mark())
                pr.on("t", fin_bar_t)

                def endmarks_s(s, b=b):
                    mput(f"b{b}ffn_cast_s", s.mark())
                pr.on("s", endmarks_s)

                def endmarks_g(g_, b=b):
                    mput(f"b{b}ffn_cast_g", g_.mark())
                pr.on("g", endmarks_g)

                def endmarks_v(v, b=b):
                    mput(f"b{b}ffn_v_done", v.mark())
                pr.on("v", endmarks_v)
                pr.flush()
                ck()


        except _StopBuild:
            pr.flush()
        finals = dict(pr.dcounts)

        def final_d(raw):
            for k, n in finals.items():
                if n > 0:
                    raw.wait_ge(dsems[k], n)
        block.sync(final_d)

        marks_out = dict(pr.M)
    return nc, marks_out


def kernel(**inputs):
    S = 2048
    x = np.ascontiguousarray(inputs["L0_attn_x"])
    per_core = [
        {
            "x": x[2 * c:2 * c + 2],
            "Wq": inputs["L0_attn_Wq"], "Wk": inputs["L0_attn_Wk"],
            "Wv": inputs["L0_attn_Wv"],
            "bq": np.asarray(inputs["L0_attn_bq"]).reshape(1, D),
            "bk": np.asarray(inputs["L0_attn_bk"]).reshape(1, D),
            "bv": np.asarray(inputs["L0_attn_bv"]).reshape(1, D),
            "gate_w": inputs["L0_ffn_gate_w"],
            "w1": inputs["L0_ffn_w1"], "w2": inputs["L0_ffn_w2"],
            "w3": inputs["L0_ffn_w3"],
            "ws1": inputs["L0_ffn_ws1"], "ws2": inputs["L0_ffn_ws2"],
            "ws3": inputs["L0_ffn_ws3"],
        }
        for c in range(NCORES)
    ]
    nc, marks = build(S, None)
    nc, marks = build(S, marks)
    nc, _ = build(S, marks)
    res = run_bass_kernel_spmd(nc, per_core, core_ids=list(range(NCORES)))
    out = np.concatenate([r["out"] for r in res.results], axis=0)
    return np.ascontiguousarray(out.astype(np.float32))


# version 16 (final)
# speedup vs baseline: 1.0449x; 1.0021x over previous
import numpy as np
import contextlib
import concourse.bass as bass
import concourse.mybir as mybir
from concourse.bass_utils import run_bass_kernel_spmd

F32 = mybir.dt.float32
BF16 = mybir.dt.bfloat16
AF = mybir.ActivationFunctionType
ALU = mybir.AluOpType
AX = mybir.AxisListType

P = 128
D = 1024
DS = D // P
INTER = 512
NEXP = 16
NPE = 18
EPS = 1e-5
NCORES = 8
A3LVL = 99


class Eng:
    def __init__(self, eng, sem, n):
        self.e = eng
        self.sem = sem
        self.n = n
        self._last = None
        self._marked = True

    def op(self, name, *a, **k):
        self._last = getattr(self.e, name)(*a, **k)
        self._marked = False
        return self._last

    def __getattr__(self, name):
        if name.startswith("_") or name in ("e", "sem", "n"):
            raise AttributeError(name)

        def f(*a, **k):
            return self.op(name, *a, **k)
        return f

    def mark(self):
        if not self._marked:
            self._last.then_inc(self.sem, 1)
            self.n += 1
            self._marked = True
        return (self.sem, self.n)

    def wait(self, mark):
        if mark is None:
            return
        sem, n = mark
        if n > 0:
            self.e.wait_ge(sem, n)

    def dma(self, out, in_, s):
        sem, counts, key = s
        self.e.dma_start(out=out, in_=in_).then_inc(sem, 16)
        counts[key] += 16
        return (sem, counts[key])


class Prog:
    ENGS = ("t", "v", "s", "g", "d")

    def __init__(self, block, sems, dsems, marks_in):
        self.block = block
        self.sems = sems
        self.dsems = dsems
        self.dcounts = {k: 0 for k in dsems}
        self.counts = {e: 0 for e in self.ENGS}
        self.M = dict(marks_in) if marks_in else {}
        self.q = {e: [] for e in self.ENGS}

    def ds(self, key):
        return (self.dsems[key], self.dcounts, key)

    def mget(self, key):
        return self.M.get(key)

    def mput(self, key, mark):
        self.M[key] = mark

    def on(self, eng, fn):
        self.q[eng].append(fn)

    def flush(self):
        openers = {"t": self.block.tensor, "v": self.block.vector,
                   "s": self.block.scalar, "g": self.block.gpsimd,
                   "d": self.block.sync}
        for e in self.ENGS:
            fns = self.q[e]
            if not fns:
                continue
            self.q[e] = []

            def run(raw, e=e, fns=fns):
                w = Eng(raw, self.sems[e], self.counts[e])
                for fn in fns:
                    fn(w)
                self.counts[e] = w.n
            openers[e](run)


def build(S, marks_in, stop=99):
    NB = 2
    TT = S // P
    CH = 512
    NCH = S // CH
    QB = S // P
    KC = max(1, S // 512)
    TC = NCH
    NU = NPE * TC

    nc = bass.Bass("TRN2", target_bir_lowering=False, dynamic_dma_scratch_size=4096)

    x_in = nc.dram_tensor("x", [NB, S, D], F32, kind="ExternalInput").ap()
    Wq = nc.dram_tensor("Wq", [D, D], F32, kind="ExternalInput").ap()
    Wk = nc.dram_tensor("Wk", [D, D], F32, kind="ExternalInput").ap()
    Wv = nc.dram_tensor("Wv", [D, D], F32, kind="ExternalInput").ap()
    bqd = nc.dram_tensor("bq", [1, D], F32, kind="ExternalInput").ap()
    bkd = nc.dram_tensor("bk", [1, D], F32, kind="ExternalInput").ap()
    bvd = nc.dram_tensor("bv", [1, D], F32, kind="ExternalInput").ap()
    gate_w = nc.dram_tensor("gate_w", [NEXP, D], F32, kind="ExternalInput").ap()
    w1 = nc.dram_tensor("w1", [NEXP, D, INTER], F32, kind="ExternalInput").ap()
    w2 = nc.dram_tensor("w2", [NEXP, INTER, D], F32, kind="ExternalInput").ap()
    w3 = nc.dram_tensor("w3", [NEXP, D, INTER], F32, kind="ExternalInput").ap()
    ws1 = nc.dram_tensor("ws1", [D, 2 * INTER], F32, kind="ExternalInput").ap()
    ws2 = nc.dram_tensor("ws2", [2 * INTER, D], F32, kind="ExternalInput").ap()
    ws3 = nc.dram_tensor("ws3", [D, 2 * INTER], F32, kind="ExternalInput").ap()
    out_t = nc.dram_tensor("out", [NB, S, D], F32, kind="ExternalOutput").ap()
    x1park = nc.dram_tensor("x1park", [S, D], F32, kind="Internal").ap()
    wbf16 = nc.dram_tensor("wbf16", [NPE, P, 12288], BF16, kind="Internal").ap()

    with contextlib.ExitStack() as ctx:
        ent = ctx.enter_context
        identb = ent(nc.sbuf_tensor("identb", [P, P], BF16))
        identf = ent(nc.sbuf_tensor("identf", [P, P], F32))
        ones = ent(nc.sbuf_tensor("ones", [1, P], F32))
        onescol = ent(nc.sbuf_tensor("onescol", [P, 1], F32))
        gwT = ent(nc.sbuf_tensor("gwT", [P, DS * NEXP], F32))
        bqc = ent(nc.sbuf_tensor("bqc", [P, DS], F32))
        bkc = ent(nc.sbuf_tensor("bkc", [P, DS], F32))
        st = ent(nc.sbuf_tensor("st", [P, 32], F32))
        den = ent(nc.sbuf_tensor("den", [P, 2 * QB], F32))
        c_buf = ent(nc.sbuf_tensor("c_buf", [P, TT * NEXP], F32))
        glog = ent(nc.sbuf_tensor("glog", [P, QB * NEXP], F32))
        smkb = ent(nc.sbuf_tensor("smkb", [P, QB * NEXP], F32))
        gsc = ent(nc.sbuf_tensor("gsc", [P, 10 * QB], F32))

        RA = ent(nc.sbuf_tensor("RA", [P, 16384], F32))
        RQ = ent(nc.sbuf_tensor("RQ", [P, 16384], BF16))
        RK = ent(nc.sbuf_tensor("RK", [P, 8192], F32))
        RV = ent(nc.sbuf_tensor("RV", [P, 8192], F32))
        RW = ent(nc.sbuf_tensor("RW", [P, 24576], BF16))

        PS = ent(nc.psum_tensor("PS", [P, 4096], F32))

        sems = {}
        for e in Prog.ENGS:
            sems[e] = ent(nc.semaphore("sem_" + e))
        DSEM_NAMES = ["xA", "xB", "xC", "xD", "wA", "wB", "fA", "fB", "xrA", "xrB",
                      "pkA", "pkB", "x1rA", "x1rB", "outA", "outB",
                      "br0", "br1", "br2", "teA", "teB", "teC", "teD",
                      "w2A", "w2B"]
        dsems = {k: ent(nc.semaphore("dm_" + k)) for k in DSEM_NAMES}
        block = ent(nc.Block())
        pr = Prog(block, sems, dsems, marks_in)
        mget, mput = pr.mget, pr.mput
        phase_no = [0]

        def phase_on():
            phase_no[0] += 1
            return phase_no[0] <= stop

        RAb = RA[:].bitcast(BF16)
        Wbf = RAb[:, 0:8192]
        wstA = RA[:, 12288:16384]
        brow = RV[0:1, 0:3072]
        gwsv = RV[0:NEXP, 3072:4096]
        xstage = RA[:, 4096:8192]
        xnbf = RAb[:, 16384:20480]
        sqA1 = RA[:, 10240:11264]
        bvrep = RA[:, 11264:12288]
        xnTc = RAb[:, 0:4096]
        Ab = RAb
        Ysb = RA

        QTb = RQ
        xn2T = RQ
        KTb = RK[:].bitcast(BF16)
        sqdum = RK[:, 0:2048]
        x1t = RK[:, 2048:4096]
        silu_sb = RK[:, 0:2048]
        xnft = RK[:, 2048:4096]
        xfTt = RK[:, 4096:6144]
        apb = RK[:].bitcast(BF16)[:, 12288:16384]
        Vb = RV[:].bitcast(BF16)
        wstF = RV[:, 0:4096]
        outt = RV[:, 4096:6144]
        lnt2 = RV[:, 6144:8192]
        WTq = RW[:, 0:8192]
        WTk = RW[:, 8192:16384]
        WTv = RW[:, 16384:24576]
        wbufs = [WTq, WTk]
        w2bufs = [WTv[:, 0:4096], WTv[:, 4096:8192]]

        def bank(i, w=1):
            return PS[:, i * 512:(i + w) * 512]

        def bankb(i, w=1):
            return PS[:, i * 512:(i + w) * 512].bitcast(BF16)

        def emit_ln(key, src_ap, dst_ap, slot, sq_ap, pre_v=None, pre_s=None,
                    out_dt=None):
            c0 = slot * 4
            negm = st[:, c0:c0 + 1]
            veps = st[:, c0 + 1:c0 + 2]
            rstd = st[:, c0 + 2:c0 + 3]
            mrstd = st[:, c0 + 3:c0 + 4]

            def vb(v):
                for m in (pre_v or []):
                    v.wait(mget(m) if isinstance(m, str) else m)
                v.reduce_sum(negm, src_ap, axis=AX.X)
                v.wait(v.mark())
                v.tensor_scalar_mul(negm, negm, -1.0 / D)
                v.wait(v.mark())
                v.scalar_tensor_tensor(out=sq_ap, in0=src_ap, scalar=negm,
                                       in1=src_ap, op0=ALU.add, op1=ALU.mult,
                                       accum_out=veps)
                mput(key + "v1", v.mark())
                v.wait(mget(key + "v1"))
                v.tensor_scalar(out=veps, in0=veps, scalar1=1.0 / D, scalar2=EPS,
                                op0=ALU.mult, op1=ALU.add)
                mput(key + "vs", v.mark())
            pr.on("v", vb)

            def sb(s):
                s.wait(mget(key + "vs"))
                s.activation(rstd, veps, AF.Sqrt)
                mput(key + "sq", s.mark())
            pr.on("s", sb)

            def vb2(v):
                v.wait(mget(key + "sq"))
                v.reciprocal(rstd, rstd)
                v.wait(v.mark())
                v.tensor_tensor(out=mrstd, in0=negm, in1=rstd, op=ALU.mult)
                v.wait(v.mark())
                for m in (pre_s or []):
                    v.wait(mget(m) if isinstance(m, str) else m)
                v.tensor_scalar(out=dst_ap, in0=src_ap, scalar1=rstd,
                                scalar2=mrstd, op0=ALU.mult, op1=ALU.add)
                mput(key + "ln", v.mark())
            pr.on("v", vb2)

        def ph0_d(q):
            mput("d_bq", q.dma(brow[:, 0:D], bqd, pr.ds("br0")))
            mput("d_bk", q.dma(brow[:, D:2 * D], bkd, pr.ds("br1")))
            mput("d_gws", q.dma(gwsv[:], gate_w, pr.ds("br2")))
        pr.on("d", ph0_d)

        def ph0_g(g):
            g.memset(identb[:], 0.0)
            g.memset(identf[:], 0.0)
            g.memset(ones[:], 1.0)
            g.memset(onescol[:], 1.0)
            g.wait(g.mark())
            g.affine_select(out=identb[:], in_=identb[:], compare_op=ALU.not_equal,
                            fill=1.0, base=0, pattern=[[-1, P]], channel_multiplier=1)
            g.affine_select(out=identf[:], in_=identf[:], compare_op=ALU.not_equal,
                            fill=1.0, base=0, pattern=[[-1, P]], channel_multiplier=1)
            mput("g_ident", g.mark())
        pr.on("g", ph0_g)

        def ph0_t(t):
            t.wait(mget("d_bq"))
            t.wait(mget("d_bk"))
            t.wait(mget("d_gws"))
            t.wait(mget("g_ident"))
            for j in range(DS):
                t.matmul(bank(7)[:, j:j + 1], brow[0:1, j * P:(j + 1) * P],
                         ones[0:1, 0:1], start=True, stop=True)
                t.matmul(bank(7)[:, 8 + j:9 + j], brow[0:1, D + j * P:D + (j + 1) * P],
                         ones[0:1, 0:1], start=True, stop=True)
            for j in range(DS):
                t.transpose(bank(7)[:, 16 + j * 16:16 + (j + 1) * 16],
                            gwsv[:, j * P:(j + 1) * P], identf[0:NEXP, 0:NEXP])
            mput("t_ph0", t.mark())
        pr.on("t", ph0_t)

        def ph0_s(s):
            s.wait(mget("t_ph0"))
            s.copy(bqc[:], bank(7)[:, 0:DS])
            s.copy(bkc[:], bank(7)[:, 8:8 + DS])
            s.copy(gwT[:], bank(7)[:, 16:16 + DS * NEXP])
            mput("s_bias", s.mark())
        pr.on("s", ph0_s)
        pr.flush()

        class _StopBuild(Exception):
            pass

        fcount = [0]

        def ck():
            fcount[0] += 1
            if fcount[0] >= stop:
                raise _StopBuild()

        try:
            for b in range(NB):
                def a0_brow(q, b=b):
                    if b == 0:
                        q.wait(mget("t_ph0"))
                    else:
                        q.wait(mget(f"b{b - 1}bvmm"))
                    mput(f"b{b}brow", q.dma(brow[:, 2 * D:3 * D], bvd, pr.ds("br2")))
                pr.on("d", a0_brow)

                for wi, (Wsrc, WTdst) in enumerate([(Wq, WTq), (Wk, WTk), (Wv, WTv)]):
                    key = f"b{b}w{wi}"

                    def a0_d(q, Wsrc=Wsrc, key=key, wi=wi, b=b):
                        if wi > 0:
                            q.wait(mget(f"b{b}w{wi - 1}cast"))
                        elif b > 0:
                            q.wait(mget(f"b{b - 1}ffn_cast_s"))
                            q.wait(mget(f"b{b - 1}t{TT - 1}lny"))
                        mput(key + "h0", q.dma(
                            wstA[:].rearrange("p (a d) -> p a d", a=4),
                            Wsrc[0:512, :].rearrange("(a p) d -> p a d", p=P),
                            pr.ds("wA")))
                    pr.on("d", a0_d)

                    def a0_v(v, key=key, wi=wi, b=b):
                        if wi > 0:
                            v.wait(mget(f"b{b}w{wi - 1}tr"))
                        elif b > 0:
                            v.wait(mget(f"b{b - 1}fin_done"))
                        v.wait(mget(key + "h0"))
                        v.tensor_copy(Wbf[:, 0:4096], wstA[:])
                        mput(key + "c0", v.mark())
                    pr.on("v", a0_v)

                    def a0_d1(q, Wsrc=Wsrc, key=key, b=b):
                        q.wait(mget(key + "c0"))
                        mput(key + "h1", q.dma(
                            wstA[:].rearrange("p (a d) -> p a d", a=4),
                            Wsrc[512:1024, :].rearrange("(a p) d -> p a d", p=P),
                            pr.ds("wB")))
                    pr.on("d", a0_d1)

                    def a0_v1(v, key=key, b=b):
                        v.wait(mget(key + "h1"))
                        v.tensor_copy(Wbf[:, 4096:8192], wstA[:])
                        mput(key + "cast", v.mark())
                    pr.on("v", a0_v1)

                    def a0_t(t, key=key, wi=wi, b=b):
                        t.wait(mget(key + "cast"))
                        if b == 0 and wi == 0:
                            t.wait(mget("s_bias"))
                        if b > 0 and wi == 0:
                            t.wait(mget(f"b{b - 1}fin_done"))
                        for dt in range(DS):
                            pb = 6 + dt % 2
                            if dt >= 2:
                                t.wait(mget(key + f"tre{dt - 2}"))
                            elif wi > 0:
                                t.wait(mget(f"b{b}w{wi - 1}tre{dt + 6}"))
                            for a in range(DS):
                                t.transpose(bankb(pb)[:, a * P:(a + 1) * P],
                                            Wbf[:, a * 1024 + dt * P: a * 1024 + dt * P + P],
                                            identb[:])
                            mput(key + f"tr{dt}", t.mark())
                        mput(key + "tr", t.mark())
                    pr.on("t", a0_t)

                    def a0_s(s, key=key, WTdst=WTdst):
                        for dt in range(DS):
                            pb = 6 + dt % 2
                            s.wait(mget(key + f"tr{dt}"))
                            s.copy(WTdst[:, dt * 1024:(dt + 1) * 1024], bankb(pb)[:, 0:1024])
                            mput(key + f"tre{dt}", s.mark())
                    pr.on("s", a0_s)

                def a0_bv_t(t, b=b):
                    t.wait(mget(f"b{b}brow"))
                    t.wait(mget(f"b{b}w2tre7"))
                    if b > 0:
                        t.wait(mget(f"b{b - 1}k{QB - 1}gevf"))
                        t.wait(mget(f"b{b - 1}k{QB - 1}gevac"))
                    t.matmul(bank(4), ones[0:1, 0:P], brow[0:1, 2 * D:2 * D + 512],
                             start=True, stop=True)
                    t.matmul(bank(5), ones[0:1, 0:P], brow[0:1, 2 * D + 512:3 * D],
                             start=True, stop=True)
                    mput(f"b{b}bvmm", t.mark())
                pr.on("t", a0_bv_t)

                def a0_bv_s(s, b=b):
                    s.wait(mget(f"b{b}bvmm"))
                    s.wait(mget(f"b{b}w2cast"))
                    s.copy(bvrep[:, 0:512], bank(4))
                    s.copy(bvrep[:, 512:1024], bank(5))
                    mput(f"b{b}bvrep", s.mark())
                pr.on("s", a0_bv_s)

                for cc in range(NCH):
                    gcc = b * NCH + cc

                    def a1_d(q, b=b, cc=cc, gcc=gcc):
                        if cc == 0 and b > 0:
                            q.wait(mget(f"b{b - 1}t{TT - 1}lny"))
                        for ti in range(4):
                            gt = cc * 4 + ti
                            gti = b * TT + gt
                            q.wait(mget(f"T{gti - 4}ln"))
                            mput(f"T{gti}x",
                                 q.dma(xstage[:, (gti % 4) * D:(gti % 4 + 1) * D],
                                       x_in[b, gt * P:(gt + 1) * P, :],
                                       pr.ds(["xA", "xB", "xC", "xD"][gti % 4])))
                    pr.on("d", a1_d)

                    for ti in range(4):
                        gt = cc * 4 + ti
                        gti = b * TT + gt
                        xt = xstage[:, (gti % 4) * D:(gti % 4 + 1) * D]
                        emit_ln(f"T{gti}", xt, xnbf[:, (gti % 4) * D:(gti % 4 + 1) * D],
                                slot=gti % 4,
                                sq_ap=sqA1[:, 0:D],
                                pre_v=[f"T{gti}x"],
                                pre_s=[f"T{gti - 4}tp"])

                        def a1_tr(t, b=b, cc=cc, gcc=gcc, ti=ti, gti=gti):
                            t.wait(mget(f"T{gti}ln"))
                            pb = 6 + gti % 2
                            t.wait(mget(f"T{gti - 2}tre"))
                            if cc == 0 and ti < 2:
                                t.wait(mget(f"b{b}w2tre{6 + ti}"))
                            for js in range(DS):
                                t.transpose(bankb(pb)[:, js * P:(js + 1) * P],
                                            xnbf[:, (gti % 4) * D + js * P:
                                                 (gti % 4) * D + (js + 1) * P],
                                            identb[:])
                            mput(f"T{gti}tp", t.mark())
                        pr.on("t", a1_tr)

                        def a1_ev(v, gcc=gcc, ti=ti, gti=gti):
                            v.wait(mget(f"T{gti}tp"))
                            if ti == 0:
                                v.wait(mget(f"g{gcc - 1}qkvmm"))
                            pb = 6 + gti % 2
                            v.tensor_copy(
                                xnTc.rearrange("p (a t) -> p a t", a=DS)[:, :, ti * P:(ti + 1) * P],
                                bankb(pb)[:, 0:1024].rearrange("p (a t) -> p a t", a=DS))
                            mput(f"T{gti}tre", v.mark())
                        pr.on("v", a1_ev)

                    def a1_t(t, b=b, cc=cc, gcc=gcc):
                        for ti in range(4):
                            t.wait(mget(f"T{b * TT + cc * 4 + ti}tre"))
                        if cc == 0:
                            t.wait(mget(f"b{b}w2tre7"))
                        def vtile(ti):
                            pp = 2 + (ti % 2) * 2
                            if ti >= 2:
                                t.wait(mget(f"g{gcc}vev{ti - 2}"))
                            else:
                                t.wait(mget(f"g{gcc - 1}vev{ti + 2}"))
                                if cc == 0 and ti == 1:
                                    t.wait(mget(f"b{b}bvrep"))
                            for eh in range(2):
                                for dt in range(DS):
                                    t.matmul(bank(pp + eh),
                                             xnTc[:, dt * CH + ti * P: dt * CH + (ti + 1) * P],
                                             WTv[:, dt * 1024 + eh * 512: dt * 1024 + (eh + 1) * 512],
                                             start=(dt == 0), stop=(dt == DS - 1))
                            mput(f"g{gcc}vmm{ti}", t.mark())
                        for qki in range(2 * DS):
                            which, es = qki // DS, qki % DS
                            WT = WTq if which == 0 else WTk
                            pb = qki % 4
                            if qki >= 4:
                                t.wait(mget(f"g{gcc}qke{qki - 4}"))
                            else:
                                t.wait(mget(f"g{gcc - 1}qke{qki + 12}"))
                            for dt in range(DS):
                                t.matmul(bank(pb),
                                         WT[:, dt * 1024 + es * P: dt * 1024 + (es + 1) * P],
                                         xnTc[:, dt * CH:(dt + 1) * CH],
                                         start=(dt == 0), stop=(dt == DS - 1))
                            mput(f"g{gcc}qkmm{qki}", t.mark())
                        for ti in range(4):
                            if ti == 0:
                                t.wait(mget(f"g{gcc}qke{14}"))
                                t.wait(mget(f"g{gcc}qke{15}"))
                            vtile(ti)
                        mput(f"g{gcc}qkvmm", t.mark())
                    pr.on("t", a1_t)

                    def a1_se(s, b=b, cc=cc, gcc=gcc):
                        for qki in range(2 * DS):
                            which, es = qki // DS, qki % DS
                            dstb = QTb if which == 0 else KTb
                            bc = bqc if which == 0 else bkc
                            pb = qki % 4
                            s.wait(mget(f"g{gcc}qkmm{qki}"))
                            s.activation(dstb[:, es * S + cc * CH: es * S + (cc + 1) * CH],
                                         bank(pb), AF.Identity,
                                         bias=bc[:, es:es + 1], scale=1.0)
                            mput(f"g{gcc}qke{qki}", s.mark())
                    pr.on("s", a1_se)

                    def a1_ve(v, b=b, cc=cc, gcc=gcc):
                        for ti in range(4):
                            gt = cc * 4 + ti
                            pp = 2 + (ti % 2) * 2
                            v.wait(mget(f"g{gcc}vmm{ti}"))
                            if cc == 0 and ti == 0:
                                v.wait(mget(f"b{b}bvrep"))
                            v.tensor_tensor(out=Vb[:, gt * D:(gt + 1) * D],
                                            in0=bank(pp, 2), in1=bvrep[:], op=ALU.add)
                            mput(f"g{gcc}vev{ti}", v.mark())
                    pr.on("v", a1_ve)
                pr.flush()
                ck()

                for qb in range(QB):
                    st4 = (qb % 2) * 4

                    def a2_t(t, b=b, qb=qb, st4=st4):
                        if qb == 0:
                            t.wait(mget(f"g{b * NCH + NCH - 1}qke{2 * DS - 1}"))
                            t.wait(mget(f"g{b * NCH + NCH - 1}vev3"))
                        t.wait(mget(f"b{b}q{qb - 2}exp"))
                        for kc in range(KC):
                            for dt in range(DS):
                                t.matmul(bank(st4 + kc),
                                         QTb[:, dt * S + qb * P: dt * S + (qb + 1) * P],
                                         KTb[:, dt * S + kc * 512: dt * S + (kc + 1) * 512],
                                         start=(dt == 0), stop=(dt == DS - 1))
                        mput(f"b{b}q{qb}mm", t.mark())
                    pr.on("t", a2_t)

                    def a2_v(v, b=b, qb=qb, st4=st4):
                        v.wait(mget(f"b{b}q{qb}mm"))
                        v.reduce_max(den[:, qb:qb + 1], bank(st4, KC), axis=AX.X,
                                     negate=True)
                        mput(f"b{b}q{qb}max", v.mark())
                    pr.on("v", a2_v)

                    def a2_s(s, b=b, qb=qb, st4=st4):
                        s.wait(mget(f"b{b}q{qb}max"))
                        if qb == 0 and b > 0:
                            s.wait(mget(f"b{b - 1}fin_done"))
                        s.mul(den[:, QB + qb:QB + qb + 1], den[:, qb:qb + 1], 1.0 / 32.0)
                        s.wait(s.mark())
                        s.activation(Ab[:, qb * S:(qb + 1) * S], bank(st4, KC), AF.Exp,
                                     bias=den[:, QB + qb:QB + qb + 1], scale=1.0 / 32.0,
                                     accum_out=den[:, qb:qb + 1])
                        mput(f"b{b}q{qb}exp", s.mark())
                    pr.on("s", a2_s)

                    def a2_v2(v, b=b, qb=qb):
                        v.wait(mget(f"b{b}q{qb}exp"))
                        v.reciprocal(den[:, QB + qb:QB + qb + 1], den[:, qb:qb + 1])
                        mput(f"b{b}q{qb}rec", v.mark())
                    pr.on("v", a2_v2)

                    def a2_s2(s, b=b, qb=qb):
                        s.wait(mget(f"b{b}q{qb}rec"))
                        s.activation(Vb[:, qb * D:(qb + 1) * D], Vb[:, qb * D:(qb + 1) * D],
                                     AF.Identity, bias=0.0,
                                     scale=den[:, QB + qb:QB + qb + 1])
                        mput(f"b{b}q{qb}vs", s.mark())
                    pr.on("s", a2_s2)
                pr.flush()
                ck()

                for kb in range(QB):
                    pp = (kb % 2) * 2

                    def a3_d(q, b=b, kb=kb):
                        q.wait(mget(f"b{b}k{kb - 2}park"))
                        if kb < 2:
                            q.wait(mget(f"b{b}q{QB - 1}mm"))
                            if b > 0:
                                q.wait(mget(f"b{b - 1}k{QB - 1}gtr"))
                                q.wait(mget(f"b{b - 1}ffn_v_done"))
                        mput(f"b{b}k{kb}xr",
                             q.dma(x1t[:, (kb % 2) * D:(kb % 2 + 1) * D],
                                   x_in[b, kb * P:(kb + 1) * P, :],
                                   pr.ds("xrA" if kb % 2 == 0 else "xrB")))
                    pr.on("d", a3_d)

                    def a3_t(t, b=b, kb=kb, pp=pp):
                        if kb == 0:
                            t.wait(mget(f"b{b}q{QB - 1}vs"))
                            t.wait(mget(f"b{b}q{QB - 1}exp"))
                            t.wait(mget(f"b{b}q{QB - 1}rec"))
                            if b == 0:
                                t.wait(mget("s_bias"))
                        t.wait(mget(f"b{b}k{kb - 2}lnev"))
                        for qc in range(QB):
                            for dh in range(2):
                                t.matmul(bank(pp + dh),
                                         Ab[:, qc * S + kb * P: qc * S + (kb + 1) * P],
                                         Vb[:, qc * D + dh * 512: qc * D + (dh + 1) * 512],
                                         start=(qc == 0), stop=(qc == QB - 1))
                        mput(f"b{b}k{kb}mm", t.mark())
                    pr.on("t", a3_t)

                    def a3_sq(s, b=b, kb=kb, pp=pp):
                        c0 = (4 + kb % 2) * 4
                        s.wait(mget(f"b{b}k{kb}mm"))
                        if kb < 2 and b > 0:
                            s.wait(mget(f"b{b - 1}ffn_v_done"))
                        s.activation(sqdum[:, (kb % 2) * D:(kb % 2 + 1) * D],
                                     bank(pp, 2), AF.Square,
                                     accum_out=st[:, c0 + 1:c0 + 2])
                        mput(f"b{b}k{kb}asq", s.mark())
                    pr.on("s", a3_sq)

                    def a3_av(v, b=b, kb=kb, pp=pp):
                        c0 = (4 + kb % 2) * 4
                        negm = st[:, c0:c0 + 1]
                        veps = st[:, c0 + 1:c0 + 2]
                        rstd = st[:, c0 + 2:c0 + 3]
                        v.wait(mget(f"b{b}k{kb}asq"))
                        v.reduce_sum(negm, bank(pp, 2), axis=AX.X)
                        v.wait(v.mark())
                        v.tensor_scalar_mul(negm, negm, -1.0 / D)
                        v.wait(v.mark())
                        v.tensor_tensor(out=rstd, in0=negm, in1=negm, op=ALU.mult)
                        v.wait(v.mark())
                        v.tensor_scalar(out=veps, in0=veps, scalar1=1.0 / D,
                                        scalar2=EPS, op0=ALU.mult, op1=ALU.add)
                        v.wait(v.mark())
                        v.tensor_tensor(out=veps, in0=veps, in1=rstd, op=ALU.subtract)
                        mput(f"b{b}k{kb}avs", v.mark())
                    pr.on("v", a3_av)

                    def a3_as(s, b=b, kb=kb):
                        c0 = (4 + kb % 2) * 4
                        s.wait(mget(f"b{b}k{kb}avs"))
                        s.activation(st[:, c0 + 2:c0 + 3], st[:, c0 + 1:c0 + 2],
                                     AF.Sqrt)
                        mput(f"b{b}k{kb}asr", s.mark())
                    pr.on("s", a3_as)

                    def a3_av2(v, b=b, kb=kb, pp=pp):
                        c0 = (4 + kb % 2) * 4
                        sl = (kb % 2) * D
                        v.wait(mget(f"b{b}k{kb}asr"))
                        v.reciprocal(st[:, c0 + 2:c0 + 3], st[:, c0 + 2:c0 + 3])
                        v.wait(v.mark())
                        v.tensor_tensor(out=st[:, c0 + 3:c0 + 4], in0=st[:, c0:c0 + 1],
                                        in1=st[:, c0 + 2:c0 + 3], op=ALU.mult)
                        v.wait(v.mark())
                        v.wait(mget(f"b{b}k{kb}xr"))
                        v.scalar_tensor_tensor(out=x1t[:, sl:sl + D], in0=bank(pp, 2),
                                               scalar=st[:, c0 + 2:c0 + 3],
                                               in1=x1t[:, sl:sl + D],
                                               op0=ALU.mult, op1=ALU.add)
                        v.wait(v.mark())
                        v.tensor_scalar(out=x1t[:, sl:sl + D], in0=x1t[:, sl:sl + D],
                                        scalar1=st[:, c0 + 3:c0 + 4], scalar2=None,
                                        op0=ALU.add)
                        mput(f"b{b}k{kb}lnev", v.mark())
                    pr.on("v", a3_av2)

                    def a3_d2(q, b=b, kb=kb):
                        q.wait(mget(f"b{b}k{kb}lnev"))
                        mput(f"b{b}k{kb}park",
                             q.dma(x1park[kb * P:(kb + 1) * P, :],
                                   x1t[:, (kb % 2) * D:(kb % 2 + 1) * D],
                                   pr.ds("pkA" if kb % 2 == 0 else "pkB")))
                    pr.on("d", a3_d2)

                def wsrc(e, piece):
                    if e < NEXP:
                        if piece < 2:
                            srcw = w1[e]
                        elif piece < 4:
                            srcw = w3[e]
                        else:
                            srcw = w2[e]
                        h = piece % 2
                        if piece < 4:
                            return srcw.rearrange("(a p) i -> p a i", p=P)[:, h * 4:(h + 1) * 4, :]
                        return srcw.rearrange("(a p) d -> p a d", p=P)[:, h * 2:(h + 1) * 2, :]
                    hh = e - NEXP
                    if piece < 2:
                        return ws1.rearrange("(a p) i -> p a i", p=P)[
                            :, piece * 4:(piece + 1) * 4, hh * 512:(hh + 1) * 512]
                    if piece < 4:
                        return ws3.rearrange("(a p) i -> p a i", p=P)[
                            :, (piece - 2) * 4:(piece - 1) * 4, hh * 512:(hh + 1) * 512]
                    return ws2.rearrange("(a p) d -> p a d", p=P)[
                        :, hh * 4 + (piece - 4) * 2: hh * 4 + (piece - 4) * 2 + 2, :]

                def back_ln(kb, b=b):
                    def bk_d(q, b=b, kb=kb):
                        q.wait(mget(f"b{b}k{kb - 2}gtr"))
                        if kb < 2:
                            q.wait(mget(f"b{b}k{QB - 2 + kb % 2}park"))
                        mput(f"b{b}k{kb}x1l",
                             q.dma(xnft[:, (kb % 2) * D:(kb % 2 + 1) * D],
                                   x1park[kb * P:(kb + 1) * P, :],
                                   pr.ds("xrA" if kb % 2 == 0 else "xrB")))
                    pr.on("d", bk_d)

                    emit_ln(f"b{b}k{kb}x1",
                            xnft[:, (kb % 2) * D:(kb % 2 + 1) * D],
                            xnft[:, (kb % 2) * D:(kb % 2 + 1) * D],
                            slot=6 + kb % 2,
                            sq_ap=xfTt[:, (kb % 2) * D:(kb % 2 + 1) * D],
                            pre_v=[f"b{b}k{kb}x1l", f"b{b}k{kb - 2}gmm",
                                   f"b{b}k{kb - 2}gevf"] +
                                  ([f"b{b - 1}k{QB - 1}gmm"] if kb < 2 and b > 0 else []),
                            pre_s=[])
                    mput(f"b{b}k{kb}xn2ln", mget(f"b{b}k{kb}x1ln"))

                def back_tr(kb, b=b):
                    def bk_t(t, b=b, kb=kb):
                        t.wait(mget(f"b{b}k{kb}xn2ln"))
                        t.wait(mget(f"b{b}k{kb - 1}gevac"))
                        if kb % 4 == 0 and kb >= 4:
                            uL = (kb // 4 - 1) * NPE + NPE - 1
                            t.wait(mget(f"b{b}U{uL}stt2"))
                            t.wait(mget(f"b{b}U{uL}stt3"))
                        for js in range(DS):
                            t.transpose(bank(4 + js // 4)[:, (js % 4) * P:(js % 4 + 1) * P],
                                        xnft[:, (kb % 2) * D + js * P: (kb % 2) * D + (js + 1) * P],
                                        identf[:])
                        mput(f"b{b}k{kb}gtr", t.mark())
                    pr.on("t", bk_t)

                    def bk_s(s, b=b, kb=kb):
                        s.wait(mget(f"b{b}k{kb}gtr"))
                        s.wait(mget(f"b{b}k{kb - 2}gmm"))
                        if kb < 2 and b > 0:
                            s.wait(mget(f"b{b - 1}ffn_mm_done"))
                        sl = (kb % 2) * D
                        s.copy(xfTt[:, sl:sl + 512], bank(4))
                        s.copy(xfTt[:, sl + 512:sl + D], bank(5))
                        mput(f"b{b}k{kb}gevf", s.mark())
                        dst = xn2T[:, 0:DS * S].rearrange("p (a t) -> p a t", a=DS)
                        s.copy(dst[:, 0:4, kb * P:(kb + 1) * P],
                               bank(4).rearrange("p (a t) -> p a t", a=4))
                        s.copy(dst[:, 4:8, kb * P:(kb + 1) * P],
                               bank(5).rearrange("p (a t) -> p a t", a=4))
                        mput(f"b{b}k{kb}gevac", s.mark())
                    pr.on("s", bk_s)

                    def bk_t2(t, b=b, kb=kb):
                        t.wait(mget(f"b{b}k{kb}gevf"))
                        t.wait(mget(f"b{b}k{kb - 2}glog"))
                        gb = 6 + kb % 2
                        sl = (kb % 2) * D
                        for dt in range(DS):
                            t.matmul(bank(gb)[:, 0:NEXP],
                                     xfTt[:, sl + dt * P: sl + (dt + 1) * P],
                                     gwT[:, dt * NEXP:(dt + 1) * NEXP],
                                     start=(dt == 0), stop=(dt == DS - 1))
                        mput(f"b{b}k{kb}gmm", t.mark())
                    pr.on("t", bk_t2)

                    def bk_gl(s, b=b, kb=kb):
                        gb = 6 + kb % 2
                        s.wait(mget(f"b{b}k{kb}gmm"))
                        if kb < 2 and b > 0:
                            s.wait(mget(f"b{b - 1}gate{NCH - 1}"))
                        s.copy(glog[:, kb * NEXP:(kb + 1) * NEXP],
                               bank(gb)[:, 0:NEXP])
                        mput(f"b{b}k{kb}glog", s.mark())
                    pr.on("s", bk_gl)

                def gate_group(g, b=b):
                    KB = 4
                    E = NEXP

                    def gexp(s, b=b, g=g):
                        s.wait(mget(f"b{b}k{4 * g + 3}glog"))
                        s.activation(glog[:, g * KB * E:(g + 1) * KB * E],
                                     glog[:, g * KB * E:(g + 1) * KB * E], AF.Exp)
                        mput(f"b{b}gexp{g}", s.mark())
                    pr.on("s", gexp)

                    def gv(v, b=b, g=g):
                        gl = glog[:, g * KB * E:(g + 1) * KB * E]
                        sm = smkb[:, g * KB * E:(g + 1) * KB * E]
                        cb = c_buf[:, g * KB * E:(g + 1) * KB * E]
                        gmax = gsc[:, 0:4 * KB]
                        den16 = gsc[:, 4 * KB:5 * KB]
                        gtmp = gsc[:, 5 * KB:9 * KB]
                        gcnt = sm[:, 0:4 * KB]
                        v.wait(mget(f"b{b}gexp{g}"))
                        if g == 0 and b > 0:
                            v.wait(mget(f"b{b - 1}ffn_v_done"))
                        g3 = gl.rearrange("p (k e) -> p k e", k=KB)
                        v.tensor_reduce(out=den16, in_=g3, axis=AX.X, op=ALU.add)
                        v.wait(v.mark())
                        v.reciprocal(den16, den16)
                        v.wait(v.mark())
                        v.tensor_tensor(out=g3, in0=g3,
                                        in1=den16.rearrange("p (k o) -> p k o", o=1
                                                            ).to_broadcast([P, KB, E]),
                                        op=ALU.mult)
                        v.wait(v.mark())
                        v.tensor_reduce(out=gmax,
                                        in_=gl.rearrange("p (kg c) -> p kg c", c=4),
                                        axis=AX.X, op=ALU.max)
                        v.wait(v.mark())
                        gm3 = gmax.rearrange("p (k g) -> p k g", g=4)
                        gc3 = gcnt.rearrange("p (k g) -> p k g", g=4)
                        gt3 = gtmp[:, 0:4 * KB].rearrange("p (k g) -> p k g", g=4)
                        for sh in range(1, 4):
                            for piece in range(2):
                                if piece == 0:
                                    w = 4 - sh
                                    a_ap, b_ap = gm3[:, :, sh:4], gm3[:, :, 0:w]
                                    o_ap, t_ap = gc3[:, :, 0:w], gt3[:, :, 0:w]
                                else:
                                    w = sh
                                    a_ap, b_ap = gm3[:, :, 0:sh], gm3[:, :, 4 - sh:4]
                                    o_ap, t_ap = gc3[:, :, 4 - sh:4], gt3[:, :, 4 - sh:4]
                                v.tensor_tensor(out=t_ap, in0=a_ap, in1=b_ap, op=ALU.is_gt)
                                v.wait(v.mark())
                                if sh == 1:
                                    v.tensor_copy(o_ap, t_ap)
                                else:
                                    v.tensor_tensor(out=o_ap, in0=o_ap, in1=t_ap, op=ALU.add)
                                v.wait(v.mark())
                        v.tensor_scalar(out=gcnt, in0=gcnt, scalar1=1.5, scalar2=None,
                                        op0=ALU.is_le)
                        v.wait(v.mark())
                        v.tensor_copy(gtmp[:, 0:4 * KB], gcnt)
                        v.wait(v.mark())
                        v.tensor_tensor(out=sm.rearrange("p (kg c) -> p kg c", c=4),
                                        in0=gl.rearrange("p (kg c) -> p kg c", c=4),
                                        in1=gtmp[:, 0:4 * KB].rearrange("p (kg o) -> p kg o", o=1
                                                                        ).to_broadcast([P, 4 * KB, 4]),
                                        op=ALU.mult)
                        v.wait(v.mark())
                        s3 = sm.rearrange("p (k e) -> p k e", k=KB)
                        c3 = cb.rearrange("p (k e) -> p k e", k=KB)
                        t3 = gl.rearrange("p (k e) -> p k e", k=KB)
                        for sh in range(1, E):
                            for piece in range(2):
                                if piece == 0:
                                    w = E - sh
                                    a_ap, b_ap = s3[:, :, sh:E], s3[:, :, 0:w]
                                    o_ap, t_ap = c3[:, :, 0:w], t3[:, :, 0:w]
                                else:
                                    w = sh
                                    a_ap, b_ap = s3[:, :, 0:sh], s3[:, :, E - sh:E]
                                    o_ap, t_ap = c3[:, :, E - sh:E], t3[:, :, E - sh:E]
                                v.tensor_tensor(out=t_ap, in0=a_ap, in1=b_ap, op=ALU.is_gt)
                                v.wait(v.mark())
                                if sh == 1:
                                    v.tensor_copy(o_ap, t_ap)
                                else:
                                    v.tensor_tensor(out=o_ap, in0=o_ap, in1=t_ap, op=ALU.add)
                                v.wait(v.mark())
                        v.tensor_scalar(out=cb, in0=cb, scalar1=3.5, scalar2=None,
                                        op0=ALU.is_le)
                        v.wait(v.mark())
                        v.tensor_tensor(out=cb, in0=cb, in1=sm, op=ALU.mult)
                        mput(f"b{b}gate{g}", v.mark())
                    pr.on("v", gv)

                def unit_dma(u, b=b):
                    g, e = u // NPE, u % NPE
                    conv = (b == 0 and g == 0)
                    par = u % 2

                    def ud(q, b=b, u=u, g=g, e=e, conv=conv, par=par):
                        if u == 0:
                            q.wait(mget(f"b{b}k{QB - 1}mm"))
                        if conv:
                            for piece in range(6):
                                gslot = (u * 6 + piece) % 2
                                prev = (u * 6 + piece) - 2
                                if prev >= 0:
                                    q.wait(mget(f"b{b}U{prev // 6}p{prev % 6}cast"))
                                na = 4 if piece < 4 else 2
                                mput(f"b{b}U{u}p{piece}dma",
                                     q.dma(wstF[:, gslot * 2048:(gslot + 1) * 2048]
                                           .rearrange("p (a i) -> p a i", a=na),
                                           wsrc(e, piece),
                                           pr.ds("fA" if gslot == 0 else "fB")))
                        else:
                            if u >= 2:
                                q.wait(mget(f"b{b}U{u - 2}h1"))
                                q.wait(mget(f"b{b}U{u - 2}h3"))
                            if b == 0 and 0 <= u - 2 < NPE:
                                q.wait(mget(f"T{u - 2}tee1"))
                            m1 = q.dma(wbufs[par], wbf16[e, :, 0:8192],
                                       pr.ds("fA" if par == 0 else "fB"))
                            if u >= 2:
                                q.wait(mget(f"b{b}U{u - 2}y3"))
                            if b == 0 and 0 <= u - 2 < NPE:
                                q.wait(mget(f"T{u - 2}tee2"))
                            m2 = q.dma(w2bufs[par], wbf16[e, :, 8192:12288],
                                       pr.ds("w2A" if par == 0 else "w2B"))
                            for piece in range(4):
                                mput(f"b{b}U{u}p{piece}cast", m1)
                            mput(f"b{b}U{u}p4cast", m2)
                            mput(f"b{b}U{u}p5cast", m2)
                    pr.on("d", ud)

                    if conv:
                        def tee(q, u=u, e=e, par=par):
                            q.wait(mget(f"b0U{u}p0cast"))
                            q.wait(mget(f"b0U{u}p1cast"))
                            q.wait(mget(f"b0U{u}p2cast"))
                            q.wait(mget(f"b0U{u}p3cast"))
                            q.wait(mget(f"T{u - 2}tee1"))
                            mput(f"T{u}tee1",
                                 q.dma(wbf16[e, :, 0:8192], wbufs[par],
                                       pr.ds("teA" if par == 0 else "teB")))
                        pr.on("d", tee)

                        def tee2(q, u=u, e=e, par=par):
                            q.wait(mget(f"b0U{u}p4cast"))
                            q.wait(mget(f"b0U{u}p5cast"))
                            q.wait(mget(f"T{u - 2}tee2"))
                            mput(f"T{u}tee2",
                                 q.dma(wbf16[e, :, 8192:12288], w2bufs[par],
                                       pr.ds("teC" if par == 0 else "teD")))
                        pr.on("d", tee2)

                def cast_body(x, u, piece, b=b):
                    gslot = (u * 6 + piece) % 2
                    par = u % 2
                    x.wait(mget(f"b{b}U{u}p{piece}dma"))
                    if u >= 2:
                        x.wait(mget(f"T{u - 2}tee1" if piece < 4 else f"T{u - 2}tee2"))
                    if u >= 2:
                        if piece < 2:
                            x.wait(mget(f"b{b}U{u - 2}h1"))
                        elif piece < 4:
                            x.wait(mget(f"b{b}U{u - 2}h3"))
                        else:
                            x.wait(mget(f"b{b}U{u - 2}y3"))
                    else:
                        x.wait(mget(f"g{b * NCH + NCH - 1}qkvmm"))
                        if b > 0:
                            x.wait(mget(f"b{b - 1}ffn_mm_done"))
                    if piece < 2:
                        col, dstb = piece * 2048, wbufs[par]
                    elif piece < 4:
                        col, dstb = 4096 + (piece - 2) * 2048, wbufs[par]
                    else:
                        col, dstb = (piece - 4) * 2048, w2bufs[par]
                    if isinstance(x.e, bass.BassScalarEngine):
                        x.copy(dstb[:, col:col + 2048],
                               wstF[:, gslot * 2048:(gslot + 1) * 2048])
                    else:
                        x.tensor_copy(dstb[:, col:col + 2048],
                                      wstF[:, gslot * 2048:(gslot + 1) * 2048])
                    mput(f"b{b}U{u}p{piece}cast", x.mark())

                def unit_casts(u, b=b):
                    def uc_s(s, u=u):
                        for piece in (0, 2, 4):
                            cast_body(s, u, piece)
                    pr.on("s", uc_s)

                    def uc_v(v, u=u):
                        for piece in (1, 3, 5):
                            cast_body(v, u, piece)
                    pr.on("v", uc_v)

                def unit_h1(t, u, b=b):
                    g, e = u // NPE, u % NPE
                    par = u % 2
                    ch = g
                    if u == 0:
                        t.wait(mget(f"b{b}k{QB - 1}lnev"))
                        t.wait(mget(f"b{b}k{QB - 2}lnev"))
                    if e == 0:
                        t.wait(mget(f"b{b}k{4 * g + 3}gevac"))
                    t.wait(mget(f"b{b}U{u}p0cast"))
                    t.wait(mget(f"b{b}U{u}p1cast"))
                    for gp in range(4):
                        t.wait(mget(f"b{b}U{u - 1}amul{gp}"))
                        for dt in range(DS):
                            t.matmul(bank(gp),
                                     wbufs[par][:, dt * 512 + gp * P: dt * 512 + (gp + 1) * P],
                                     xn2T[:, dt * S + ch * CH: dt * S + (ch + 1) * CH],
                                     start=(dt == 0), stop=(dt == DS - 1))
                        mput(f"b{b}U{u}h1g{gp}", t.mark())
                    mput(f"b{b}U{u}h1", t.mark())

                def unit_h3(t, u, b=b):
                    g = u // NPE
                    par = u % 2
                    ch = g
                    t.wait(mget(f"b{b}U{u}p2cast"))
                    t.wait(mget(f"b{b}U{u}p3cast"))
                    for gp in range(4):
                        t.wait(mget(f"b{b}U{u}silu{gp}"))
                        for dt in range(DS):
                            t.matmul(bank(gp),
                                     wbufs[par][:, 4096 + dt * 512 + gp * P:
                                                4096 + dt * 512 + (gp + 1) * P],
                                     xn2T[:, dt * S + ch * CH: dt * S + (ch + 1) * CH],
                                     start=(dt == 0), stop=(dt == DS - 1))
                        mput(f"b{b}U{u}h3g{gp}", t.mark())
                    mput(f"b{b}U{u}h3", t.mark())

                def unit_y(t, u, b=b):
                    g, e = u // NPE, u % NPE
                    par = u % 2
                    t.wait(mget(f"b{b}U{u}amul3"))
                    t.wait(mget(f"b{b}U{u}p4cast"))
                    t.wait(mget(f"b{b}U{u}p5cast"))
                    for q_ in range(4):
                        yp = 4 + (q_ % 2) * 2
                        if q_ >= 2:
                            t.wait(mget(f"b{b}U{u}stt{q_ - 2}"))
                        else:
                            t.wait(mget(f"b{b}U{u - 1}stt{q_ + 2}"))
                        if e == 0 and q_ < 2:
                            t.wait(mget(f"b{b}k{4 * g + 3}gevac"))
                            t.wait(mget(f"b{b}k{4 * g + 3}glog"))
                        for is_ in range(4):
                            for dh in range(2):
                                t.matmul(bank(yp + dh),
                                         apb[:, par * 2048 + is_ * CH + q_ * P:
                                             par * 2048 + is_ * CH + (q_ + 1) * P],
                                         w2bufs[par][:, is_ * 1024 + dh * 512:
                                                     is_ * 1024 + (dh + 1) * 512],
                                         start=(is_ == 0), stop=(is_ == 3))
                        mput(f"b{b}U{u}y{q_}", t.mark())
                    mput(f"b{b}U{u}y3", t.mark())

                def unit_sv(u, b=b):
                    g, e = u // NPE, u % NPE

                    def us(s, u=u):
                        for gp in range(4):
                            s.wait(mget(f"b{b}U{u}h1g{gp}"))
                            s.wait(mget(f"b{b}U{u - 1}amul{gp}"))
                            if u == 0 and b > 0:
                                s.wait(mget(f"b{b - 1}ffn_v_done"))
                            s.activation(silu_sb[:, gp * CH:(gp + 1) * CH],
                                         bank(gp), AF.Silu)
                            mput(f"b{b}U{u}silu{gp}", s.mark())
                    pr.on("s", us)

                    def uv(v, u=u, g=g, e=e):
                        for gp in range(4):
                            v.wait(mget(f"b{b}U{u}silu{gp}"))
                            v.wait(mget(f"b{b}U{u}h3g{gp}"))
                            if u >= 2:
                                v.wait(mget(f"b{b}U{u - 2}y3"))
                            elif b > 0:
                                v.wait(mget(f"b{b - 1}ffn_mm_done"))
                            v.tensor_tensor(
                                out=apb[:, (u % 2) * 2048 + gp * CH:
                                        (u % 2) * 2048 + (gp + 1) * CH],
                                in0=silu_sb[:, gp * CH:(gp + 1) * CH],
                                in1=bank(gp), op=ALU.mult)
                            mput(f"b{b}U{u}amul{gp}", v.mark())
                    pr.on("v", uv)

                def unit_stt(u, b=b):
                    g, e = u // NPE, u % NPE

                    def uw(v, u=u, g=g, e=e):
                        for q_ in range(4):
                            gt = g * 4 + q_
                            yp = 4 + (q_ % 2) * 2
                            v.wait(mget(f"b{b}U{u}y{q_}"))
                            if e > 0:
                                v.wait(mget(f"b{b}U{u - 1}stt{q_}"))
                            elif b > 0:
                                v.wait(mget(f"b{b - 1}t{TT - 1}lny"))
                            ydst = Ysb[:, gt * D:(gt + 1) * D]
                            if e == 0:
                                cs = c_buf[:, gt * NEXP: gt * NEXP + 1]
                                v.tensor_scalar(out=ydst, in0=bank(yp, 2),
                                                scalar1=cs, scalar2=None,
                                                op0=ALU.mult)
                            elif e < NEXP:
                                cs = c_buf[:, gt * NEXP + e: gt * NEXP + e + 1]
                                v.scalar_tensor_tensor(out=ydst, in0=bank(yp, 2),
                                                       scalar=cs, in1=ydst,
                                                       op0=ALU.mult, op1=ALU.add)
                            else:
                                v.tensor_tensor(out=ydst, in0=bank(yp, 2),
                                                in1=ydst, op=ALU.add)
                            mput(f"b{b}U{u}stt{q_}", v.mark())
                    pr.on("v", uw)

                def fin_tile(ti, b=b):
                    g = ti // 4
                    uL = g * NPE + NPE - 1

                    def fin_v1pre(v, b=b, ti=ti):
                        pass
                    emit_ln(f"b{b}t{ti}y", Ysb[:, ti * D:(ti + 1) * D],
                            lnt2[:, (ti % 2) * D:(ti % 2 + 1) * D],
                            slot=ti % 4,
                            sq_ap=outt[:, (ti % 2) * D:(ti % 2 + 1) * D],
                            pre_v=[f"b{b}U{uL}stt{ti % 4}"] +
                                  ([f"b{b}t{ti - 2}outdma"] if ti >= 2 else
                                   ([f"b{b - 1}t{TT - 2 + ti}outdma"] if b > 0 else [])) +
                                  ([f"b{b}t{ti - 4}yln"] if ti >= 4 else []),
                            pre_s=([f"b{b}t{ti - 2}outadd"] if ti >= 2 else []))
                    mput(f"b{b}t{ti}lny", mget(f"b{b}t{ti}yln"))

                    def fin_d(q, b=b, ti=ti):
                        q.wait(mget(f"b{b}t{ti}yv1"))
                        if ti >= 2:
                            q.wait(mget(f"b{b}t{ti - 2}outadd"))
                        mput(f"b{b}t{ti}x1r",
                             q.dma(outt[:, (ti % 2) * D:(ti % 2 + 1) * D],
                                   x1park[ti * P:(ti + 1) * P, :],
                                   pr.ds("x1rA" if ti % 2 == 0 else "x1rB")))
                    pr.on("d", fin_d)

                    def fin_g2(g_, b=b, ti=ti):
                        g_.wait(mget(f"b{b}t{ti}lny"))
                        g_.wait(mget(f"b{b}t{ti}x1r"))
                        sl = (ti % 2) * D
                        g_.tensor_tensor(out=outt[:, sl:sl + D], in0=outt[:, sl:sl + D],
                                         in1=lnt2[:, sl:sl + D], op=ALU.add)
                        mput(f"b{b}t{ti}outadd", g_.mark())
                    pr.on("g", fin_g2)

                    def fin_d2(q, b=b, ti=ti):
                        q.wait(mget(f"b{b}t{ti}outadd"))
                        mput(f"b{b}t{ti}outdma",
                             q.dma(out_t[b, ti * P:(ti + 1) * P, :],
                                   outt[:, (ti % 2) * D:(ti % 2 + 1) * D],
                                   pr.ds("outA" if ti % 2 == 0 else "outB")))
                    pr.on("d", fin_d2)

                for g in range(NCH):
                    if g == 0:
                        back_ln(0)
                        back_ln(1)
                        back_tr(0)
                        back_ln(2)
                        back_tr(1)
                        back_ln(3)
                        back_tr(2)
                        back_tr(3)
                    if g > 0:
                        def yprev(t, b=b, g=g):
                            unit_y(t, g * NPE - 1)
                        pr.on("t", yprev)
                        unit_stt(g * NPE - 1)
                        back_tr(4 * g)
                        back_tr(4 * g + 1)
                        back_ln(4 * g + 2)
                        back_ln(4 * g + 3)
                        back_tr(4 * g + 2)
                        back_tr(4 * g + 3)
                    gate_group(g)
                    for e in range(NPE):
                        u = g * NPE + e
                        unit_dma(u)
                        if b == 0 and g == 0:
                            unit_casts(u)

                        def ut(t, b=b, u=u, e=e):
                            unit_h1(t, u)
                            if e > 0:
                                unit_y(t, u - 1)
                            unit_h3(t, u)
                        pr.on("t", ut)
                        if e > 0:
                            unit_stt(u - 1)
                        unit_sv(u)
                        if e == 12 and g + 1 < NCH:
                            back_ln(4 * (g + 1))
                            back_ln(4 * (g + 1) + 1)
                    if g > 0:
                        for ti in range(4 * (g - 1), 4 * g):
                            fin_tile(ti)

                def tail_t(t, b=b):
                    unit_y(t, NU - 1)
                    mput(f"b{b}ffn_mm_done", t.mark())
                pr.on("t", tail_t)
                unit_stt(NU - 1)
                for ti in range(4 * (NCH - 1), 4 * NCH):
                    fin_tile(ti)

                def fin_bar_t(t, b=b):
                    t.wait(mget(f"b{b}t{TT - 1}outdma"))
                    t.wait(mget(f"b{b}t{TT - 2}outdma"))
                    t.wait(mget(f"b{b}t{TT - 1}x1r"))
                    t.wait(mget(f"b{b}t{TT - 2}x1r"))
                    if b == 0:
                        for uu in (NPE - 2, NPE - 1):
                            t.wait(mget(f"T{uu}tee1"))
                            t.wait(mget(f"T{uu}tee2"))
                    mput(f"b{b}fin_done", t.mark())
                pr.on("t", fin_bar_t)

                def endmarks_s(s, b=b):
                    mput(f"b{b}ffn_cast_s", s.mark())
                pr.on("s", endmarks_s)

                def endmarks_g(g_, b=b):
                    mput(f"b{b}ffn_cast_g", g_.mark())
                pr.on("g", endmarks_g)

                def endmarks_v(v, b=b):
                    mput(f"b{b}ffn_v_done", v.mark())
                pr.on("v", endmarks_v)
                pr.flush()
                ck()


        except _StopBuild:
            pr.flush()
        finals = dict(pr.dcounts)

        def final_d(raw):
            for k, n in finals.items():
                if n > 0:
                    raw.wait_ge(dsems[k], n)
        block.sync(final_d)

        marks_out = dict(pr.M)
    return nc, marks_out


def kernel(**inputs):
    S = 2048
    x = np.ascontiguousarray(inputs["L0_attn_x"])
    per_core = [
        {
            "x": x[2 * c:2 * c + 2],
            "Wq": inputs["L0_attn_Wq"], "Wk": inputs["L0_attn_Wk"],
            "Wv": inputs["L0_attn_Wv"],
            "bq": np.asarray(inputs["L0_attn_bq"]).reshape(1, D),
            "bk": np.asarray(inputs["L0_attn_bk"]).reshape(1, D),
            "bv": np.asarray(inputs["L0_attn_bv"]).reshape(1, D),
            "gate_w": inputs["L0_ffn_gate_w"],
            "w1": inputs["L0_ffn_w1"], "w2": inputs["L0_ffn_w2"],
            "w3": inputs["L0_ffn_w3"],
            "ws1": inputs["L0_ffn_ws1"], "ws2": inputs["L0_ffn_ws2"],
            "ws3": inputs["L0_ffn_ws3"],
        }
        for c in range(NCORES)
    ]
    nc, marks = build(S, None)
    nc, marks = build(S, marks)
    nc, _ = build(S, marks)
    res = run_bass_kernel_spmd(nc, per_core, core_ids=list(range(NCORES)))
    out = np.concatenate([r["out"] for r in res.results], axis=0)
    return np.ascontiguousarray(out.astype(np.float32))


# version 17
# speedup vs baseline: 1.0534x; 1.0082x over previous
import numpy as np
import contextlib
import concourse.bass as bass
import concourse.mybir as mybir
from concourse.bass_utils import run_bass_kernel_spmd

F32 = mybir.dt.float32
BF16 = mybir.dt.bfloat16
AF = mybir.ActivationFunctionType
ALU = mybir.AluOpType
AX = mybir.AxisListType

P = 128
D = 1024
DS = D // P
INTER = 512
NEXP = 16
NPE = 18
EPS = 1e-5
NCORES = 8
A3LVL = 99


class Eng:
    def __init__(self, eng, sem, n):
        self.e = eng
        self.sem = sem
        self.n = n
        self._last = None
        self._marked = True

    def op(self, name, *a, **k):
        self._last = getattr(self.e, name)(*a, **k)
        self._marked = False
        return self._last

    def __getattr__(self, name):
        if name.startswith("_") or name in ("e", "sem", "n"):
            raise AttributeError(name)

        def f(*a, **k):
            return self.op(name, *a, **k)
        return f

    def mark(self):
        if not self._marked:
            self._last.then_inc(self.sem, 1)
            self.n += 1
            self._marked = True
        return (self.sem, self.n)

    def wait(self, mark):
        if mark is None:
            return
        sem, n = mark
        if n > 0:
            self.e.wait_ge(sem, n)

    def dma(self, out, in_, s):
        sem, counts, key = s
        self.e.dma_start(out=out, in_=in_).then_inc(sem, 16)
        counts[key] += 16
        return (sem, counts[key])


class Prog:
    ENGS = ("t", "v", "s", "g", "d")

    def __init__(self, block, sems, dsems, marks_in):
        self.block = block
        self.sems = sems
        self.dsems = dsems
        self.dcounts = {k: 0 for k in dsems}
        self.counts = {e: 0 for e in self.ENGS}
        self.M = dict(marks_in) if marks_in else {}
        self.q = {e: [] for e in self.ENGS}

    def ds(self, key):
        return (self.dsems[key], self.dcounts, key)

    def mget(self, key):
        return self.M.get(key)

    def mput(self, key, mark):
        self.M[key] = mark

    def on(self, eng, fn):
        self.q[eng].append(fn)

    def flush(self):
        openers = {"t": self.block.tensor, "v": self.block.vector,
                   "s": self.block.scalar, "g": self.block.gpsimd,
                   "d": self.block.sync}
        for e in self.ENGS:
            fns = self.q[e]
            if not fns:
                continue
            self.q[e] = []

            def run(raw, e=e, fns=fns):
                w = Eng(raw, self.sems[e], self.counts[e])
                for fn in fns:
                    fn(w)
                self.counts[e] = w.n
            openers[e](run)


def build(S, marks_in, stop=99):
    NB = 2
    TT = S // P
    CH = 512
    NCH = S // CH
    QB = S // P
    KC = max(1, S // 512)
    TC = NCH
    NU = NPE * TC

    nc = bass.Bass("TRN2", target_bir_lowering=False, dynamic_dma_scratch_size=2048)

    x_in = nc.dram_tensor("x", [NB, S, D], F32, kind="ExternalInput").ap()
    Wq = nc.dram_tensor("Wq", [D, D], F32, kind="ExternalInput").ap()
    Wk = nc.dram_tensor("Wk", [D, D], F32, kind="ExternalInput").ap()
    Wv = nc.dram_tensor("Wv", [D, D], F32, kind="ExternalInput").ap()
    bqd = nc.dram_tensor("bq", [1, D], F32, kind="ExternalInput").ap()
    bkd = nc.dram_tensor("bk", [1, D], F32, kind="ExternalInput").ap()
    bvd = nc.dram_tensor("bv", [1, D], F32, kind="ExternalInput").ap()
    gate_w = nc.dram_tensor("gate_w", [NEXP, D], F32, kind="ExternalInput").ap()
    w1 = nc.dram_tensor("w1", [NEXP, D, INTER], F32, kind="ExternalInput").ap()
    w2 = nc.dram_tensor("w2", [NEXP, INTER, D], F32, kind="ExternalInput").ap()
    w3 = nc.dram_tensor("w3", [NEXP, D, INTER], F32, kind="ExternalInput").ap()
    ws1 = nc.dram_tensor("ws1", [D, 2 * INTER], F32, kind="ExternalInput").ap()
    ws2 = nc.dram_tensor("ws2", [2 * INTER, D], F32, kind="ExternalInput").ap()
    ws3 = nc.dram_tensor("ws3", [D, 2 * INTER], F32, kind="ExternalInput").ap()
    out_t = nc.dram_tensor("out", [NB, S, D], F32, kind="ExternalOutput").ap()
    x1park = nc.dram_tensor("x1park", [S, D], F32, kind="Internal").ap()
    wbf16 = nc.dram_tensor("wbf16", [NPE, P, 12288], BF16, kind="Internal").ap()

    with contextlib.ExitStack() as ctx:
        ent = ctx.enter_context
        identb = ent(nc.sbuf_tensor("identb", [P, P], BF16))
        identf = ent(nc.sbuf_tensor("identf", [P, P], F32))
        ones = ent(nc.sbuf_tensor("ones", [1, P], F32))
        onescol = ent(nc.sbuf_tensor("onescol", [P, 1], F32))
        gwT = ent(nc.sbuf_tensor("gwT", [P, DS * NEXP], F32))
        bqc = ent(nc.sbuf_tensor("bqc", [P, DS], F32))
        bkc = ent(nc.sbuf_tensor("bkc", [P, DS], F32))
        st = ent(nc.sbuf_tensor("st", [P, 32], F32))
        den = ent(nc.sbuf_tensor("den", [P, 2 * QB], F32))
        c_buf = ent(nc.sbuf_tensor("c_buf", [P, TT * NEXP], F32))
        glog = ent(nc.sbuf_tensor("glog", [P, QB * NEXP], F32))
        smkb = ent(nc.sbuf_tensor("smkb", [P, QB * NEXP], F32))
        gsc = ent(nc.sbuf_tensor("gsc", [P, 40], F32))

        RA = ent(nc.sbuf_tensor("RA", [P, 16384], F32))
        RQ = ent(nc.sbuf_tensor("RQ", [P, 16384], BF16))
        RK = ent(nc.sbuf_tensor("RK", [P, 8192], F32))
        RV = ent(nc.sbuf_tensor("RV", [P, 8192], F32))
        RW = ent(nc.sbuf_tensor("RW", [P, 24576], BF16))
        xnf2 = ent(nc.sbuf_tensor("xnf2", [P, 2048], F32))

        PS = ent(nc.psum_tensor("PS", [P, 4096], F32))

        sems = {}
        for e in Prog.ENGS:
            sems[e] = ent(nc.semaphore("sem_" + e))
        DSEM_NAMES = ["xA", "xB", "xC", "xD", "wA", "wB", "fA", "fB", "xrA", "xrB",
                      "pkA", "pkB", "x1rA", "x1rB", "outA", "outB",
                      "br0", "br1", "br2", "teA", "teB", "teC", "teD",
                      "w2A", "w2B", "xgA", "xgB"]
        dsems = {k: ent(nc.semaphore("dm_" + k)) for k in DSEM_NAMES}
        block = ent(nc.Block())
        pr = Prog(block, sems, dsems, marks_in)
        mget, mput = pr.mget, pr.mput
        phase_no = [0]

        def phase_on():
            phase_no[0] += 1
            return phase_no[0] <= stop

        RAb = RA[:].bitcast(BF16)
        Wbf = RAb[:, 0:8192]
        wstA = RA[:, 12288:16384]
        brow = RV[0:1, 0:3072]
        gwsv = RV[0:NEXP, 3072:4096]
        xstage = RA[:, 4096:8192]
        xnbf = RAb[:, 16384:20480]
        sqA1 = RA[:, 10240:11264]
        bvrep = RA[:, 11264:12288]
        xnTc = RAb[:, 0:4096]
        Ab = RAb
        Ysb = RA

        QTb = RQ
        xn2T = RQ
        KTb = RK[:].bitcast(BF16)
        sqdum = RK[:, 0:2048]
        x1t = RK[:, 2048:4096]
        silu_sb = RK[:, 0:2048]
        xnft = RK[:, 2048:4096]
        xfTt = RK[:, 4096:6144]
        apb = RK[:].bitcast(BF16)[:, 12288:16384]
        Vb = RV[:].bitcast(BF16)
        wstF = RV[:, 0:4096]
        outt = RV[:, 4096:6144]
        lnt2 = RV[:, 6144:8192]
        WTq = RW[:, 0:8192]
        WTk = RW[:, 8192:16384]
        WTv = RW[:, 16384:24576]
        wbufs = [WTq, WTk]
        w2bufs = [WTv[:, 0:4096], WTv[:, 4096:8192]]

        def bank(i, w=1):
            return PS[:, i * 512:(i + w) * 512]

        def bankb(i, w=1):
            return PS[:, i * 512:(i + w) * 512].bitcast(BF16)

        def emit_ln(key, src_ap, dst_ap, slot, sq_ap, pre_v=None, pre_s=None,
                    out_dt=None):
            c0 = slot * 4
            negm = st[:, c0:c0 + 1]
            veps = st[:, c0 + 1:c0 + 2]
            rstd = st[:, c0 + 2:c0 + 3]
            mrstd = st[:, c0 + 3:c0 + 4]

            def vb(v):
                for m in (pre_v or []):
                    v.wait(mget(m) if isinstance(m, str) else m)
                v.reduce_sum(negm, src_ap, axis=AX.X)
                v.wait(v.mark())
                v.tensor_scalar_mul(negm, negm, -1.0 / D)
                v.wait(v.mark())
                v.scalar_tensor_tensor(out=sq_ap, in0=src_ap, scalar=negm,
                                       in1=src_ap, op0=ALU.add, op1=ALU.mult,
                                       accum_out=veps)
                mput(key + "v1", v.mark())
                v.wait(mget(key + "v1"))
                v.tensor_scalar(out=veps, in0=veps, scalar1=1.0 / D, scalar2=EPS,
                                op0=ALU.mult, op1=ALU.add)
                mput(key + "vs", v.mark())
            pr.on("v", vb)

            def sb(s):
                s.wait(mget(key + "vs"))
                s.activation(rstd, veps, AF.Sqrt)
                mput(key + "sq", s.mark())
            pr.on("s", sb)

            def vb2(v):
                v.wait(mget(key + "sq"))
                v.reciprocal(rstd, rstd)
                v.wait(v.mark())
                v.tensor_tensor(out=mrstd, in0=negm, in1=rstd, op=ALU.mult)
                v.wait(v.mark())
                for m in (pre_s or []):
                    v.wait(mget(m) if isinstance(m, str) else m)
                v.tensor_scalar(out=dst_ap, in0=src_ap, scalar1=rstd,
                                scalar2=mrstd, op0=ALU.mult, op1=ALU.add)
                mput(key + "ln", v.mark())
            pr.on("v", vb2)

        def ph0_d(q):
            mput("d_bq", q.dma(brow[:, 0:D], bqd, pr.ds("br0")))
            mput("d_bk", q.dma(brow[:, D:2 * D], bkd, pr.ds("br1")))
            mput("d_gws", q.dma(gwsv[:], gate_w, pr.ds("br2")))
        pr.on("d", ph0_d)

        def ph0_g(g):
            g.memset(identb[:], 0.0)
            g.memset(identf[:], 0.0)
            g.memset(ones[:], 1.0)
            g.memset(onescol[:], 1.0)
            g.wait(g.mark())
            g.affine_select(out=identb[:], in_=identb[:], compare_op=ALU.not_equal,
                            fill=1.0, base=0, pattern=[[-1, P]], channel_multiplier=1)
            g.affine_select(out=identf[:], in_=identf[:], compare_op=ALU.not_equal,
                            fill=1.0, base=0, pattern=[[-1, P]], channel_multiplier=1)
            mput("g_ident", g.mark())
        pr.on("g", ph0_g)

        def ph0_t(t):
            t.wait(mget("d_bq"))
            t.wait(mget("d_bk"))
            t.wait(mget("d_gws"))
            t.wait(mget("g_ident"))
            for j in range(DS):
                t.matmul(bank(7)[:, j:j + 1], brow[0:1, j * P:(j + 1) * P],
                         ones[0:1, 0:1], start=True, stop=True)
                t.matmul(bank(7)[:, 8 + j:9 + j], brow[0:1, D + j * P:D + (j + 1) * P],
                         ones[0:1, 0:1], start=True, stop=True)
            for j in range(DS):
                t.transpose(bank(7)[:, 16 + j * 16:16 + (j + 1) * 16],
                            gwsv[:, j * P:(j + 1) * P], identf[0:NEXP, 0:NEXP])
            mput("t_ph0", t.mark())
        pr.on("t", ph0_t)

        def ph0_s(s):
            s.wait(mget("t_ph0"))
            s.copy(bqc[:], bank(7)[:, 0:DS])
            s.copy(bkc[:], bank(7)[:, 8:8 + DS])
            s.copy(gwT[:], bank(7)[:, 16:16 + DS * NEXP])
            mput("s_bias", s.mark())
        pr.on("s", ph0_s)
        pr.flush()

        class _StopBuild(Exception):
            pass

        fcount = [0]

        def ck():
            fcount[0] += 1
            if fcount[0] >= stop:
                raise _StopBuild()

        try:
            for b in range(NB):
                def a0_brow(q, b=b):
                    if b == 0:
                        q.wait(mget("t_ph0"))
                    else:
                        q.wait(mget(f"b{b - 1}bvmm"))
                    mput(f"b{b}brow", q.dma(brow[:, 2 * D:3 * D], bvd, pr.ds("br2")))
                pr.on("d", a0_brow)

                for wi, (Wsrc, WTdst) in enumerate([(Wq, WTq), (Wk, WTk), (Wv, WTv)]):
                    key = f"b{b}w{wi}"

                    def a0_d(q, Wsrc=Wsrc, key=key, wi=wi, b=b):
                        if wi > 0:
                            q.wait(mget(f"b{b}w{wi - 1}cast"))
                        elif b > 0:
                            q.wait(mget(f"b{b - 1}ffn_cast_s"))
                            q.wait(mget(f"b{b - 1}t{TT - 1}lny"))
                        mput(key + "h0", q.dma(
                            wstA[:].rearrange("p (a d) -> p a d", a=4),
                            Wsrc[0:512, :].rearrange("(a p) d -> p a d", p=P),
                            pr.ds("wA")))
                    pr.on("d", a0_d)

                    def a0_v(v, key=key, wi=wi, b=b):
                        if wi > 0:
                            v.wait(mget(f"b{b}w{wi - 1}tr"))
                        elif b > 0:
                            v.wait(mget(f"b{b - 1}fin_done"))
                        v.wait(mget(key + "h0"))
                        v.tensor_copy(Wbf[:, 0:4096], wstA[:])
                        mput(key + "c0", v.mark())
                    pr.on("v", a0_v)

                    def a0_d1(q, Wsrc=Wsrc, key=key, b=b):
                        q.wait(mget(key + "c0"))
                        mput(key + "h1", q.dma(
                            wstA[:].rearrange("p (a d) -> p a d", a=4),
                            Wsrc[512:1024, :].rearrange("(a p) d -> p a d", p=P),
                            pr.ds("wB")))
                    pr.on("d", a0_d1)

                    def a0_v1(v, key=key, b=b):
                        v.wait(mget(key + "h1"))
                        v.tensor_copy(Wbf[:, 4096:8192], wstA[:])
                        mput(key + "cast", v.mark())
                    pr.on("v", a0_v1)

                    def a0_t(t, key=key, wi=wi, b=b):
                        t.wait(mget(key + "cast"))
                        if b == 0 and wi == 0:
                            t.wait(mget("s_bias"))
                        if b > 0 and wi == 0:
                            t.wait(mget(f"b{b - 1}fin_done"))
                        for dt in range(DS):
                            pb = 6 + dt % 2
                            if dt >= 2:
                                t.wait(mget(key + f"tre{dt - 2}"))
                            elif wi > 0:
                                t.wait(mget(f"b{b}w{wi - 1}tre{dt + 6}"))
                            for a in range(DS):
                                t.transpose(bankb(pb)[:, a * P:(a + 1) * P],
                                            Wbf[:, a * 1024 + dt * P: a * 1024 + dt * P + P],
                                            identb[:])
                            mput(key + f"tr{dt}", t.mark())
                        mput(key + "tr", t.mark())
                    pr.on("t", a0_t)

                    def a0_s(s, key=key, WTdst=WTdst):
                        for dt in range(DS):
                            pb = 6 + dt % 2
                            s.wait(mget(key + f"tr{dt}"))
                            s.copy(WTdst[:, dt * 1024:(dt + 1) * 1024], bankb(pb)[:, 0:1024])
                            mput(key + f"tre{dt}", s.mark())
                    pr.on("s", a0_s)

                def a0_bv_t(t, b=b):
                    t.wait(mget(f"b{b}brow"))
                    t.wait(mget(f"b{b}w2tre7"))
                    if b > 0:
                        t.wait(mget(f"b{b - 1}k{QB - 1}gevf"))
                        t.wait(mget(f"b{b - 1}k{QB - 1}gevac"))
                    t.matmul(bank(4), ones[0:1, 0:P], brow[0:1, 2 * D:2 * D + 512],
                             start=True, stop=True)
                    t.matmul(bank(5), ones[0:1, 0:P], brow[0:1, 2 * D + 512:3 * D],
                             start=True, stop=True)
                    mput(f"b{b}bvmm", t.mark())
                pr.on("t", a0_bv_t)

                def a0_bv_s(s, b=b):
                    s.wait(mget(f"b{b}bvmm"))
                    s.wait(mget(f"b{b}w2cast"))
                    s.copy(bvrep[:, 0:512], bank(4))
                    s.copy(bvrep[:, 512:1024], bank(5))
                    mput(f"b{b}bvrep", s.mark())
                pr.on("s", a0_bv_s)

                for cc in range(NCH):
                    gcc = b * NCH + cc

                    def a1_d(q, b=b, cc=cc, gcc=gcc):
                        if cc == 0 and b > 0:
                            q.wait(mget(f"b{b - 1}t{TT - 1}lny"))
                        for ti in range(4):
                            gt = cc * 4 + ti
                            gti = b * TT + gt
                            q.wait(mget(f"T{gti - 4}ln"))
                            mput(f"T{gti}x",
                                 q.dma(xstage[:, (gti % 4) * D:(gti % 4 + 1) * D],
                                       x_in[b, gt * P:(gt + 1) * P, :],
                                       pr.ds(["xA", "xB", "xC", "xD"][gti % 4])))
                    pr.on("d", a1_d)

                    for ti in range(4):
                        gt = cc * 4 + ti
                        gti = b * TT + gt
                        xt = xstage[:, (gti % 4) * D:(gti % 4 + 1) * D]
                        emit_ln(f"T{gti}", xt, xnbf[:, (gti % 4) * D:(gti % 4 + 1) * D],
                                slot=gti % 4,
                                sq_ap=sqA1[:, 0:D],
                                pre_v=[f"T{gti}x"],
                                pre_s=[f"T{gti - 4}tp"])

                        def a1_tr(t, b=b, cc=cc, gcc=gcc, ti=ti, gti=gti):
                            t.wait(mget(f"T{gti}ln"))
                            pb = 6 + gti % 2
                            t.wait(mget(f"T{gti - 2}tre"))
                            if cc == 0 and ti < 2:
                                t.wait(mget(f"b{b}w2tre{6 + ti}"))
                            for js in range(DS):
                                t.transpose(bankb(pb)[:, js * P:(js + 1) * P],
                                            xnbf[:, (gti % 4) * D + js * P:
                                                 (gti % 4) * D + (js + 1) * P],
                                            identb[:])
                            mput(f"T{gti}tp", t.mark())
                        pr.on("t", a1_tr)

                        def a1_ev(v, gcc=gcc, ti=ti, gti=gti):
                            v.wait(mget(f"T{gti}tp"))
                            if ti == 0:
                                v.wait(mget(f"g{gcc - 1}qkvmm"))
                            pb = 6 + gti % 2
                            v.tensor_copy(
                                xnTc.rearrange("p (a t) -> p a t", a=DS)[:, :, ti * P:(ti + 1) * P],
                                bankb(pb)[:, 0:1024].rearrange("p (a t) -> p a t", a=DS))
                            mput(f"T{gti}tre", v.mark())
                        pr.on("v", a1_ev)

                    def a1_t(t, b=b, cc=cc, gcc=gcc):
                        for ti in range(4):
                            t.wait(mget(f"T{b * TT + cc * 4 + ti}tre"))
                        if cc == 0:
                            t.wait(mget(f"b{b}w2tre7"))
                        def vtile(ti):
                            pp = 2 + (ti % 2) * 2
                            if ti >= 2:
                                t.wait(mget(f"g{gcc}vev{ti - 2}"))
                            else:
                                t.wait(mget(f"g{gcc - 1}vev{ti + 2}"))
                                if cc == 0 and ti == 1:
                                    t.wait(mget(f"b{b}bvrep"))
                            for eh in range(2):
                                for dt in range(DS):
                                    t.matmul(bank(pp + eh),
                                             xnTc[:, dt * CH + ti * P: dt * CH + (ti + 1) * P],
                                             WTv[:, dt * 1024 + eh * 512: dt * 1024 + (eh + 1) * 512],
                                             start=(dt == 0), stop=(dt == DS - 1))
                            mput(f"g{gcc}vmm{ti}", t.mark())
                        for qki in range(2 * DS):
                            which, es = qki // DS, qki % DS
                            WT = WTq if which == 0 else WTk
                            pb = qki % 4
                            if qki >= 4:
                                t.wait(mget(f"g{gcc}qke{qki - 4}"))
                            else:
                                t.wait(mget(f"g{gcc - 1}qke{qki + 12}"))
                            for dt in range(DS):
                                t.matmul(bank(pb),
                                         WT[:, dt * 1024 + es * P: dt * 1024 + (es + 1) * P],
                                         xnTc[:, dt * CH:(dt + 1) * CH],
                                         start=(dt == 0), stop=(dt == DS - 1))
                            mput(f"g{gcc}qkmm{qki}", t.mark())
                        for ti in range(4):
                            if ti == 0:
                                t.wait(mget(f"g{gcc}qke{14}"))
                                t.wait(mget(f"g{gcc}qke{15}"))
                            vtile(ti)
                        mput(f"g{gcc}qkvmm", t.mark())
                    pr.on("t", a1_t)

                    def a1_se(s, b=b, cc=cc, gcc=gcc):
                        for qki in range(2 * DS):
                            which, es = qki // DS, qki % DS
                            dstb = QTb if which == 0 else KTb
                            bc = bqc if which == 0 else bkc
                            pb = qki % 4
                            s.wait(mget(f"g{gcc}qkmm{qki}"))
                            s.activation(dstb[:, es * S + cc * CH: es * S + (cc + 1) * CH],
                                         bank(pb), AF.Identity,
                                         bias=bc[:, es:es + 1], scale=1.0)
                            mput(f"g{gcc}qke{qki}", s.mark())
                    pr.on("s", a1_se)

                    def a1_ve(v, b=b, cc=cc, gcc=gcc):
                        for ti in range(4):
                            gt = cc * 4 + ti
                            pp = 2 + (ti % 2) * 2
                            v.wait(mget(f"g{gcc}vmm{ti}"))
                            if cc == 0 and ti == 0:
                                v.wait(mget(f"b{b}bvrep"))
                            v.tensor_tensor(out=Vb[:, gt * D:(gt + 1) * D],
                                            in0=bank(pp, 2), in1=bvrep[:], op=ALU.add)
                            mput(f"g{gcc}vev{ti}", v.mark())
                    pr.on("v", a1_ve)
                pr.flush()
                ck()

                for qb in range(QB):
                    st4 = (qb % 2) * 4

                    def a2_t(t, b=b, qb=qb, st4=st4):
                        if qb == 0:
                            t.wait(mget(f"g{b * NCH + NCH - 1}qke{2 * DS - 1}"))
                            t.wait(mget(f"g{b * NCH + NCH - 1}vev3"))
                        t.wait(mget(f"b{b}q{qb - 2}exp"))
                        for kc in range(KC):
                            for dt in range(DS):
                                t.matmul(bank(st4 + kc),
                                         QTb[:, dt * S + qb * P: dt * S + (qb + 1) * P],
                                         KTb[:, dt * S + kc * 512: dt * S + (kc + 1) * 512],
                                         start=(dt == 0), stop=(dt == DS - 1))
                        mput(f"b{b}q{qb}mm", t.mark())
                    pr.on("t", a2_t)

                    def a2_v(v, b=b, qb=qb, st4=st4):
                        v.wait(mget(f"b{b}q{qb}mm"))
                        v.reduce_max(den[:, qb:qb + 1], bank(st4, KC), axis=AX.X,
                                     negate=True)
                        mput(f"b{b}q{qb}max", v.mark())
                    pr.on("v", a2_v)

                    def a2_s(s, b=b, qb=qb, st4=st4):
                        s.wait(mget(f"b{b}q{qb}max"))
                        if qb == 0 and b > 0:
                            s.wait(mget(f"b{b - 1}fin_done"))
                        s.mul(den[:, QB + qb:QB + qb + 1], den[:, qb:qb + 1], 1.0 / 32.0)
                        s.wait(s.mark())
                        s.activation(Ab[:, qb * S:(qb + 1) * S], bank(st4, KC), AF.Exp,
                                     bias=den[:, QB + qb:QB + qb + 1], scale=1.0 / 32.0,
                                     accum_out=den[:, qb:qb + 1])
                        mput(f"b{b}q{qb}exp", s.mark())
                    pr.on("s", a2_s)

                    def a2_v2(v, b=b, qb=qb):
                        v.wait(mget(f"b{b}q{qb}exp"))
                        v.reciprocal(den[:, QB + qb:QB + qb + 1], den[:, qb:qb + 1])
                        mput(f"b{b}q{qb}rec", v.mark())
                    pr.on("v", a2_v2)

                    def a2_s2(s, b=b, qb=qb):
                        s.wait(mget(f"b{b}q{qb}rec"))
                        s.activation(Vb[:, qb * D:(qb + 1) * D], Vb[:, qb * D:(qb + 1) * D],
                                     AF.Identity, bias=0.0,
                                     scale=den[:, QB + qb:QB + qb + 1])
                        mput(f"b{b}q{qb}vs", s.mark())
                    pr.on("s", a2_s2)
                pr.flush()
                ck()

                for kb in range(QB):
                    pp = (kb % 2) * 2

                    def a3_d(q, b=b, kb=kb):
                        q.wait(mget(f"b{b}k{kb - 2}park"))
                        if kb < 2:
                            q.wait(mget(f"b{b}q{QB - 1}mm"))
                            if b > 0:
                                q.wait(mget(f"b{b - 1}k{QB - 1}gtr"))
                                q.wait(mget(f"b{b - 1}ffn_v_done"))
                        mput(f"b{b}k{kb}xr",
                             q.dma(x1t[:, (kb % 2) * D:(kb % 2 + 1) * D],
                                   x_in[b, kb * P:(kb + 1) * P, :],
                                   pr.ds("xrA" if kb % 2 == 0 else "xrB")))
                    pr.on("d", a3_d)

                    def a3_t(t, b=b, kb=kb, pp=pp):
                        if kb == 0:
                            t.wait(mget(f"b{b}q{QB - 1}vs"))
                            t.wait(mget(f"b{b}q{QB - 1}exp"))
                            t.wait(mget(f"b{b}q{QB - 1}rec"))
                            if b == 0:
                                t.wait(mget("s_bias"))
                        t.wait(mget(f"b{b}k{kb - 2}lnev"))
                        for qc in range(QB):
                            for dh in range(2):
                                t.matmul(bank(pp + dh),
                                         Ab[:, qc * S + kb * P: qc * S + (kb + 1) * P],
                                         Vb[:, qc * D + dh * 512: qc * D + (dh + 1) * 512],
                                         start=(qc == 0), stop=(qc == QB - 1))
                        mput(f"b{b}k{kb}mm", t.mark())
                    pr.on("t", a3_t)

                    def a3_sq(s, b=b, kb=kb, pp=pp):
                        c0 = (4 + kb % 2) * 4
                        s.wait(mget(f"b{b}k{kb}mm"))
                        if kb < 2 and b > 0:
                            s.wait(mget(f"b{b - 1}ffn_v_done"))
                        s.activation(sqdum[:, (kb % 2) * D:(kb % 2 + 1) * D],
                                     bank(pp, 2), AF.Square,
                                     accum_out=st[:, c0 + 1:c0 + 2])
                        mput(f"b{b}k{kb}asq", s.mark())
                    pr.on("s", a3_sq)

                    def a3_av(v, b=b, kb=kb, pp=pp):
                        c0 = (4 + kb % 2) * 4
                        negm = st[:, c0:c0 + 1]
                        veps = st[:, c0 + 1:c0 + 2]
                        rstd = st[:, c0 + 2:c0 + 3]
                        v.wait(mget(f"b{b}k{kb}asq"))
                        v.reduce_sum(negm, bank(pp, 2), axis=AX.X)
                        v.wait(v.mark())
                        v.tensor_scalar_mul(negm, negm, -1.0 / D)
                        v.wait(v.mark())
                        v.tensor_tensor(out=rstd, in0=negm, in1=negm, op=ALU.mult)
                        v.wait(v.mark())
                        v.tensor_scalar(out=veps, in0=veps, scalar1=1.0 / D,
                                        scalar2=EPS, op0=ALU.mult, op1=ALU.add)
                        v.wait(v.mark())
                        v.tensor_tensor(out=veps, in0=veps, in1=rstd, op=ALU.subtract)
                        mput(f"b{b}k{kb}avs", v.mark())
                    pr.on("v", a3_av)

                    def a3_as(s, b=b, kb=kb):
                        c0 = (4 + kb % 2) * 4
                        s.wait(mget(f"b{b}k{kb}avs"))
                        s.activation(st[:, c0 + 2:c0 + 3], st[:, c0 + 1:c0 + 2],
                                     AF.Sqrt)
                        mput(f"b{b}k{kb}asr", s.mark())
                    pr.on("s", a3_as)

                    def a3_av2(v, b=b, kb=kb, pp=pp):
                        c0 = (4 + kb % 2) * 4
                        sl = (kb % 2) * D
                        v.wait(mget(f"b{b}k{kb}asr"))
                        v.reciprocal(st[:, c0 + 2:c0 + 3], st[:, c0 + 2:c0 + 3])
                        v.wait(v.mark())
                        v.tensor_tensor(out=st[:, c0 + 3:c0 + 4], in0=st[:, c0:c0 + 1],
                                        in1=st[:, c0 + 2:c0 + 3], op=ALU.mult)
                        v.wait(v.mark())
                        v.wait(mget(f"b{b}k{kb}xr"))
                        v.scalar_tensor_tensor(out=x1t[:, sl:sl + D], in0=bank(pp, 2),
                                               scalar=st[:, c0 + 2:c0 + 3],
                                               in1=x1t[:, sl:sl + D],
                                               op0=ALU.mult, op1=ALU.add)
                        v.wait(v.mark())
                        v.tensor_scalar(out=x1t[:, sl:sl + D], in0=x1t[:, sl:sl + D],
                                        scalar1=st[:, c0 + 3:c0 + 4], scalar2=None,
                                        op0=ALU.add)
                        mput(f"b{b}k{kb}lnev", v.mark())
                    pr.on("v", a3_av2)

                    def a3_d2(q, b=b, kb=kb):
                        q.wait(mget(f"b{b}k{kb}lnev"))
                        mput(f"b{b}k{kb}park",
                             q.dma(x1park[kb * P:(kb + 1) * P, :],
                                   x1t[:, (kb % 2) * D:(kb % 2 + 1) * D],
                                   pr.ds("pkA" if kb % 2 == 0 else "pkB")))
                    pr.on("d", a3_d2)

                def wsrc(e, piece):
                    if e < NEXP:
                        if piece < 2:
                            srcw = w1[e]
                        elif piece < 4:
                            srcw = w3[e]
                        else:
                            srcw = w2[e]
                        h = piece % 2
                        if piece < 4:
                            return srcw.rearrange("(a p) i -> p a i", p=P)[:, h * 4:(h + 1) * 4, :]
                        return srcw.rearrange("(a p) d -> p a d", p=P)[:, h * 2:(h + 1) * 2, :]
                    hh = e - NEXP
                    if piece < 2:
                        return ws1.rearrange("(a p) i -> p a i", p=P)[
                            :, piece * 4:(piece + 1) * 4, hh * 512:(hh + 1) * 512]
                    if piece < 4:
                        return ws3.rearrange("(a p) i -> p a i", p=P)[
                            :, (piece - 2) * 4:(piece - 1) * 4, hh * 512:(hh + 1) * 512]
                    return ws2.rearrange("(a p) d -> p a d", p=P)[
                        :, hh * 4 + (piece - 4) * 2: hh * 4 + (piece - 4) * 2 + 2, :]

                def back_ln(kb, b=b):
                    alt = kb % 4 >= 2
                    buf = xnf2 if alt else xnft

                    def bk_d(q, b=b, kb=kb, alt=alt, buf=buf):
                        if alt:
                            if kb >= 4:
                                q.wait(mget(f"b{b}k{kb - 4}gtr"))
                            else:
                                q.wait(mget(f"b{b}k{QB - 2 + kb % 2}park"))
                                if b > 0:
                                    q.wait(mget(f"b{b - 1}k{QB - 2 + kb % 2}gtr"))
                        else:
                            q.wait(mget(f"b{b}k{kb - 2}gtr"))
                            if kb < 2:
                                q.wait(mget(f"b{b}k{QB - 2 + kb % 2}park"))
                        mput(f"b{b}k{kb}x1l",
                             q.dma(buf[:, (kb % 2) * D:(kb % 2 + 1) * D],
                                   x1park[kb * P:(kb + 1) * P, :],
                                   pr.ds(("xgA" if kb % 2 == 0 else "xgB") if alt
                                         else ("xrA" if kb % 2 == 0 else "xrB"))))
                    pr.on("d", bk_d)

                    if alt and kb >= 4:
                        sqp = 1 - kb % 2
                        prev = [f"b{b}k{kb}x1l",
                                f"b{b}k{kb - (3 if kb % 4 == 2 else 5)}gmm"]
                    else:
                        sqp = kb % 2
                        prev = [f"b{b}k{kb}x1l", f"b{b}k{kb - 2}gmm",
                                f"b{b}k{kb - 2}gevf"] +                                ([f"b{b - 1}k{QB - 1}gmm"] if kb < 2 and b > 0 else [])
                    emit_ln(f"b{b}k{kb}x1",
                            buf[:, (kb % 2) * D:(kb % 2 + 1) * D],
                            buf[:, (kb % 2) * D:(kb % 2 + 1) * D],
                            slot=6 + kb % 2,
                            sq_ap=xfTt[:, sqp * D:(sqp + 1) * D],
                            pre_v=prev,
                            pre_s=[])
                    mput(f"b{b}k{kb}xn2ln", mget(f"b{b}k{kb}x1ln"))

                def back_tr(kb, b=b):
                    def bk_t(t, b=b, kb=kb):
                        t.wait(mget(f"b{b}k{kb}xn2ln"))
                        t.wait(mget(f"b{b}k{kb - 1}gevac"))
                        if kb % 4 == 0 and kb >= 4:
                            uL = (kb // 4 - 1) * NPE + NPE - 1
                            t.wait(mget(f"b{b}U{uL}stt2"))
                            t.wait(mget(f"b{b}U{uL}stt3"))
                        tbuf = xnf2 if kb % 4 >= 2 else xnft
                        for js in range(DS):
                            t.transpose(bank(4 + js // 4)[:, (js % 4) * P:(js % 4 + 1) * P],
                                        tbuf[:, (kb % 2) * D + js * P: (kb % 2) * D + (js + 1) * P],
                                        identf[:])
                        mput(f"b{b}k{kb}gtr", t.mark())
                    pr.on("t", bk_t)

                    def bk_s(s, b=b, kb=kb):
                        s.wait(mget(f"b{b}k{kb}gtr"))
                        s.wait(mget(f"b{b}k{kb - 2}gmm"))
                        if kb < 2 and b > 0:
                            s.wait(mget(f"b{b - 1}ffn_mm_done"))
                        if kb >= 4:
                            s.wait(mget(f"b{b}k{(kb // 4) * 4 + (3 if kb % 2 == 0 else 2)}x1v1"))
                        sl = (kb % 2) * D
                        s.copy(xfTt[:, sl:sl + 512], bank(4))
                        s.copy(xfTt[:, sl + 512:sl + D], bank(5))
                        mput(f"b{b}k{kb}gevf", s.mark())
                        dst = xn2T[:, 0:DS * S].rearrange("p (a t) -> p a t", a=DS)
                        s.copy(dst[:, 0:4, kb * P:(kb + 1) * P],
                               bank(4).rearrange("p (a t) -> p a t", a=4))
                        s.copy(dst[:, 4:8, kb * P:(kb + 1) * P],
                               bank(5).rearrange("p (a t) -> p a t", a=4))
                        mput(f"b{b}k{kb}gevac", s.mark())
                    pr.on("s", bk_s)

                    def bk_t2(t, b=b, kb=kb):
                        t.wait(mget(f"b{b}k{kb}gevf"))
                        t.wait(mget(f"b{b}k{kb - 2}glog"))
                        gb = 6 + kb % 2
                        sl = (kb % 2) * D
                        for dt in range(DS):
                            t.matmul(bank(gb)[:, 0:NEXP],
                                     xfTt[:, sl + dt * P: sl + (dt + 1) * P],
                                     gwT[:, dt * NEXP:(dt + 1) * NEXP],
                                     start=(dt == 0), stop=(dt == DS - 1))
                        mput(f"b{b}k{kb}gmm", t.mark())
                    pr.on("t", bk_t2)

                    def bk_gl(s, b=b, kb=kb):
                        gb = 6 + kb % 2
                        s.wait(mget(f"b{b}k{kb}gmm"))
                        if kb < 2 and b > 0:
                            s.wait(mget(f"b{b - 1}gate{NCH - 1}"))
                        s.copy(glog[:, kb * NEXP:(kb + 1) * NEXP],
                               bank(gb)[:, 0:NEXP])
                        mput(f"b{b}k{kb}glog", s.mark())
                    pr.on("s", bk_gl)

                def gate_group(g, b=b):
                    KB = 4
                    E = NEXP

                    def gexp(s, b=b, g=g):
                        s.wait(mget(f"b{b}k{4 * g + 3}glog"))
                        s.activation(glog[:, g * KB * E:(g + 1) * KB * E],
                                     glog[:, g * KB * E:(g + 1) * KB * E], AF.Exp)
                        mput(f"b{b}gexp{g}", s.mark())
                    pr.on("s", gexp)

                    def gv(v, b=b, g=g):
                        gl = glog[:, g * KB * E:(g + 1) * KB * E]
                        sm = smkb[:, g * KB * E:(g + 1) * KB * E]
                        cb = c_buf[:, g * KB * E:(g + 1) * KB * E]
                        gmax = gsc[:, 0:4 * KB]
                        den16 = gsc[:, 4 * KB:5 * KB]
                        gtmp = gsc[:, 5 * KB:9 * KB]
                        gcnt = sm[:, 0:4 * KB]
                        v.wait(mget(f"b{b}gexp{g}"))
                        if g == 0 and b > 0:
                            v.wait(mget(f"b{b - 1}ffn_v_done"))
                        g3 = gl.rearrange("p (k e) -> p k e", k=KB)
                        v.tensor_reduce(out=den16, in_=g3, axis=AX.X, op=ALU.add)
                        v.wait(v.mark())
                        v.reciprocal(den16, den16)
                        v.wait(v.mark())
                        v.tensor_tensor(out=g3, in0=g3,
                                        in1=den16.rearrange("p (k o) -> p k o", o=1
                                                            ).to_broadcast([P, KB, E]),
                                        op=ALU.mult)
                        v.wait(v.mark())
                        v.tensor_reduce(out=gmax,
                                        in_=gl.rearrange("p (kg c) -> p kg c", c=4),
                                        axis=AX.X, op=ALU.max)
                        v.wait(v.mark())
                        gm3 = gmax.rearrange("p (k g) -> p k g", g=4)
                        gc3 = gcnt.rearrange("p (k g) -> p k g", g=4)
                        gt3 = gtmp[:, 0:4 * KB].rearrange("p (k g) -> p k g", g=4)
                        for sh in range(1, 4):
                            for piece in range(2):
                                if piece == 0:
                                    w = 4 - sh
                                    a_ap, b_ap = gm3[:, :, sh:4], gm3[:, :, 0:w]
                                    o_ap, t_ap = gc3[:, :, 0:w], gt3[:, :, 0:w]
                                else:
                                    w = sh
                                    a_ap, b_ap = gm3[:, :, 0:sh], gm3[:, :, 4 - sh:4]
                                    o_ap, t_ap = gc3[:, :, 4 - sh:4], gt3[:, :, 4 - sh:4]
                                v.tensor_tensor(out=t_ap, in0=a_ap, in1=b_ap, op=ALU.is_gt)
                                v.wait(v.mark())
                                if sh == 1:
                                    v.tensor_copy(o_ap, t_ap)
                                else:
                                    v.tensor_tensor(out=o_ap, in0=o_ap, in1=t_ap, op=ALU.add)
                                v.wait(v.mark())
                        v.tensor_scalar(out=gcnt, in0=gcnt, scalar1=1.5, scalar2=None,
                                        op0=ALU.is_le)
                        v.wait(v.mark())
                        v.tensor_copy(gtmp[:, 0:4 * KB], gcnt)
                        v.wait(v.mark())
                        v.tensor_tensor(out=sm.rearrange("p (kg c) -> p kg c", c=4),
                                        in0=gl.rearrange("p (kg c) -> p kg c", c=4),
                                        in1=gtmp[:, 0:4 * KB].rearrange("p (kg o) -> p kg o", o=1
                                                                        ).to_broadcast([P, 4 * KB, 4]),
                                        op=ALU.mult)
                        v.wait(v.mark())
                        s3 = sm.rearrange("p (k e) -> p k e", k=KB)
                        c3 = cb.rearrange("p (k e) -> p k e", k=KB)
                        t3 = gl.rearrange("p (k e) -> p k e", k=KB)
                        for sh in range(1, E):
                            for piece in range(2):
                                if piece == 0:
                                    w = E - sh
                                    a_ap, b_ap = s3[:, :, sh:E], s3[:, :, 0:w]
                                    o_ap, t_ap = c3[:, :, 0:w], t3[:, :, 0:w]
                                else:
                                    w = sh
                                    a_ap, b_ap = s3[:, :, 0:sh], s3[:, :, E - sh:E]
                                    o_ap, t_ap = c3[:, :, E - sh:E], t3[:, :, E - sh:E]
                                v.tensor_tensor(out=t_ap, in0=a_ap, in1=b_ap, op=ALU.is_gt)
                                v.wait(v.mark())
                                if sh == 1:
                                    v.tensor_copy(o_ap, t_ap)
                                else:
                                    v.tensor_tensor(out=o_ap, in0=o_ap, in1=t_ap, op=ALU.add)
                                v.wait(v.mark())
                        v.tensor_scalar(out=cb, in0=cb, scalar1=3.5, scalar2=None,
                                        op0=ALU.is_le)
                        v.wait(v.mark())
                        v.tensor_tensor(out=cb, in0=cb, in1=sm, op=ALU.mult)
                        mput(f"b{b}gate{g}", v.mark())
                    pr.on("v", gv)

                def unit_dma(u, b=b):
                    g, e = u // NPE, u % NPE
                    conv = (b == 0 and g == 0)
                    par = u % 2

                    def ud(q, b=b, u=u, g=g, e=e, conv=conv, par=par):
                        if u == 0:
                            q.wait(mget(f"b{b}k{QB - 1}mm"))
                        if conv:
                            for piece in range(6):
                                gslot = (u * 6 + piece) % 2
                                prev = (u * 6 + piece) - 2
                                if prev >= 0:
                                    q.wait(mget(f"b{b}U{prev // 6}p{prev % 6}cast"))
                                na = 4 if piece < 4 else 2
                                mput(f"b{b}U{u}p{piece}dma",
                                     q.dma(wstF[:, gslot * 2048:(gslot + 1) * 2048]
                                           .rearrange("p (a i) -> p a i", a=na),
                                           wsrc(e, piece),
                                           pr.ds("fA" if gslot == 0 else "fB")))
                        else:
                            if u >= 2:
                                q.wait(mget(f"b{b}U{u - 2}h1"))
                                q.wait(mget(f"b{b}U{u - 2}h3"))
                            if b == 0 and 0 <= u - 2 < NPE:
                                q.wait(mget(f"T{u - 2}tee1"))
                            m1 = q.dma(wbufs[par], wbf16[e, :, 0:8192],
                                       pr.ds("fA" if par == 0 else "fB"))
                            if u >= 2:
                                q.wait(mget(f"b{b}U{u - 2}y3"))
                            if b == 0 and 0 <= u - 2 < NPE:
                                q.wait(mget(f"T{u - 2}tee2"))
                            m2 = q.dma(w2bufs[par], wbf16[e, :, 8192:12288],
                                       pr.ds("w2A" if par == 0 else "w2B"))
                            for piece in range(4):
                                mput(f"b{b}U{u}p{piece}cast", m1)
                            mput(f"b{b}U{u}p4cast", m2)
                            mput(f"b{b}U{u}p5cast", m2)
                    pr.on("d", ud)

                    if conv:
                        def tee(q, u=u, e=e, par=par):
                            q.wait(mget(f"b0U{u}p0cast"))
                            q.wait(mget(f"b0U{u}p1cast"))
                            q.wait(mget(f"b0U{u}p2cast"))
                            q.wait(mget(f"b0U{u}p3cast"))
                            q.wait(mget(f"T{u - 2}tee1"))
                            mput(f"T{u}tee1",
                                 q.dma(wbf16[e, :, 0:8192], wbufs[par],
                                       pr.ds("teA" if par == 0 else "teB")))
                        pr.on("d", tee)

                        def tee2(q, u=u, e=e, par=par):
                            q.wait(mget(f"b0U{u}p4cast"))
                            q.wait(mget(f"b0U{u}p5cast"))
                            q.wait(mget(f"T{u - 2}tee2"))
                            mput(f"T{u}tee2",
                                 q.dma(wbf16[e, :, 8192:12288], w2bufs[par],
                                       pr.ds("teC" if par == 0 else "teD")))
                        pr.on("d", tee2)

                def cast_body(x, u, piece, b=b):
                    gslot = (u * 6 + piece) % 2
                    par = u % 2
                    x.wait(mget(f"b{b}U{u}p{piece}dma"))
                    if u >= 2:
                        x.wait(mget(f"T{u - 2}tee1" if piece < 4 else f"T{u - 2}tee2"))
                    if u >= 2:
                        if piece < 2:
                            x.wait(mget(f"b{b}U{u - 2}h1"))
                        elif piece < 4:
                            x.wait(mget(f"b{b}U{u - 2}h3"))
                        else:
                            x.wait(mget(f"b{b}U{u - 2}y3"))
                    else:
                        x.wait(mget(f"g{b * NCH + NCH - 1}qkvmm"))
                        if b > 0:
                            x.wait(mget(f"b{b - 1}ffn_mm_done"))
                    if piece < 2:
                        col, dstb = piece * 2048, wbufs[par]
                    elif piece < 4:
                        col, dstb = 4096 + (piece - 2) * 2048, wbufs[par]
                    else:
                        col, dstb = (piece - 4) * 2048, w2bufs[par]
                    if isinstance(x.e, bass.BassScalarEngine):
                        x.copy(dstb[:, col:col + 2048],
                               wstF[:, gslot * 2048:(gslot + 1) * 2048])
                    else:
                        x.tensor_copy(dstb[:, col:col + 2048],
                                      wstF[:, gslot * 2048:(gslot + 1) * 2048])
                    mput(f"b{b}U{u}p{piece}cast", x.mark())

                def unit_casts(u, b=b):
                    def uc_s(s, u=u):
                        for piece in (0, 2, 4):
                            cast_body(s, u, piece)
                    pr.on("s", uc_s)

                    def uc_v(v, u=u):
                        for piece in (1, 3, 5):
                            cast_body(v, u, piece)
                    pr.on("v", uc_v)

                def unit_h1(t, u, b=b):
                    g, e = u // NPE, u % NPE
                    par = u % 2
                    ch = g
                    if u == 0:
                        t.wait(mget(f"b{b}k{QB - 1}lnev"))
                        t.wait(mget(f"b{b}k{QB - 2}lnev"))
                    if e == 0:
                        t.wait(mget(f"b{b}k{4 * g + 3}gevac"))
                    t.wait(mget(f"b{b}U{u}p0cast"))
                    t.wait(mget(f"b{b}U{u}p1cast"))
                    for gp in range(4):
                        t.wait(mget(f"b{b}U{u - 1}amul{gp}"))
                        for dt in range(DS):
                            t.matmul(bank(gp),
                                     wbufs[par][:, dt * 512 + gp * P: dt * 512 + (gp + 1) * P],
                                     xn2T[:, dt * S + ch * CH: dt * S + (ch + 1) * CH],
                                     start=(dt == 0), stop=(dt == DS - 1))
                        mput(f"b{b}U{u}h1g{gp}", t.mark())
                    mput(f"b{b}U{u}h1", t.mark())

                def unit_h3(t, u, b=b):
                    g = u // NPE
                    par = u % 2
                    ch = g
                    t.wait(mget(f"b{b}U{u}p2cast"))
                    t.wait(mget(f"b{b}U{u}p3cast"))
                    for gp in range(4):
                        t.wait(mget(f"b{b}U{u}silu{gp}"))
                        for dt in range(DS):
                            t.matmul(bank(gp),
                                     wbufs[par][:, 4096 + dt * 512 + gp * P:
                                                4096 + dt * 512 + (gp + 1) * P],
                                     xn2T[:, dt * S + ch * CH: dt * S + (ch + 1) * CH],
                                     start=(dt == 0), stop=(dt == DS - 1))
                        mput(f"b{b}U{u}h3g{gp}", t.mark())
                    mput(f"b{b}U{u}h3", t.mark())

                def unit_y(t, u, b=b):
                    g, e = u // NPE, u % NPE
                    par = u % 2
                    t.wait(mget(f"b{b}U{u}amul3"))
                    t.wait(mget(f"b{b}U{u}p4cast"))
                    t.wait(mget(f"b{b}U{u}p5cast"))
                    for q_ in range(4):
                        yp = 4 + (q_ % 2) * 2
                        if q_ >= 2:
                            t.wait(mget(f"b{b}U{u}stt{q_ - 2}"))
                        else:
                            t.wait(mget(f"b{b}U{u - 1}stt{q_ + 2}"))
                        if e == 0 and q_ < 2:
                            t.wait(mget(f"b{b}k{4 * g + 3}gevac"))
                            t.wait(mget(f"b{b}k{4 * g + 3}glog"))
                        for is_ in range(4):
                            for dh in range(2):
                                t.matmul(bank(yp + dh),
                                         apb[:, par * 2048 + is_ * CH + q_ * P:
                                             par * 2048 + is_ * CH + (q_ + 1) * P],
                                         w2bufs[par][:, is_ * 1024 + dh * 512:
                                                     is_ * 1024 + (dh + 1) * 512],
                                         start=(is_ == 0), stop=(is_ == 3))
                        mput(f"b{b}U{u}y{q_}", t.mark())
                    mput(f"b{b}U{u}y3", t.mark())

                def unit_sv(u, b=b):
                    g, e = u // NPE, u % NPE

                    def us(s, u=u):
                        for gp in range(4):
                            s.wait(mget(f"b{b}U{u}h1g{gp}"))
                            s.wait(mget(f"b{b}U{u - 1}amul{gp}"))
                            if u == 0 and b > 0:
                                s.wait(mget(f"b{b - 1}ffn_v_done"))
                            s.activation(silu_sb[:, gp * CH:(gp + 1) * CH],
                                         bank(gp), AF.Silu)
                            mput(f"b{b}U{u}silu{gp}", s.mark())
                    pr.on("s", us)

                    def uv(v, u=u, g=g, e=e):
                        for gp in range(4):
                            v.wait(mget(f"b{b}U{u}silu{gp}"))
                            v.wait(mget(f"b{b}U{u}h3g{gp}"))
                            if u >= 2:
                                v.wait(mget(f"b{b}U{u - 2}y3"))
                            elif b > 0:
                                v.wait(mget(f"b{b - 1}ffn_mm_done"))
                            v.tensor_tensor(
                                out=apb[:, (u % 2) * 2048 + gp * CH:
                                        (u % 2) * 2048 + (gp + 1) * CH],
                                in0=silu_sb[:, gp * CH:(gp + 1) * CH],
                                in1=bank(gp), op=ALU.mult)
                            mput(f"b{b}U{u}amul{gp}", v.mark())
                    pr.on("v", uv)

                def unit_stt(u, b=b):
                    g, e = u // NPE, u % NPE

                    def uw(v, u=u, g=g, e=e):
                        for q_ in range(4):
                            gt = g * 4 + q_
                            yp = 4 + (q_ % 2) * 2
                            v.wait(mget(f"b{b}U{u}y{q_}"))
                            if e > 0:
                                v.wait(mget(f"b{b}U{u - 1}stt{q_}"))
                            elif b > 0:
                                v.wait(mget(f"b{b - 1}t{TT - 1}lny"))
                            ydst = Ysb[:, gt * D:(gt + 1) * D]
                            if e == 0:
                                cs = c_buf[:, gt * NEXP: gt * NEXP + 1]
                                v.tensor_scalar(out=ydst, in0=bank(yp, 2),
                                                scalar1=cs, scalar2=None,
                                                op0=ALU.mult)
                            elif e < NEXP:
                                cs = c_buf[:, gt * NEXP + e: gt * NEXP + e + 1]
                                v.scalar_tensor_tensor(out=ydst, in0=bank(yp, 2),
                                                       scalar=cs, in1=ydst,
                                                       op0=ALU.mult, op1=ALU.add)
                            else:
                                v.tensor_tensor(out=ydst, in0=bank(yp, 2),
                                                in1=ydst, op=ALU.add)
                            mput(f"b{b}U{u}stt{q_}", v.mark())
                    pr.on("v", uw)

                def fin_tile(ti, b=b):
                    g = ti // 4
                    uL = g * NPE + NPE - 1

                    def fin_v1pre(v, b=b, ti=ti):
                        pass
                    emit_ln(f"b{b}t{ti}y", Ysb[:, ti * D:(ti + 1) * D],
                            lnt2[:, (ti % 2) * D:(ti % 2 + 1) * D],
                            slot=ti % 4,
                            sq_ap=outt[:, (ti % 2) * D:(ti % 2 + 1) * D],
                            pre_v=[f"b{b}U{uL}stt{ti % 4}"] +
                                  ([f"b{b}t{ti - 2}outdma"] if ti >= 2 else
                                   ([f"b{b - 1}t{TT - 2 + ti}outdma"] if b > 0 else [])) +
                                  ([f"b{b}t{ti - 4}yln"] if ti >= 4 else []),
                            pre_s=([f"b{b}t{ti - 2}outadd"] if ti >= 2 else []))
                    mput(f"b{b}t{ti}lny", mget(f"b{b}t{ti}yln"))

                    def fin_d(q, b=b, ti=ti):
                        q.wait(mget(f"b{b}t{ti}yv1"))
                        if ti >= 2:
                            q.wait(mget(f"b{b}t{ti - 2}outadd"))
                        mput(f"b{b}t{ti}x1r",
                             q.dma(outt[:, (ti % 2) * D:(ti % 2 + 1) * D],
                                   x1park[ti * P:(ti + 1) * P, :],
                                   pr.ds("x1rA" if ti % 2 == 0 else "x1rB")))
                    pr.on("d", fin_d)

                    def fin_g2(g_, b=b, ti=ti):
                        g_.wait(mget(f"b{b}t{ti}lny"))
                        g_.wait(mget(f"b{b}t{ti}x1r"))
                        sl = (ti % 2) * D
                        g_.tensor_tensor(out=outt[:, sl:sl + D], in0=outt[:, sl:sl + D],
                                         in1=lnt2[:, sl:sl + D], op=ALU.add)
                        mput(f"b{b}t{ti}outadd", g_.mark())
                    pr.on("g", fin_g2)

                    def fin_d2(q, b=b, ti=ti):
                        q.wait(mget(f"b{b}t{ti}outadd"))
                        mput(f"b{b}t{ti}outdma",
                             q.dma(out_t[b, ti * P:(ti + 1) * P, :],
                                   outt[:, (ti % 2) * D:(ti % 2 + 1) * D],
                                   pr.ds("outA" if ti % 2 == 0 else "outB")))
                    pr.on("d", fin_d2)

                for g in range(NCH):
                    if g == 0:
                        back_ln(0)
                        back_ln(1)
                        back_tr(0)
                        back_ln(2)
                        back_tr(1)
                        back_ln(3)
                        back_tr(2)
                        back_tr(3)
                    if g > 0:
                        def yprev(t, b=b, g=g):
                            unit_y(t, g * NPE - 1)
                        pr.on("t", yprev)
                        unit_stt(g * NPE - 1)
                        back_tr(4 * g)
                        back_tr(4 * g + 1)
                        back_tr(4 * g + 2)
                        back_tr(4 * g + 3)
                    gate_group(g)
                    for e in range(NPE):
                        u = g * NPE + e
                        unit_dma(u)
                        if b == 0 and g == 0:
                            unit_casts(u)

                        def ut(t, b=b, u=u, e=e):
                            unit_h1(t, u)
                            if e > 0:
                                unit_y(t, u - 1)
                            unit_h3(t, u)
                        pr.on("t", ut)
                        if e > 0:
                            unit_stt(u - 1)
                        unit_sv(u)
                        if e == 12 and g + 1 < NCH:
                            back_ln(4 * (g + 1))
                            back_ln(4 * (g + 1) + 1)
                        if e == 13 and g + 1 < NCH:
                            back_ln(4 * (g + 1) + 2)
                            back_ln(4 * (g + 1) + 3)
                    if g > 0:
                        for ti in range(4 * (g - 1), 4 * g):
                            fin_tile(ti)

                def tail_t(t, b=b):
                    unit_y(t, NU - 1)
                    mput(f"b{b}ffn_mm_done", t.mark())
                pr.on("t", tail_t)
                unit_stt(NU - 1)
                for ti in range(4 * (NCH - 1), 4 * NCH):
                    fin_tile(ti)

                def fin_bar_t(t, b=b):
                    t.wait(mget(f"b{b}t{TT - 1}outdma"))
                    t.wait(mget(f"b{b}t{TT - 2}outdma"))
                    t.wait(mget(f"b{b}t{TT - 1}x1r"))
                    t.wait(mget(f"b{b}t{TT - 2}x1r"))
                    if b == 0:
                        for uu in (NPE - 2, NPE - 1):
                            t.wait(mget(f"T{uu}tee1"))
                            t.wait(mget(f"T{uu}tee2"))
                    mput(f"b{b}fin_done", t.mark())
                pr.on("t", fin_bar_t)

                def endmarks_s(s, b=b):
                    mput(f"b{b}ffn_cast_s", s.mark())
                pr.on("s", endmarks_s)

                def endmarks_g(g_, b=b):
                    mput(f"b{b}ffn_cast_g", g_.mark())
                pr.on("g", endmarks_g)

                def endmarks_v(v, b=b):
                    mput(f"b{b}ffn_v_done", v.mark())
                pr.on("v", endmarks_v)
                pr.flush()
                ck()


        except _StopBuild:
            pr.flush()
        finals = dict(pr.dcounts)

        def final_d(raw):
            for k, n in finals.items():
                if n > 0:
                    raw.wait_ge(dsems[k], n)
        block.sync(final_d)

        marks_out = dict(pr.M)
    return nc, marks_out


def kernel(**inputs):
    S = 2048
    x = np.ascontiguousarray(inputs["L0_attn_x"])
    per_core = [
        {
            "x": x[2 * c:2 * c + 2],
            "Wq": inputs["L0_attn_Wq"], "Wk": inputs["L0_attn_Wk"],
            "Wv": inputs["L0_attn_Wv"],
            "bq": np.asarray(inputs["L0_attn_bq"]).reshape(1, D),
            "bk": np.asarray(inputs["L0_attn_bk"]).reshape(1, D),
            "bv": np.asarray(inputs["L0_attn_bv"]).reshape(1, D),
            "gate_w": inputs["L0_ffn_gate_w"],
            "w1": inputs["L0_ffn_w1"], "w2": inputs["L0_ffn_w2"],
            "w3": inputs["L0_ffn_w3"],
            "ws1": inputs["L0_ffn_ws1"], "ws2": inputs["L0_ffn_ws2"],
            "ws3": inputs["L0_ffn_ws3"],
        }
        for c in range(NCORES)
    ]
    nc, marks = build(S, None)
    nc, marks = build(S, marks)
    nc, _ = build(S, marks)
    res = run_bass_kernel_spmd(nc, per_core, core_ids=list(range(NCORES)))
    out = np.concatenate([r["out"] for r in res.results], axis=0)
    return np.ascontiguousarray(out.astype(np.float32))
